# version 02
# baseline (speedup 1.0000x reference)
import numpy as np
import ml_dtypes
import contextlib
import concourse.bass as bass
import concourse.mybir as mybir
from concourse.bass_utils import run_bass_kernel_spmd

F32 = mybir.dt.float32
BF16 = mybir.dt.bfloat16
I32 = mybir.dt.int32
AF = mybir.ActivationFunctionType
OP = mybir.AluOpType
AX = mybir.AxisListType

SEQ, TOWN, D, N, R, HID = 4096, 2048, 2048, 16, 128, 5632
NDC, ODC, HR = 16, 8, 44
TC, NTC = 1024, 4
EPS = 1e-5


def load_act_set(nc, e, name):
    from concourse.hw_specs import get_activation_tables
    idx = list(get_activation_tables(nc.m.arch).keys()).index(name)
    e.add_instruction(mybir.InstLoadActFuncSet(
        name=nc.get_next_instruction_name(), act_func_set_id=idx, ins=[], outs=[]))


class Sem:
    def __init__(self, h):
        self.h = h
        self.v = 0

    def inc(self, k=1):
        self.v += k
        return self.v


def build(debug=False):
    nc = bass.Bass("TRN2", target_bir_lowering=False)

    x_seq = nc.dram_tensor("x_seq", [SEQ, D], F32, kind="ExternalInput").ap()
    x_own = nc.dram_tensor("x_own", [TOWN, D], F32, kind="ExternalInput").ap()
    xprojT = nc.dram_tensor("xprojT", [D, 160], F32, kind="ExternalInput").ap()
    wdtT_own = nc.dram_tensor("wdtT_own", [R, ODC * 128], F32, kind="ExternalInput").ap()
    dtb_own = nc.dram_tensor("dtb_own", [128, ODC], F32, kind="ExternalInput").ap()
    Alog_own = nc.dram_tensor("Alog_own", [128, ODC * N], F32, kind="ExternalInput").ap()
    Dp_own = nc.dram_tensor("Dp_own", [128, ODC], F32, kind="ExternalInput").ap()
    wgT = nc.dram_tensor("wgT", [D, HID], I32, kind="ExternalInput").ap()
    wuT = nc.dram_tensor("wuT", [D, HID], I32, kind="ExternalInput").ap()
    wdn = nc.dram_tensor("wdn", [D, HID], I32, kind="ExternalInput").ap()
    sg_in = nc.dram_tensor("sg_in", [128, HR], F32, kind="ExternalInput").ap()
    su_in = nc.dram_tensor("su_in", [128, HR], F32, kind="ExternalInput").ap()
    sd_in = nc.dram_tensor("sd_in", [128, NDC], F32, kind="ExternalInput").ap()
    cst_in = nc.dram_tensor("cst_in", [128, 4], F32, kind="ExternalInput").ap()
    ones_in = nc.dram_tensor("ones_in", [128, 1], BF16, kind="ExternalInput").ap()
    n128_in = nc.dram_tensor("n128_in", [1, 128], BF16, kind="ExternalInput").ap()
    out_own = nc.dram_tensor("out_own", [TOWN, D], F32, kind="ExternalOutput").ap()

    xn_dram = nc.dram_tensor("xn_dram", [SEQ, D], BF16, kind="Internal").ap()
    xnown_dram = nc.dram_tensor("xnown_dram", [TOWN, D], BF16, kind="Internal").ap()
    bc_dram = nc.dram_tensor("bc_dram", [32, SEQ], BF16, kind="Internal").ap()
    dt_dram = nc.dram_tensor("dt_dram", [ODC * 128, SEQ], BF16, kind="Internal").ap()
    u_dram = nc.dram_tensor("u_dram", [ODC * 128, SEQ], BF16, kind="Internal").ap()
    dxn_dram = nc.dram_tensor("dxn_dram", [ODC * 128, SEQ], BF16, kind="Internal").ap()
    wg_bf = nc.dram_tensor("wg_bf", [D, HID], BF16, kind="Internal").ap()
    wu_bf = nc.dram_tensor("wu_bf", [D, HID], BF16, kind="Internal").ap()
    wd_bf = nc.dram_tensor("wd_bf", [D, HID], BF16, kind="Internal").ap()
    act_dram = nc.dram_tensor("act_dram", [HID, TOWN], BF16, kind="Internal").ap()
    cc_in = nc.dram_tensor("cc_in", [ODC * 128, SEQ], BF16, kind="Internal").ap()
    cc_out = nc.dram_tensor("cc_out", [2 * ODC * 128, SEQ], BF16, kind="Internal").ap()
    sg_dram = nc.dram_tensor("sg_dram", [D, TOWN], BF16, kind="Internal").ap()
    st_dram = nc.dram_tensor("st_dram", [128, 16 * 48], F32, kind="Internal").ap()
    dbg = {}
    if debug:
        for nm, shp in (("dbg_xn", [SEQ, D]), ("dbg_bc", [32, SEQ]),
                        ("dbg_dt", [ODC * 128, SEQ]), ("dbg_u", [ODC * 128, SEQ]),
                        ("dbg_s", [ODC * 128, SEQ]), ("dbg_act", [HID, TOWN]),
                        ("dbg_st", [128, 16 * 48])):
            dbg[nm] = nc.dram_tensor(nm, shp, F32, kind="ExternalOutput").ap()

    es = contextlib.ExitStack()

    def sb(name, shape, dt):
        return es.enter_context(nc.sbuf_tensor(name, shape, dt))

    with es, nc.Block() as block:
        xprojT_bf = sb("xprojT_bf", [128, NDC * 160], BF16)
        wdtT_bf = sb("wdtT_bf", [128, ODC * 128], BF16)
        dtb_sb = sb("dtb_sb", [128, ODC], F32)
        A_sb = sb("A_sb", [128, ODC * N], F32)
        Dp_sb = sb("Dp_sb", [128, ODC], F32)
        sgh_sb = sb("sgh_sb", [128, HR], F32)
        su_sb = sb("su_sb", [128, HR], F32)
        sdh_sb = sb("sdh_sb", [128, NDC], F32)
        ones_bf = sb("ones_bf", [128, 1], BF16)
        n128_bf = sb("n128_bf", [1, 128], BF16)
        dtlowT_bf = sb("dtlowT_bf", [128, SEQ], BF16)
        bcT_bf = sb("bcT_bf", [32, SEQ], BF16)
        x1row_f = sb("x1row_f", [1, TOWN], F32)
        x1row_bf = sb("x1row_bf", [1, TOWN], BF16)
        a1row_f = sb("a1row_f", [1, 512], F32)
        a1row_bf = sb("a1row_bf", [1, 512], BF16)
        state_sb = sb("state_sb", [128, 128], F32)
        const_sb = sb("const_sb", [128, 4], F32)
        nc.const_aps.register(F32, EPS, const_sb[:, 0:1])
        nc.const_aps.register(F32, -0.5, const_sb[:, 1:2])
        nc.const_aps.register(F32, -1.0, const_sb[:, 2:3])
        nc.const_aps.register(F32, 1.0, const_sb[:, 3:4])
        ps = es.enter_context(nc.psum_tensor("ps", [128, 4096], F32))

        pe = Sem(es.enter_context(nc.semaphore("pe")))
        act = Sem(es.enter_context(nc.semaphore("act")))
        dve = Sem(es.enter_context(nc.semaphore("dve")))
        pool = Sem(es.enter_context(nc.semaphore("pool")))
        ccs = Sem(es.enter_context(nc.semaphore("ccs")))
        bar = Sem(es.enter_context(nc.semaphore("bar")))
        rings = {}

        def ring(name):
            if name not in rings:
                rings[name] = Sem(es.enter_context(nc.semaphore(name)))
            return rings[name]

        gq = ring("gq")

        def barrier():
            tgt = bar.v + 5
            ring_waits = [(r.h, r.v) for r in rings.values() if r.v]

            @block.sync
            def _(e):
                for h, v in ring_waits:
                    e.wait_ge(h, v)
                e.sem_inc(bar.h, 1)
                e.wait_ge(bar.h, tgt)

            @block.tensor
            def _(e):
                e.wait_ge(pe.h, pe.v)
                e.sem_inc(bar.h, 1)
                e.wait_ge(bar.h, tgt)

            @block.vector
            def _(e):
                e.wait_ge(dve.h, dve.v)
                e.sem_inc(bar.h, 1)
                e.wait_ge(bar.h, tgt)

            @block.scalar
            def _(e):
                e.wait_ge(act.h, act.v)
                e.sem_inc(bar.h, 1)
                e.wait_ge(bar.h, tgt)

            @block.gpsimd
            def _(e):
                e.wait_ge(gq.h, gq.v)
                e.wait_ge(pool.h, pool.v)
                e.sem_inc(bar.h, 1)
                e.wait_ge(bar.h, tgt)
            bar.v = tgt

        @block.gpsimd
        def _(g):
            g.dma_start(out=xprojT_bf[:].rearrange("p (c f) -> p c f", c=NDC),
                        in_=xprojT.rearrange("(c p) f -> p c f", p=128)).then_inc(gq.h, 16)
            g.dma_start(out=wdtT_bf[:], in_=wdtT_own).then_inc(gq.h, 16)
            g.dma_start(out=wg_bf, in_=wgT).then_inc(gq.h, 16)
            g.dma_start(out=wu_bf, in_=wuT).then_inc(gq.h, 16)
            g.dma_start(out=wd_bf, in_=wdn).then_inc(gq.h, 16)
        gq.inc(16 * 5)

        q_par = ring("q_par")

        @block.sync
        def _(e):
            e.dma_start(out=dtb_sb[:], in_=dtb_own).then_inc(q_par.h, 16)
            e.dma_start(out=A_sb[:], in_=Alog_own).then_inc(q_par.h, 16)
            e.dma_start(out=Dp_sb[:], in_=Dp_own).then_inc(q_par.h, 16)
            e.dma_start(out=sgh_sb[:], in_=sg_in).then_inc(q_par.h, 16)
            e.dma_start(out=su_sb[:], in_=su_in).then_inc(q_par.h, 16)
            e.dma_start(out=sdh_sb[:], in_=sd_in).then_inc(q_par.h, 16)
            e.dma_start(out=const_sb[:], in_=cst_in).then_inc(q_par.h, 16)
            e.dma_start(out=ones_bf[:], in_=ones_in).then_inc(q_par.h, 16)
            e.dma_start(out=n128_bf[:], in_=n128_in).then_inc(q_par.h, 16)
        q_par.inc(16 * 9)

        @block.scalar
        def _(e):
            load_act_set(nc, e, "natural_log_exp_and_others")
            e.wait_ge(q_par.h, q_par.v)
            e.activation(out=A_sb[:], in_=A_sb[:], func=AF.Exp).then_inc(act.h, 1)
        a_exp = act.inc()

        @block.vector
        def _(e):
            e.wait_ge(act.h, a_exp)
            e.wait_ge(q_par.h, q_par.v)
            e.tensor_scalar_mul(out=A_sb[:], in0=A_sb[:], scalar1=-1.0)
            e.tensor_scalar_mul(out=sgh_sb[:], in0=sgh_sb[:], scalar1=0.5)
            e.tensor_scalar_mul(out=sdh_sb[:], in0=sdh_sb[:], scalar1=0.5).then_inc(dve.h, 1)
        dve.inc()

        p1 = contextlib.ExitStack()
        with p1:
            xt_sb = p1.enter_context(nc.sbuf_tensor("xt_sb", [128, 2 * D], F32))
            dump_sb = p1.enter_context(nc.sbuf_tensor("dump_sb", [128, 2 * D], F32))
            st_sb = p1.enter_context(nc.sbuf_tensor("st_sb", [128, 16 * 48], F32))
            xnt_sb = p1.enter_context(nc.sbuf_tensor("xnt_sb", [128, 4 * D], BF16))

            ln_tiles = [("seq", t) for t in range(SEQ // 128)] + \
                       [("own", t) for t in range(TOWN // 128)]
            NT = len(ln_tiles)
            SSQ, SSUM, M_, SS_, MSQ, VAR, LNV, RSTD, NEGM = [i * 48 for i in range(9)]
            sum_act, sum_dve, xn_ms, st_ms = {}, {}, {}, {}
            q_xt = [ring("q_xt0"), ring("q_xt1")]
            q_xnt = [ring("q_xnt%d" % i) for i in range(4)]

            for k, (src_, t) in enumerate(ln_tiles):
                sl = k % 2
                xsrc = x_seq if src_ == "seq" else x_own

                @block.sync
                def _(e, k=k, sl=sl, t=t, xsrc=xsrc):
                    if k >= 2:
                        e.wait_ge(dve.h, sum_dve[k - 2])
                    e.dma_start(out=xt_sb[:, sl * D:(sl + 1) * D],
                                in_=xsrc[t * 128:(t + 1) * 128, :]).then_inc(q_xt[sl].h, 16)
                xt_tgt = q_xt[sl].inc(16)

                @block.scalar
                def _(e, k=k, sl=sl, xt_tgt=xt_tgt):
                    e.wait_ge(q_xt[sl].h, xt_tgt)
                    if k >= 2:
                        e.wait_ge(dve.h, sum_dve[k - 2])
                    e.activation(out=dump_sb[:, sl * D:(sl + 1) * D],
                                 in_=xt_sb[:, sl * D:(sl + 1) * D],
                                 func=AF.Square).then_inc(act.h, 1)
                sum_act[k] = act.inc()

                @block.vector
                def _(e, k=k, sl=sl, xt_tgt=xt_tgt):
                    e.wait_ge(q_xt[sl].h, xt_tgt)
                    e.wait_ge(act.h, sum_act[k])
                    e.reduce_sum(out=st_sb[:, SSQ + k:SSQ + k + 1],
                                 in_=dump_sb[:, sl * D:(sl + 1) * D], axis=AX.X)
                    e.reduce_sum(out=st_sb[:, SSUM + k:SSUM + k + 1],
                                 in_=xt_sb[:, sl * D:(sl + 1) * D],
                                 axis=AX.X).then_inc(dve.h, 1)
                sum_dve[k] = dve.inc()

            @block.vector
            def _(e):
                e.wait_ge(act.h, sum_act[NT - 1])
                e.wait_ge(dve.h, sum_dve[NT - 1])
                e.tensor_scalar_mul(out=st_sb[:, M_:M_ + NT],
                                    in0=st_sb[:, SSUM:SSUM + NT], scalar1=1.0 / D)
                e.tensor_scalar_mul(out=st_sb[:, SS_:SS_ + NT],
                                    in0=st_sb[:, SSQ:SSQ + NT], scalar1=1.0 / D)
                e.tensor_copy(out=dump_sb[:, 0:D], in_=dump_sb[:, 0:D])
                e.tensor_copy(out=dump_sb[:, 0:D], in_=dump_sb[:, 0:D])
                e.tensor_mul(out=st_sb[:, MSQ:MSQ + NT], in0=st_sb[:, M_:M_ + NT],
                             in1=st_sb[:, M_:M_ + NT])
                e.tensor_copy(out=dump_sb[:, 0:D], in_=dump_sb[:, 0:D])
                e.tensor_copy(out=dump_sb[:, 0:D], in_=dump_sb[:, 0:D])
                e.tensor_sub(out=st_sb[:, VAR:VAR + NT], in0=st_sb[:, SS_:SS_ + NT],
                             in1=st_sb[:, MSQ:MSQ + NT])
                e.tensor_copy(out=dump_sb[:, 0:D], in_=dump_sb[:, 0:D])
                e.tensor_copy(out=dump_sb[:, 0:D],
                              in_=dump_sb[:, 0:D]).then_inc(dve.h, 1)
            stats_dve = dve.inc()

            @block.scalar
            def _(e):
                e.wait_ge(dve.h, stats_dve)
                e.activation(out=st_sb[:, LNV:LNV + NT], in_=st_sb[:, VAR:VAR + NT],
                             func=AF.Ln, bias=EPS)
                e.activation(out=dump_sb[:, 0:D], in_=dump_sb[:, 0:D], func=AF.Identity)
                e.activation(out=dump_sb[:, 0:D], in_=dump_sb[:, 0:D], func=AF.Identity)
                e.activation(out=st_sb[:, RSTD:RSTD + NT], in_=st_sb[:, LNV:LNV + NT],
                             func=AF.Exp, scale=const_sb[:, 1:2])
                e.activation(out=dump_sb[:, 0:D], in_=dump_sb[:, 0:D], func=AF.Identity)
                e.activation(out=dump_sb[:, 0:D], in_=dump_sb[:, 0:D],
                             func=AF.Identity).then_inc(act.h, 1)
            stats_act = act.inc()

            @block.vector
            def _(e):
                e.wait_ge(act.h, stats_act)
                e.scalar_tensor_tensor(out=st_sb[:, NEGM:NEGM + NT],
                                       in0=st_sb[:, M_:M_ + NT], scalar=-1.0,
                                       in1=st_sb[:, RSTD:RSTD + NT],
                                       op0=OP.mult, op1=OP.mult)
                e.tensor_copy(out=dump_sb[:, 0:D], in_=dump_sb[:, 0:D])
                e.tensor_copy(out=dump_sb[:, 0:D],
                              in_=dump_sb[:, 0:D]).then_inc(dve.h, 1)
            negm_dve = dve.inc()

            def emit_xn_store(j):
                src_, t = ln_tiles[j]
                xdst = xn_dram if src_ == "seq" else xnown_dram
                gate = xn_ms[j + 1] if (j + 1) in xn_ms else xn_ms[j]
                qs = q_xnt[j % 4]

                @block.sync
                def _(e, j=j, t=t, xdst=xdst, gate=gate, qs=qs):
                    e.wait_ge(dve.h, gate)
                    e.dma_start(out=xdst[t * 128:(t + 1) * 128, :],
                                in_=xnt_sb[:, (j % 4) * D:(j % 4 + 1) * D]
                                ).then_inc(qs.h, 16)
                st_ms[j] = qs.inc(16)

            for k, (src_, t) in enumerate(ln_tiles):
                sl = k % 2
                xsrc = x_seq if src_ == "seq" else x_own

                @block.sync
                def _(e, k=k, sl=sl, t=t, xsrc=xsrc):
                    if k >= 2:
                        e.wait_ge(dve.h, xn_ms[k - 2])
                    else:
                        e.wait_ge(dve.h, sum_dve[NT - 2 + k])
                    e.dma_start(out=xt_sb[:, sl * D:(sl + 1) * D],
                                in_=xsrc[t * 128:(t + 1) * 128, :]).then_inc(q_xt[sl].h, 16)
                xt_tgt = q_xt[sl].inc(16)

                @block.vector
                def _(e, k=k, sl=sl, xt_tgt=xt_tgt):
                    e.wait_ge(q_xt[sl].h, xt_tgt)
                    e.wait_ge(dve.h, negm_dve)
                    if k >= 4:
                        e.wait_ge(q_xnt[k % 4].h, st_ms[k - 4])
                    e.tensor_scalar(out=xnt_sb[:, (k % 4) * D:(k % 4 + 1) * D],
                                    in0=xt_sb[:, sl * D:(sl + 1) * D],
                                    scalar1=st_sb[:, RSTD + k:RSTD + k + 1],
                                    scalar2=st_sb[:, NEGM + k:NEGM + k + 1],
                                    op0=OP.mult, op1=OP.add).then_inc(dve.h, 1)
                xn_ms[k] = dve.inc()
                if k >= 1:
                    emit_xn_store(k - 1)

            @block.vector
            def _(e):
                e.tensor_copy(out=dump_sb[:, 0:D],
                              in_=dump_sb[:, 0:D]).then_inc(dve.h, 1)
            xn_ms[NT] = dve.inc()
            emit_xn_store(NT - 1)

            q_std = ring("q_std")

            @block.sync
            def _(e):
                e.wait_ge(dve.h, xn_ms[NT])
                e.dma_start(out=st_dram, in_=st_sb[:]).then_inc(q_std.h, 16)
            q_std.inc(16)

        barrier()

        p2 = contextlib.ExitStack()
        with p2:
            xnT_sb = p2.enter_context(nc.sbuf_tensor("xnT_sb", [128, 2 * 2048], BF16))
            q_xnT = [ring("q_xnT0"), ring("q_xnT1")]
            q_bcd = ring("q_bcd")
            mm_ms, evac_ms = {}, {}
            for th2 in range(2):
                for dc in range(NDC):
                    j = th2 * NDC + dc
                    sl = dc % 2

                    @block.sync
                    def _(e, j=j, sl=sl, th2=th2, dc=dc):
                        if j >= 2:
                            e.wait_ge(pe.h, mm_ms[j - 2])
                        e.dma_start(out=xnT_sb[:, sl * 2048:(sl + 1) * 2048],
                                    in_=xn_dram[th2 * 2048:(th2 + 1) * 2048,
                                                dc * 128:(dc + 1) * 128],
                                    transpose=True).then_inc(q_xnT[sl].h, 16)
                    xnT_tgt = q_xnT[sl].inc(16)

                    @block.tensor
                    def _(e, j=j, sl=sl, dc=dc, th2=th2, xnT_tgt=xnT_tgt):
                        e.wait_ge(q_xnT[sl].h, xnT_tgt)
                        if dc == 0 and th2 == 1:
                            e.wait_ge(dve.h, evac_ms[0])
                        for tch in range(4):
                            e.matmul(out=ps[:, tch * 512:(tch + 1) * 512],
                                     lhsT=xprojT_bf[:, dc * 160:dc * 160 + 128],
                                     rhs=xnT_sb[:, sl * 2048 + tch * 512:
                                                sl * 2048 + (tch + 1) * 512],
                                     start=(dc == 0), stop=(dc == NDC - 1))
                        for tch in range(4):
                            mm = e.matmul(out=ps[0:32, 2048 + tch * 512:2048 + (tch + 1) * 512],
                                          lhsT=xprojT_bf[:, dc * 160 + 128:dc * 160 + 160],
                                          rhs=xnT_sb[:, sl * 2048 + tch * 512:
                                                     sl * 2048 + (tch + 1) * 512],
                                          start=(dc == 0), stop=(dc == NDC - 1))
                            if tch == 3:
                                mm.then_inc(pe.h, 1)
                    mm_ms[j] = pe.inc()

                @block.vector
                def _(e, th2=th2):
                    e.wait_ge(pe.h, mm_ms[th2 * NDC + NDC - 1])
                    e.tensor_copy(out=dtlowT_bf[:, th2 * 2048:(th2 + 1) * 2048],
                                  in_=ps[:, 0:2048])
                    e.tensor_copy(out=bcT_bf[:, th2 * 2048:(th2 + 1) * 2048],
                                  in_=ps[0:32, 2048:4096])
                    e.tensor_copy(out=xnT_sb[:, 0:1024],
                                  in_=xnT_sb[:, 0:1024]).then_inc(dve.h, 1)
                evac_ms[th2] = dve.inc()

            @block.sync
            def _(e):
                e.wait_ge(dve.h, evac_ms[1])
                e.dma_start(out=bc_dram, in_=bcT_bf[:]).then_inc(q_bcd.h, 16)
            q_bcd.inc(16)

        barrier()

        p25 = contextlib.ExitStack()
        with p25:
            sp_a = p25.enter_context(nc.sbuf_tensor("sp_a", [128, 2048], F32))
            sp_b = p25.enter_context(nc.sbuf_tensor("sp_b", [128, 2048], F32))
            dtu_sb = p25.enter_context(nc.sbuf_tensor("dtu_sb", [128, 3 * 2048], BF16))
            xnm_sb = p25.enter_context(nc.sbuf_tensor("xnm_sb", [128, 2048], BF16))
            q_xnm = ring("q_xnm")
            q_dtu = ring("q_dtu")

            act_p25, dve_p25, park_ms = {}, {}, {}
            for th2 in range(2):
                for oc in range(ODC):
                    j = th2 * ODC + oc

                    @block.sync
                    def _(e, j=j, th2=th2, oc=oc):
                        if j >= 1:
                            e.wait_ge(dve.h, dve_p25[j - 1])
                        e.dma_start(out=xnm_sb[:],
                                    in_=xn_dram[th2 * 2048:(th2 + 1) * 2048,
                                                oc * 128:(oc + 1) * 128],
                                    transpose=True).then_inc(q_xnm.h, 16)
                    xnm_tgt = q_xnm.inc(16)

                    @block.tensor
                    def _(e, j=j, th2=th2, oc=oc):
                        if j >= 1:
                            e.wait_ge(act.h, act_p25[j - 1])
                        for i in range(4):
                            mm = e.matmul(out=ps[:, i * 512:(i + 1) * 512],
                                          lhsT=wdtT_bf[:, oc * 128:(oc + 1) * 128],
                                          rhs=dtlowT_bf[:, th2 * 2048 + i * 512:
                                                        th2 * 2048 + (i + 1) * 512],
                                          start=True, stop=True)
                            if i == 3:
                                mm.then_inc(pe.h, 1)
                    pe_dt = pe.inc()

                    @block.scalar
                    def _(e, j=j, oc=oc, pe_dt=pe_dt):
                        e.wait_ge(pe.h, pe_dt)
                        if j >= 1:
                            e.wait_ge(dve.h, dve_p25[j - 1])
                        bias = dtb_sb[:, oc:oc + 1]
                        e.activation(out=sp_a[:], in_=ps[:, 0:2048], func=AF.Abs, bias=bias)
                        e.activation(out=sp_b[:], in_=sp_a[:], func=AF.Exp,
                                     scale=const_sb[:, 2:3])
                        e.activation(out=sp_a[:], in_=sp_b[:], func=AF.Ln, bias=1.0)
                        e.activation(out=sp_b[:], in_=ps[:, 0:2048], func=AF.Relu,
                                     bias=bias).then_inc(act.h, 1)
                    act_p25[j] = act.inc()

                    @block.vector
                    def _(e, j=j, oc=oc, xnm_tgt=xnm_tgt):
                        e.wait_ge(act.h, act_p25[j])
                        if j >= 1:
                            e.wait_ge(q_dtu.h, park_ms[j - 1])
                        e.tensor_add(out=dtu_sb[:, 0:2048], in0=sp_a[:], in1=sp_b[:])
                        e.wait_ge(q_xnm.h, xnm_tgt)
                        e.tensor_mul(out=dtu_sb[:, 2048:4096],
                                     in0=dtu_sb[:, 0:2048], in1=xnm_sb[:])
                        e.tensor_scalar_mul(out=dtu_sb[:, 4096:6144], in0=xnm_sb[:],
                                            scalar1=Dp_sb[:, oc:oc + 1])
                        e.tensor_copy(out=sp_a[:], in_=sp_a[:]).then_inc(dve.h, 1)
                    dve_p25[j] = dve.inc()

                    @block.sync
                    def _(e, j=j, th2=th2, oc=oc):
                        e.wait_ge(dve.h, dve_p25[j])
                        rows = slice(oc * 128, (oc + 1) * 128)
                        cols = slice(th2 * 2048, (th2 + 1) * 2048)
                        e.dma_start(out=dt_dram[rows, cols],
                                    in_=dtu_sb[:, 0:2048]).then_inc(q_dtu.h, 16)
                        e.dma_start(out=u_dram[rows, cols],
                                    in_=dtu_sb[:, 2048:4096]).then_inc(q_dtu.h, 16)
                        e.dma_start(out=dxn_dram[rows, cols],
                                    in_=dtu_sb[:, 4096:6144]).then_inc(q_dtu.h, 16)
                    park_ms[j] = q_dtu.inc(48)

        barrier()

        @block.scalar
        def _(e):
            load_act_set(nc, e, "exp_and_others")

        p3 = contextlib.ExitStack()
        with p3:
            def p3sb(nm, shp, dt):
                return p3.enter_context(nc.sbuf_tensor(nm, shp, dt))
            dt_tc = p3sb("dt_tc", [128, ODC * TC], BF16)
            u_tc = p3sb("u_tc", [128, ODC * TC], BF16)
            y_tc = p3sb("y_tc", [128, ODC * TC], BF16)
            bbc_sb = p3sb("bbc_sb", [128, 2 * TC], BF16)
            cbc_sb = p3sb("cbc_sb", [128, 2 * TC], BF16)
            abar_sb = p3sb("abar_sb", [128, 2 * TC], F32)
            bx_sb = p3sb("bx_sb", [128, 2 * TC], BF16)
            h_sb = p3sb("h_sb", [128, 2 * TC], BF16)
            w_sb = p3sb("w_sb", [128, 2 * TC], BF16)
            xnf_sb = p3sb("xnf_sb", [128, NDC * 1024], BF16)
            wg2_sb = p3sb("wg2_sb", [128, 2 * NDC * 256], BF16)
            wu2_sb = p3sb("wu2_sb", [128, 2 * NDC * 256], BF16)
            th_sb = p3sb("th_sb", [128, 2 * 1024], BF16)
            gh_sb = p3sb("gh_sb", [128, 2 * 1024], BF16)
            ups_sb = p3sb("ups_sb", [128, 2 * 1024], BF16)
            th1_sb = p3sb("th1_sb", [128, 2 * 1024], BF16)
            t3_sb = p3sb("t3_sb", [128, 2 * 1024], BF16)
            actt_sb = p3sb("actt_sb", [128, 2 * 1024], BF16)

            q_tc = ring("q_tc")
            q_bc = [ring("q_bc0"), ring("q_bc1")]
            q_park = ring("q_park")
            q_xnf = ring("q_xnf")
            q_wp = [ring("q_wp0"), ring("q_wp1")]
            q_ast = [ring("q_ast0"), ring("q_ast1")]

            scan_ms, ybar_ms, w_ms, abar_ms = {}, {}, {}, {}
            bxlast_u, wlast_u, park_tc, tcload_ms = {}, {}, {}, {}
            fw_ms, fpe_ms, fact_ps, fact_th1, fpool_ms, fst_ms = {}, {}, {}, {}, {}, {}
            xnf_ms, x1_ms = {}, {}

            def emit_tc_loads(tc):
                @block.sync
                def _(e, tc=tc):
                    cols = slice(tc * TC, (tc + 1) * TC)
                    if tc >= 1:
                        kp = ((tc - 1) * 16 + 15) * ODC + ODC - 1
                        e.wait_ge(act.h, abar_ms[kp])
                        e.wait_ge(dve.h, bxlast_u[(tc - 1) * 16 + 15])
                        e.wait_ge(q_park.h, park_tc[(tc - 1, ODC - 1)])
                    for oc in range(ODC):
                        rows = slice(oc * 128, (oc + 1) * 128)
                        dst = slice(oc * TC, (oc + 1) * TC)
                        e.dma_start(out=dt_tc[:, dst],
                                    in_=dt_dram[rows, cols]).then_inc(q_tc.h, 16)
                        e.dma_start(out=u_tc[:, dst],
                                    in_=u_dram[rows, cols]).then_inc(q_tc.h, 16)
                        e.dma_start(out=y_tc[:, dst],
                                    in_=dxn_dram[rows, cols]).then_inc(q_tc.h, 16)
                tcload_ms[tc] = q_tc.inc(16 * 3 * ODC)

            def emit_mixer_unit(m):
                tc, n = divmod(m, 16)
                if n == 0:
                    emit_tc_loads(tc)
                bsl = m % 2
                bdst = slice(bsl * TC, (bsl + 1) * TC)

                @block.sync
                def _(e, m=m, tc=tc, n=n, bdst=bdst, bsl=bsl):
                    cols = slice(tc * TC, (tc + 1) * TC)
                    e.wait_ge(q_bcd.h, 16)
                    if m >= 2:
                        e.wait_ge(dve.h, bxlast_u[m - 2])
                        e.wait_ge(pool.h, wlast_u[m - 2])
                    e.dma_start(out=bbc_sb[:, bdst],
                                in_=bc_dram[n:n + 1, cols]
                                .partition_broadcast(128)).then_inc(q_bc[bsl].h, 16)
                    e.dma_start(out=cbc_sb[:, bdst],
                                in_=bc_dram[16 + n:17 + n, cols]
                                .partition_broadcast(128)).then_inc(q_bc[bsl].h, 16)
                bc_tgt = q_bc[bsl].inc(32)

                for oc in range(ODC):
                    k = m * ODC + oc
                    sl = k % 2
                    col = oc * 16 + n
                    tsl = slice(sl * TC, (sl + 1) * TC)
                    osl = slice(oc * TC, (oc + 1) * TC)

                    @block.scalar
                    def _(e, k=k, tc=tc, col=col, tsl=tsl, osl=osl):
                        e.wait_ge(q_tc.h, tcload_ms[tc])
                        if k >= 2:
                            e.wait_ge(dve.h, scan_ms[k - 2])
                        e.activation(out=abar_sb[:, tsl], in_=dt_tc[:, osl], func=AF.Exp,
                                     scale=A_sb[:, col:col + 1]).then_inc(act.h, 1)
                    abar_ms[k] = act.inc()

                    @block.vector
                    def _(e, k=k, tc=tc, col=col, sl=sl, tsl=tsl, osl=osl,
                          bsl=bsl, bc_tgt=bc_tgt):
                        e.wait_ge(q_bc[bsl].h, bc_tgt)
                        e.wait_ge(q_tc.h, tcload_ms[tc])
                        e.tensor_mul(out=bx_sb[:, tsl], in0=u_tc[:, osl],
                                     in1=bbc_sb[:, bsl * TC:(bsl + 1) * TC])
                        e.wait_ge(act.h, abar_ms[k])
                        if k >= 2:
                            e.wait_ge(pool.h, w_ms[k - 2])
                        init = 0.0 if tc == 0 else state_sb[:, col:col + 1]
                        e.tensor_tensor_scan(out=h_sb[:, tsl], data0=abar_sb[:, tsl],
                                             data1=bx_sb[:, tsl], initial=init,
                                             op0=OP.mult, op1=OP.add).then_inc(dve.h, 1)
                    scan_ms[k] = dve.inc()
                    if oc == ODC - 1:
                        bxlast_u[m] = scan_ms[k]

                    @block.gpsimd
                    def _(e, k=k, tsl=tsl, bsl=bsl):
                        e.wait_ge(dve.h, scan_ms[k])
                        if k >= 2:
                            e.wait_ge(dve.h, ybar_ms[k - 2])
                        e.tensor_mul(out=w_sb[:, tsl], in0=h_sb[:, tsl],
                                     in1=cbc_sb[:, bsl * TC:(bsl + 1) * TC]
                                     ).then_inc(pool.h, 1)
                    w_ms[k] = pool.inc()
                    if oc == ODC - 1:
                        wlast_u[m] = w_ms[k]

                    @block.vector
                    def _(e, k=k, sl=sl, col=col, tsl=tsl, osl=osl):
                        e.wait_ge(pool.h, w_ms[k])
                        e.tensor_add(out=y_tc[:, osl], in0=y_tc[:, osl],
                                     in1=w_sb[:, tsl])
                        e.tensor_copy(out=state_sb[:, col:col + 1],
                                      in_=h_sb[:, sl * TC + TC - 1:sl * TC + TC]
                                      ).then_inc(dve.h, 1)
                    ybar_ms[k] = dve.inc()

                if n == 15:
                    @block.vector
                    def _(e):
                        e.tensor_copy(out=bx_sb[:, 0:TC],
                                      in_=bx_sb[:, 0:TC]).then_inc(dve.h, 1)
                    park_gate = dve.inc()

                    for oc in range(ODC):
                        @block.sync
                        def _(e, tc=tc, oc=oc, park_gate=park_gate):
                            e.wait_ge(dve.h, park_gate)
                            e.dma_start(out=cc_in[oc * 128:(oc + 1) * 128,
                                                  tc * TC:(tc + 1) * TC],
                                        in_=y_tc[:, oc * TC:(oc + 1) * TC]
                                        ).then_inc(q_park.h, 16)
                        park_tc[(tc, oc)] = q_park.inc(16)

            def emit_ffn_pair_load(hf, qp):
                sl = qp % 2

                @block.sync
                def _(e, hf=hf, qp=qp, sl=sl):
                    if qp >= 2:
                        e.wait_ge(pe.h, fpe_ms[(hf, 2 * qp - 3)])
                    elif hf == 1:
                        e.wait_ge(pe.h, fpe_ms[(0, HR - 1)])
                    srcg = wg_bf.rearrange("(c p) h -> p c h", p=128)
                    srcu = wu_bf.rearrange("(c p) h -> p c h", p=128)
                    e.dma_start(out=wg2_sb[:, sl * NDC * 256:(sl + 1) * NDC * 256]
                                .rearrange("p (c h) -> p c h", c=NDC),
                                in_=srcg[:, :, qp * 256:(qp + 1) * 256]).then_inc(q_wp[sl].h, 16)
                    e.dma_start(out=wu2_sb[:, sl * NDC * 256:(sl + 1) * NDC * 256]
                                .rearrange("p (c h) -> p c h", c=NDC),
                                in_=srcu[:, :, qp * 256:(qp + 1) * 256]).then_inc(q_wp[sl].h, 16)
                fw_ms[(hf, qp)] = q_wp[sl].inc(32)

            def emit_ffn_half_prologue(hf):
                @block.sync
                def _(e, hf=hf):
                    if hf == 1:
                        e.wait_ge(pe.h, fpe_ms[(0, HR - 1)])
                    for dc in range(NDC):
                        e.dma_start(out=xnf_sb[:, dc * 1024:(dc + 1) * 1024],
                                    in_=xnown_dram[hf * 1024:(hf + 1) * 1024,
                                                   dc * 128:(dc + 1) * 128],
                                    transpose=True).then_inc(q_xnf.h, 16)
                xnf_ms[hf] = q_xnf.inc(16 * NDC)

                for tch in range(2):
                    @block.tensor
                    def _(e, hf=hf, tch=tch):
                        if tch == 0:
                            e.wait_ge(q_xnf.h, xnf_ms[hf])
                            if hf == 1:
                                e.wait_ge(dve.h, x1_ms[0])
                                e.wait_ge(act.h, fact_ps[(0, HR - 2)])
                                e.wait_ge(act.h, fact_ps[(0, HR - 1)])
                        else:
                            e.wait_ge(dve.h, dve.v)
                        for dc in range(NDC):
                            mm = e.matmul(out=ps[0:1, 0:512], lhsT=ones_bf[:],
                                          rhs=xnf_sb[:, dc * 1024 + tch * 512:
                                                     dc * 1024 + (tch + 1) * 512],
                                          start=(dc == 0), stop=(dc == NDC - 1))
                            if dc == NDC - 1:
                                mm.then_inc(pe.h, 1)
                    x1pe = pe.inc()

                    @block.vector
                    def _(e, hf=hf, tch=tch, x1pe=x1pe):
                        e.wait_ge(pe.h, x1pe)
                        dst = slice(hf * 1024 + tch * 512, hf * 1024 + (tch + 1) * 512)
                        e.tensor_copy(out=x1row_f[0:1, dst], in_=ps[0:1, 0:512])
                        e.tensor_copy(out=x1row_bf[0:1, dst], in_=x1row_f[0:1, dst])
                        e.tensor_copy(out=x1row_f[0:1, dst], in_=x1row_f[0:1, dst])
                        e.tensor_copy(out=x1row_f[0:1, dst],
                                      in_=x1row_f[0:1, dst]).then_inc(dve.h, 1)
                    x1_ms[hf] = dve.inc()

            def emit_ffn_unit(f):
                hf, hr = divmod(f, HR)
                if hr == 0:
                    emit_ffn_half_prologue(hf)
                    emit_ffn_pair_load(hf, 0)
                    emit_ffn_pair_load(hf, 1)
                if hr % 2 == 0 and hr + 2 < HR:
                    emit_ffn_pair_load(hf, hr // 2 + 1)
                qp = hr // 2
                wsl = qp % 2
                pb = (hr % 2) * 2048
                hh = (hr % 2) * 128
                pf = None
                if f >= 2:
                    pf = (hf, hr - 2) if hr >= 2 else (0, HR - 2 + hr)

                @block.tensor
                def _(e, f=f, hf=hf, hr=hr, qp=qp, wsl=wsl, pb=pb, hh=hh, pf=pf):
                    e.wait_ge(q_wp[wsl].h, fw_ms[(hf, qp)])
                    e.wait_ge(q_xnf.h, xnf_ms[hf])
                    e.wait_ge(dve.h, x1_ms[hf])
                    if pf is not None:
                        e.wait_ge(act.h, fact_ps[pf])
                    for mat, wsb in ((0, wg2_sb), (1, wu2_sb)):
                        ub = pb + mat * 1024
                        for dc in range(NDC):
                            lhs = wsb[:, wsl * NDC * 256 + dc * 256 + hh:
                                      wsl * NDC * 256 + dc * 256 + hh + 128]
                            for tch in range(2):
                                e.matmul(out=ps[:, ub + tch * 512:ub + (tch + 1) * 512],
                                         lhsT=lhs,
                                         rhs=xnf_sb[:, dc * 1024 + tch * 512:
                                                    dc * 1024 + (tch + 1) * 512],
                                         start=(dc == 0), stop=False)
                        for tch in range(2):
                            mm = e.matmul(out=ps[:, ub + tch * 512:ub + (tch + 1) * 512],
                                          lhsT=n128_bf[:],
                                          rhs=x1row_bf[0:1, hf * 1024 + tch * 512:
                                                       hf * 1024 + (tch + 1) * 512],
                                          start=False, stop=True)
                            if mat == 1 and tch == 1:
                                mm.then_inc(pe.h, 1)
                fpe_ms[(hf, hr)] = pe.inc()

                @block.scalar
                def _(e, f=f, hf=hf, hr=hr, pb=pb, pf=pf):
                    e.wait_ge(pe.h, fpe_ms[(hf, hr)])
                    if pf is not None:
                        e.wait_ge(pool.h, fpool_ms[pf])
                    sl = f % 2
                    dst = slice(sl * 1024, (sl + 1) * 1024)
                    sgc = sgh_sb[:, hr:hr + 1]
                    suc = su_sb[:, hr:hr + 1]
                    e.activation(out=th_sb[:, dst], in_=ps[:, pb:pb + 1024],
                                 func=AF.Tanh, scale=sgc)
                    e.activation(out=gh_sb[:, dst], in_=ps[:, pb:pb + 1024],
                                 func=AF.Identity, scale=sgc)
                    e.activation(out=ups_sb[:, dst], in_=ps[:, pb + 1024:pb + 2048],
                                 func=AF.Identity, scale=suc).then_inc(act.h, 1)
                    e.activation(out=th1_sb[:, dst], in_=th_sb[:, dst],
                                 func=AF.Identity, bias=1.0).then_inc(act.h, 1)
                fact_ps[(hf, hr)] = act.inc()
                fact_th1[(hf, hr)] = act.inc()

                @block.gpsimd
                def _(e, f=f, hf=hf, hr=hr, pf=pf):
                    e.wait_ge(act.h, fact_th1[(hf, hr)])
                    if pf is not None:
                        e.wait_ge(q_ast[f % 2].h, fst_ms[pf])
                    sl = f % 2
                    dst = slice(sl * 1024, (sl + 1) * 1024)
                    e.tensor_mul(out=t3_sb[:, dst], in0=gh_sb[:, dst], in1=ups_sb[:, dst])
                    e.tensor_mul(out=actt_sb[:, dst], in0=t3_sb[:, dst],
                                 in1=th1_sb[:, dst]).then_inc(pool.h, 1)
                fpool_ms[(hf, hr)] = pool.inc()

                @block.sync
                def _(e, f=f, hf=hf, hr=hr):
                    e.wait_ge(pool.h, fpool_ms[(hf, hr)])
                    sl = f % 2
                    e.dma_start(out=act_dram[hr * 128:(hr + 1) * 128,
                                             hf * 1024:(hf + 1) * 1024],
                                in_=actt_sb[:, sl * 1024:(sl + 1) * 1024]
                                ).then_inc(q_ast[sl].h, 16)
                fst_ms[(hf, hr)] = q_ast[f % 2].inc(16)

            fidx = 0
            for m in range(NTC * 16):
                emit_mixer_unit(m)
                tgt = (m + 1) * (2 * HR) // (NTC * 16)
                while fidx < tgt:
                    emit_ffn_unit(fidx)
                    fidx += 1
            while fidx < 2 * HR:
                emit_ffn_unit(fidx)
                fidx += 1

            @block.gpsimd
            def _(e):
                e.wait_ge(q_park.h, q_park.v)
                e.collective_compute(
                    "AllGather", OP.bypass,
                    replica_groups=[[0, 1], [2, 3], [4, 5], [6, 7]],
                    ins=[cc_in.opt()], outs=[cc_out.opt()]).then_inc(ccs.h)
            ccs.inc()

        barrier()

        p4 = contextlib.ExitStack()
        with p4:
            def p4sb(nm, shp, dt):
                return p4.enter_context(nc.sbuf_tensor(nm, shp, dt))
            actq_sb = p4sb("actq_sb", [128, HR * 512], BF16)
            s_sb = p4sb("s_sb", [128, NDC * 512], BF16)
            wdt2_sb = p4sb("wdt2_sb", [128, 2 * 512], BF16)
            th2_sb = p4sb("th2_sb", [128, 8 * 512], BF16)
            gh2_sb = p4sb("gh2_sb", [128, 8 * 512], BF16)
            g2_sb = p4sb("g2_sb", [128, 8 * 512], BF16)
            sg2_sb = p4sb("sg2_sb", [128, 8 * 512], BF16)
            sgt_sb = p4sb("sgt_sb", [128, 2 * 2048], BF16)
            xo_sb = p4sb("xo_sb", [128, 2 * 2048], F32)
            ot_sb = p4sb("ot_sb", [128, 2 * 2048], F32)

            q_aq = ring("q_aq")
            q_ssb = ring("q_ssb")
            q_wd2 = [ring("q_wd20"), ring("q_wd21")]
            q_sg = [ring("q_sg0"), ring("q_sg1")]
            q_cmb = [ring("q_cmb0"), ring("q_cmb1")]
            q_out = [ring("q_out0"), ring("q_out1")]

            aq_ms, qpe_hr, qpe_ms, qact_ms, qdve_ms, qpool_ms, qst_ms = {}, {}, {}, {}, {}, {}, {}
            wd_ms, a1e_ms, out_ms, out_pool, s_ld_tq, cmb_ms = {}, {}, {}, {}, {}, {}

            for tq in range(4):
                @block.sync
                def _(e, tq=tq):
                    if tq >= 1:
                        e.wait_ge(pe.h, qpe_ms[(tq - 1, 3)])
                    for hr in range(HR):
                        e.dma_start(out=actq_sb[:, hr * 512:(hr + 1) * 512],
                                    in_=act_dram[hr * 128:(hr + 1) * 128,
                                                 tq * 512:(tq + 1) * 512]).then_inc(q_aq.h, 16)
                aq_ms[tq] = q_aq.inc(16 * HR)

                @block.sync
                def _(e, tq=tq):
                    e.wait_ge(ccs.h, 1)
                    if tq >= 1:
                        e.wait_ge(pool.h, qpool_ms[(tq - 1, 3)])
                    for arm in nc.switch_core_id(engines=[e], n=128):
                        pos = int(arm) % 2
                        off = pos * TOWN + tq * 512
                        for g in range(NDC):
                            if g < ODC:
                                srcap = cc_out[pos * 1024 + g * 128:
                                               pos * 1024 + (g + 1) * 128,
                                               off:off + 512]
                            else:
                                p_ = 1 - pos
                                srcap = cc_out[p_ * 1024 + (g - ODC) * 128:
                                               p_ * 1024 + (g - ODC + 1) * 128,
                                               off:off + 512]
                            e.dma_start(out=s_sb[:, g * 512:(g + 1) * 512],
                                        in_=srcap).then_inc(q_ssb.h, 16)
                s_ld_tq[tq] = q_ssb.inc(16 * NDC)

                for g4 in range(4):
                    qi = tq * 4 + g4
                    base = (g4 % 2) * 2048
                    for hr in range(HR):
                        sl = hr % 2

                        @block.sync
                        def _(e, tq=tq, g4=g4, hr=hr, sl=sl):
                            if hr >= 2:
                                e.wait_ge(pe.h, qpe_hr[(tq, g4, hr - 2)])
                            elif g4 >= 1:
                                e.wait_ge(pe.h, qpe_hr[(tq, g4 - 1, HR - 2 + hr)])
                            elif tq >= 1:
                                e.wait_ge(pe.h, qpe_hr[(tq - 1, 3, HR - 2 + hr)])
                            e.dma_start(out=wdt2_sb[:, sl * 512:(sl + 1) * 512],
                                        in_=wd_bf[g4 * 512:(g4 + 1) * 512,
                                                  hr * 128:(hr + 1) * 128],
                                        transpose=True).then_inc(q_wd2[sl].h, 16)
                        wd_ms[(tq, g4, hr)] = q_wd2[sl].inc(16)

                        @block.tensor
                        def _(e, tq=tq, g4=g4, hr=hr, sl=sl, base=base):
                            e.wait_ge(q_wd2[sl].h, wd_ms[(tq, g4, hr)])
                            if hr == 0:
                                e.wait_ge(q_aq.h, aq_ms[tq])
                                qi2 = tq * 4 + g4
                                if qi2 >= 2:
                                    pq = divmod(qi2 - 2, 4)
                                    e.wait_ge(act.h, qact_ms[(pq[0], pq[1])])
                                if g4 == 1:
                                    e.wait_ge(dve.h, a1e_ms[tq])
                            rhs = actq_sb[:, hr * 512:(hr + 1) * 512]
                            for j in range(4):
                                mm = e.matmul(out=ps[:, base + j * 512:base + (j + 1) * 512],
                                              lhsT=wdt2_sb[:, sl * 512 + j * 128:
                                                           sl * 512 + (j + 1) * 128],
                                              rhs=rhs, start=(hr == 0), stop=False)
                                if j == 3 and g4 != 0:
                                    mm.then_inc(pe.h, 1)
                            if g4 == 0:
                                mm = e.matmul(out=ps[0:1, 2048:2560], lhsT=ones_bf[:],
                                              rhs=rhs, start=(hr == 0), stop=(hr == HR - 1))
                                mm.then_inc(pe.h, 1)
                        qpe_hr[(tq, g4, hr)] = pe.inc()

                    if g4 == 0:
                        @block.vector
                        def _(e, tq=tq):
                            e.wait_ge(pe.h, qpe_hr[(tq, 0, HR - 1)])
                            e.tensor_copy(out=a1row_f[:], in_=ps[0:1, 2048:2560])
                            e.tensor_copy(out=a1row_bf[:], in_=a1row_f[:])
                            e.tensor_copy(out=a1row_f[:], in_=a1row_f[:])
                            e.tensor_copy(out=a1row_f[:], in_=a1row_f[:])
                            e.tensor_copy(out=a1row_f[:],
                                          in_=a1row_f[:]).then_inc(dve.h, 1)
                        a1e_ms[tq] = dve.inc()

                    @block.tensor
                    def _(e, tq=tq, g4=g4, base=base):
                        if g4 == 0:
                            e.wait_ge(dve.h, a1e_ms[tq])
                        for j in range(4):
                            mm = e.matmul(out=ps[:, base + j * 512:base + (j + 1) * 512],
                                          lhsT=n128_bf[:], rhs=a1row_bf[:],
                                          start=False, stop=True)
                            if j == 3:
                                mm.then_inc(pe.h, 1)
                    qpe_ms[(tq, g4)] = pe.inc()

                    @block.scalar
                    def _(e, tq=tq, g4=g4, base=base, qi=qi):
                        e.wait_ge(pe.h, qpe_ms[(tq, g4)])
                        if qi >= 2:
                            pq = divmod(qi - 2, 4)
                            e.wait_ge(dve.h, qdve_ms[(pq[0], pq[1])])
                        for j in range(4):
                            gl = g4 * 4 + j
                            sl = (qi % 2) * 4 + j
                            dst = slice(sl * 512, (sl + 1) * 512)
                            psv = ps[:, base + j * 512:base + (j + 1) * 512]
                            sdc = sdh_sb[:, gl:gl + 1]
                            e.activation(out=th2_sb[:, dst], in_=psv, func=AF.Tanh,
                                         scale=sdc)
                            la = e.activation(out=gh2_sb[:, dst], in_=psv,
                                              func=AF.Identity, scale=sdc)
                            if j == 3:
                                la.then_inc(act.h, 1)
                    qact_ms[(tq, g4)] = act.inc()

                    @block.vector
                    def _(e, tq=tq, g4=g4, qi=qi):
                        e.wait_ge(act.h, qact_ms[(tq, g4)])
                        if qi >= 2:
                            pq = divmod(qi - 2, 4)
                            e.wait_ge(pool.h, qpool_ms[(pq[0], pq[1])])
                        for j in range(4):
                            sl = (qi % 2) * 4 + j
                            dst = slice(sl * 512, (sl + 1) * 512)
                            la = e.scalar_tensor_tensor(out=g2_sb[:, dst],
                                                        in0=th2_sb[:, dst], scalar=1.0,
                                                        in1=gh2_sb[:, dst],
                                                        op0=OP.add, op1=OP.mult)
                            if j == 3:
                                la.then_inc(dve.h, 1)
                    qdve_ms[(tq, g4)] = dve.inc()

                    @block.gpsimd
                    def _(e, tq=tq, g4=g4, qi=qi):
                        e.wait_ge(dve.h, qdve_ms[(tq, g4)])
                        e.wait_ge(q_ssb.h, s_ld_tq[tq])
                        if qi >= 2:
                            pq = divmod(qi - 2, 4)
                            e.wait_ge(q_sg[qi % 2].h, qst_ms[(pq[0], pq[1])])
                        for j in range(4):
                            gl = g4 * 4 + j
                            sl = (qi % 2) * 4 + j
                            dst = slice(sl * 512, (sl + 1) * 512)
                            la = e.tensor_mul(out=sg2_sb[:, dst], in0=g2_sb[:, dst],
                                              in1=s_sb[:, gl * 512:(gl + 1) * 512])
                            if j == 3:
                                la.then_inc(pool.h, 1)
                    qpool_ms[(tq, g4)] = pool.inc()

                    @block.sync
                    def _(e, tq=tq, g4=g4, qi=qi):
                        e.wait_ge(pool.h, qpool_ms[(tq, g4)])
                        for j in range(4):
                            gl = g4 * 4 + j
                            sl = (qi % 2) * 4 + j
                            e.dma_start(out=sg_dram[gl * 128:(gl + 1) * 128,
                                                    tq * 512:(tq + 1) * 512],
                                        in_=sg2_sb[:, sl * 512:(sl + 1) * 512]
                                        ).then_inc(q_sg[qi % 2].h, 16)
                    qst_ms[(tq, g4)] = q_sg[qi % 2].inc(16 * 4)

                for tt in range(4):
                    ti = tq * 4 + tt
                    row0 = ti * 128
                    csl = ti % 2

                    @block.sync
                    def _(e, tq=tq, ti=ti, row0=row0, csl=csl):
                        e.wait_ge(q_sg[0].h, q_sg[0].v)
                        e.wait_ge(q_sg[1].h, q_sg[1].v)
                        if ti >= 2:
                            e.wait_ge(pool.h, out_pool[ti - 2])
                        e.dma_start(out=sgt_sb[:, csl * 2048:(csl + 1) * 2048],
                                    in_=sg_dram[0:D, ti * 128:(ti + 1) * 128],
                                    transpose=True).then_inc(q_cmb[csl].h, 16)
                        e.dma_start(out=xo_sb[:, csl * 2048:(csl + 1) * 2048],
                                    in_=x_own[row0:row0 + 128, :]).then_inc(q_cmb[csl].h, 16)
                    cmb_ms[ti] = q_cmb[csl].inc(32)

                    @block.gpsimd
                    def _(e, ti=ti, csl=csl):
                        e.wait_ge(q_cmb[csl].h, cmb_ms[ti])
                        if ti >= 2:
                            e.wait_ge(q_out[csl].h, out_ms[ti - 2])
                        a = slice(csl * 2048, (csl + 1) * 2048)
                        e.tensor_add(out=ot_sb[:, a], in0=xo_sb[:, a],
                                     in1=sgt_sb[:, a]).then_inc(pool.h, 1)
                    out_pool[ti] = pool.inc()

                    @block.sync
                    def _(e, ti=ti, row0=row0, csl=csl):
                        e.wait_ge(pool.h, out_pool[ti])
                        e.dma_start(out=out_own[row0:row0 + 128, :],
                                    in_=ot_sb[:, csl * 2048:(csl + 1) * 2048]
                                    ).then_inc(q_out[csl].h, 16)
                    out_ms[ti] = q_out[csl].inc(16)

            if debug:
                barrier()

                @block.gpsimd
                def _(e):
                    e.dma_start(out=dbg["dbg_xn"], in_=xn_dram).then_inc(gq.h, 16)
                    e.dma_start(out=dbg["dbg_bc"], in_=bc_dram).then_inc(gq.h, 16)
                    e.dma_start(out=dbg["dbg_dt"], in_=dt_dram).then_inc(gq.h, 16)
                    e.dma_start(out=dbg["dbg_u"], in_=u_dram).then_inc(gq.h, 16)
                    e.dma_start(out=dbg["dbg_s"], in_=cc_in).then_inc(gq.h, 16)
                    e.dma_start(out=dbg["dbg_act"], in_=act_dram).then_inc(gq.h, 16)
                    e.dma_start(out=dbg["dbg_st"], in_=st_dram).then_inc(gq.h, 16)
                gq.inc(16 * 7)

            @block.sync
            def _(e):
                e.wait_ge(q_out[0].h, q_out[0].v)
                e.wait_ge(q_out[1].h, q_out[1].v)

            @block.gpsimd
            def _(e):
                e.wait_ge(gq.h, gq.v)

    return nc


_NC_CACHE = {}


def _get_nc(debug=False):
    key = bool(debug)
    if key not in _NC_CACHE:
        _NC_CACHE[key] = build(debug=key)
    return _NC_CACHE[key]


def _perm(dh):
    return np.arange(D) if dh == 0 else np.concatenate(
        [np.arange(1024, 2048), np.arange(0, 1024)])


def make_inputs(L0_attn_x, L0_attn_x_proj_w, L0_attn_dt_proj_w, L0_attn_dt_proj_b,
                L0_attn_A_log, L0_attn_D_param, L0_ffn_wq_gate, L0_ffn_s_gate,
                L0_ffn_wq_up, L0_ffn_s_up, L0_ffn_wq_down, L0_ffn_s_down):
    x = np.ascontiguousarray(np.asarray(L0_attn_x, dtype=np.float32))
    xp = np.asarray(L0_attn_x_proj_w, dtype=np.float32)
    wdt = np.asarray(L0_attn_dt_proj_w, dtype=np.float32)
    dtb = np.asarray(L0_attn_dt_proj_b, dtype=np.float32)
    alog = np.asarray(L0_attn_A_log, dtype=np.float32)
    dp = np.asarray(L0_attn_D_param, dtype=np.float32)
    wqg = np.asarray(L0_ffn_wq_gate, dtype=np.int32)
    wqu = np.asarray(L0_ffn_wq_up, dtype=np.int32)
    wqd = np.asarray(L0_ffn_wq_down, dtype=np.int32)
    sgv = np.asarray(L0_ffn_s_gate, dtype=np.float32)
    suv = np.asarray(L0_ffn_s_up, dtype=np.float32)
    sdv = np.asarray(L0_ffn_s_down, dtype=np.float32)

    per_dh = {}
    for dh in range(2):
        perm = _perm(dh)
        own = perm[:1024]
        per_dh[dh] = dict(
            wgT=np.ascontiguousarray(wqg.T[perm, :]),
            wuT=np.ascontiguousarray(wqu.T[perm, :]),
            wdn=np.ascontiguousarray(wqd[perm, :]),
            xprojT=np.ascontiguousarray(xp.T[perm, :]),
            wdtT_own=np.ascontiguousarray(wdt.T[:, own]),
            dtb_own=np.ascontiguousarray(dtb[own].reshape(ODC, 128).T),
            Alog_own=np.ascontiguousarray(
                alog[own].reshape(ODC, 128, N).transpose(1, 0, 2).reshape(128, ODC * N)),
            Dp_own=np.ascontiguousarray(dp[own].reshape(ODC, 128).T),
            sd_in=np.ascontiguousarray(sdv[perm].reshape(NDC, 128).T),
        )
    sg_t = np.ascontiguousarray(sgv.reshape(HR, 128).T)
    su_t = np.ascontiguousarray(suv.reshape(HR, 128).T)
    cst_arr = np.broadcast_to(
        np.array([EPS, -0.5, -1.0, 1.0], np.float32), (128, 4)).copy()
    ones_arr = np.ones((128, 1), ml_dtypes.bfloat16)
    n128_arr = np.full((1, 128), -128.0, ml_dtypes.bfloat16)

    in_maps = []
    for c in range(8):
        seq, dh = c // 2, c % 2
        perm = _perm(dh)
        xs = np.ascontiguousarray(x[seq][:, perm])
        m = dict(per_dh[dh])
        m.update(x_seq=xs,
                 x_own=np.ascontiguousarray(xs[dh * TOWN:(dh + 1) * TOWN]),
                 sg_in=sg_t, su_in=su_t,
                 cst_in=cst_arr, ones_in=ones_arr, n128_in=n128_arr)
        in_maps.append(m)
    return in_maps


def run(inputs, debug=False, **kw):
    in_maps = make_inputs(**inputs)
    nc = _get_nc(debug=debug)
    return run_bass_kernel_spmd(nc, in_maps, core_ids=list(range(8)), **kw)


def kernel(**inputs):
    res = run(inputs, debug=False)
    out = np.zeros((4, SEQ, D), dtype=np.float32)
    for c in range(8):
        seq, dh = c // 2, c % 2
        inv = np.argsort(_perm(dh))
        r = np.asarray(res.results[c]["out_own"])
        out[seq, dh * TOWN:(dh + 1) * TOWN, :] = r[:, inv]
    return out


# version 03
# speedup vs baseline: 1.0112x; 1.0112x over previous
import numpy as np
import ml_dtypes
import contextlib
import concourse.bass as bass
import concourse.mybir as mybir
from concourse.bass_utils import run_bass_kernel_spmd

F32 = mybir.dt.float32
BF16 = mybir.dt.bfloat16
I32 = mybir.dt.int32
AF = mybir.ActivationFunctionType
OP = mybir.AluOpType
AX = mybir.AxisListType

SEQ, TOWN, D, N, R, HID = 4096, 2048, 2048, 16, 128, 5632
NDC, ODC, HR = 16, 8, 44
TC, NTC = 1024, 4
EPS = 1e-5


def load_act_set(nc, e, name):
    from concourse.hw_specs import get_activation_tables
    idx = list(get_activation_tables(nc.m.arch).keys()).index(name)
    e.add_instruction(mybir.InstLoadActFuncSet(
        name=nc.get_next_instruction_name(), act_func_set_id=idx, ins=[], outs=[]))


class Sem:
    def __init__(self, h):
        self.h = h
        self.v = 0

    def inc(self, k=1):
        self.v += k
        return self.v


def build(debug=False):
    nc = bass.Bass("TRN2", target_bir_lowering=False)

    x_seq = nc.dram_tensor("x_seq", [SEQ, D], F32, kind="ExternalInput").ap()
    x_own = nc.dram_tensor("x_own", [TOWN, D], F32, kind="ExternalInput").ap()
    xprojT = nc.dram_tensor("xprojT", [D, 160], F32, kind="ExternalInput").ap()
    wdtT_own = nc.dram_tensor("wdtT_own", [R, ODC * 128], F32, kind="ExternalInput").ap()
    dtb_own = nc.dram_tensor("dtb_own", [128, ODC], F32, kind="ExternalInput").ap()
    Alog_own = nc.dram_tensor("Alog_own", [128, ODC * N], F32, kind="ExternalInput").ap()
    Dp_own = nc.dram_tensor("Dp_own", [128, ODC], F32, kind="ExternalInput").ap()
    wgT = nc.dram_tensor("wgT", [D, HID], I32, kind="ExternalInput").ap()
    wuT = nc.dram_tensor("wuT", [D, HID], I32, kind="ExternalInput").ap()
    wdn = nc.dram_tensor("wdn", [D, HID], I32, kind="ExternalInput").ap()
    sg_in = nc.dram_tensor("sg_in", [128, HR], F32, kind="ExternalInput").ap()
    su_in = nc.dram_tensor("su_in", [128, HR], F32, kind="ExternalInput").ap()
    sd_in = nc.dram_tensor("sd_in", [128, NDC], F32, kind="ExternalInput").ap()
    cst_in = nc.dram_tensor("cst_in", [128, 4], F32, kind="ExternalInput").ap()
    ones_in = nc.dram_tensor("ones_in", [128, 1], BF16, kind="ExternalInput").ap()
    n128_in = nc.dram_tensor("n128_in", [1, 128], BF16, kind="ExternalInput").ap()
    out_own = nc.dram_tensor("out_own", [TOWN, D], F32, kind="ExternalOutput").ap()

    xn_dram = nc.dram_tensor("xn_dram", [SEQ, D], BF16, kind="Internal").ap()
    xnown_dram = nc.dram_tensor("xnown_dram", [TOWN, D], BF16, kind="Internal").ap()
    bc_dram = nc.dram_tensor("bc_dram", [32, SEQ], BF16, kind="Internal").ap()
    dt_dram = nc.dram_tensor("dt_dram", [ODC * 128, SEQ], BF16, kind="Internal").ap()
    u_dram = nc.dram_tensor("u_dram", [ODC * 128, SEQ], BF16, kind="Internal").ap()
    dxn_dram = nc.dram_tensor("dxn_dram", [ODC * 128, SEQ], BF16, kind="Internal").ap()
    wg_bf = nc.dram_tensor("wg_bf", [D, HID], BF16, kind="Internal").ap()
    wu_bf = nc.dram_tensor("wu_bf", [D, HID], BF16, kind="Internal").ap()
    wd_bf = nc.dram_tensor("wd_bf", [D, HID], BF16, kind="Internal").ap()
    act_dram = nc.dram_tensor("act_dram", [HID, TOWN], BF16, kind="Internal").ap()
    cc_in = nc.dram_tensor("cc_in", [ODC * 128, SEQ], BF16, kind="Internal").ap()
    cc_out = nc.dram_tensor("cc_out", [2 * ODC * 128, SEQ], BF16, kind="Internal").ap()
    sg_dram = nc.dram_tensor("sg_dram", [D, TOWN], BF16, kind="Internal").ap()
    st_dram = nc.dram_tensor("st_dram", [128, 16 * 48], F32, kind="Internal").ap()
    dbg = {}
    if debug:
        for nm, shp in (("dbg_xn", [SEQ, D]), ("dbg_bc", [32, SEQ]),
                        ("dbg_dt", [ODC * 128, SEQ]), ("dbg_u", [ODC * 128, SEQ]),
                        ("dbg_s", [ODC * 128, SEQ]), ("dbg_act", [HID, TOWN]),
                        ("dbg_st", [128, 16 * 48])):
            dbg[nm] = nc.dram_tensor(nm, shp, F32, kind="ExternalOutput").ap()

    es = contextlib.ExitStack()

    def sb(name, shape, dt):
        return es.enter_context(nc.sbuf_tensor(name, shape, dt))

    with es, nc.Block() as block:
        xprojT_bf = sb("xprojT_bf", [128, NDC * 160], BF16)
        wdtT_bf = sb("wdtT_bf", [128, ODC * 128], BF16)
        dtb_sb = sb("dtb_sb", [128, ODC], F32)
        A_sb = sb("A_sb", [128, ODC * N], F32)
        Dp_sb = sb("Dp_sb", [128, ODC], F32)
        sgh_sb = sb("sgh_sb", [128, HR], F32)
        su_sb = sb("su_sb", [128, HR], F32)
        sdh_sb = sb("sdh_sb", [128, NDC], F32)
        ones_bf = sb("ones_bf", [128, 1], BF16)
        n128_bf = sb("n128_bf", [1, 128], BF16)
        dtlowT_bf = sb("dtlowT_bf", [128, SEQ], BF16)
        bcT_bf = sb("bcT_bf", [32, SEQ], BF16)
        x1row_f = sb("x1row_f", [1, TOWN], F32)
        x1row_bf = sb("x1row_bf", [1, TOWN], BF16)
        a1row_f = sb("a1row_f", [1, 512], F32)
        a1row_bf = sb("a1row_bf", [1, 512], BF16)
        state_sb = sb("state_sb", [128, 128], F32)
        const_sb = sb("const_sb", [128, 4], F32)
        nc.const_aps.register(F32, EPS, const_sb[:, 0:1])
        nc.const_aps.register(F32, -0.5, const_sb[:, 1:2])
        nc.const_aps.register(F32, -1.0, const_sb[:, 2:3])
        nc.const_aps.register(F32, 1.0, const_sb[:, 3:4])
        ps = es.enter_context(nc.psum_tensor("ps", [128, 4096], F32))

        pe = Sem(es.enter_context(nc.semaphore("pe")))
        act = Sem(es.enter_context(nc.semaphore("act")))
        dve = Sem(es.enter_context(nc.semaphore("dve")))
        pool = Sem(es.enter_context(nc.semaphore("pool")))
        ccs = Sem(es.enter_context(nc.semaphore("ccs")))
        bar = Sem(es.enter_context(nc.semaphore("bar")))
        rings = {}

        def ring(name):
            if name not in rings:
                rings[name] = Sem(es.enter_context(nc.semaphore(name)))
            return rings[name]

        gq = ring("gq")

        def barrier():
            tgt = bar.v + 5
            ring_waits = [(r.h, r.v) for r in rings.values() if r.v]

            @block.sync
            def _(e):
                for h, v in ring_waits:
                    e.wait_ge(h, v)
                e.sem_inc(bar.h, 1)
                e.wait_ge(bar.h, tgt)

            @block.tensor
            def _(e):
                e.wait_ge(pe.h, pe.v)
                e.sem_inc(bar.h, 1)
                e.wait_ge(bar.h, tgt)

            @block.vector
            def _(e):
                e.wait_ge(dve.h, dve.v)
                e.sem_inc(bar.h, 1)
                e.wait_ge(bar.h, tgt)

            @block.scalar
            def _(e):
                e.wait_ge(act.h, act.v)
                e.sem_inc(bar.h, 1)
                e.wait_ge(bar.h, tgt)

            @block.gpsimd
            def _(e):
                e.wait_ge(gq.h, gq.v)
                e.wait_ge(pool.h, pool.v)
                e.sem_inc(bar.h, 1)
                e.wait_ge(bar.h, tgt)
            bar.v = tgt

        @block.gpsimd
        def _(g):
            g.dma_start(out=xprojT_bf[:].rearrange("p (c f) -> p c f", c=NDC),
                        in_=xprojT.rearrange("(c p) f -> p c f", p=128)).then_inc(gq.h, 16)
            g.dma_start(out=wdtT_bf[:], in_=wdtT_own).then_inc(gq.h, 16)
            g.dma_start(out=wg_bf, in_=wgT).then_inc(gq.h, 16)
            g.dma_start(out=wu_bf, in_=wuT).then_inc(gq.h, 16)
            g.dma_start(out=wd_bf, in_=wdn).then_inc(gq.h, 16)
        gq.inc(16 * 5)

        q_par = ring("q_par")

        @block.sync
        def _(e):
            e.dma_start(out=dtb_sb[:], in_=dtb_own).then_inc(q_par.h, 16)
            e.dma_start(out=A_sb[:], in_=Alog_own).then_inc(q_par.h, 16)
            e.dma_start(out=Dp_sb[:], in_=Dp_own).then_inc(q_par.h, 16)
            e.dma_start(out=sgh_sb[:], in_=sg_in).then_inc(q_par.h, 16)
            e.dma_start(out=su_sb[:], in_=su_in).then_inc(q_par.h, 16)
            e.dma_start(out=sdh_sb[:], in_=sd_in).then_inc(q_par.h, 16)
            e.dma_start(out=const_sb[:], in_=cst_in).then_inc(q_par.h, 16)
            e.dma_start(out=ones_bf[:], in_=ones_in).then_inc(q_par.h, 16)
            e.dma_start(out=n128_bf[:], in_=n128_in).then_inc(q_par.h, 16)
        q_par.inc(16 * 9)

        @block.scalar
        def _(e):
            load_act_set(nc, e, "natural_log_exp_and_others")
            e.wait_ge(q_par.h, q_par.v)
            e.activation(out=A_sb[:], in_=A_sb[:], func=AF.Exp).then_inc(act.h, 1)
        a_exp = act.inc()

        @block.vector
        def _(e):
            e.wait_ge(act.h, a_exp)
            e.wait_ge(q_par.h, q_par.v)
            e.tensor_scalar_mul(out=A_sb[:], in0=A_sb[:], scalar1=-1.0)
            e.tensor_scalar_mul(out=sgh_sb[:], in0=sgh_sb[:], scalar1=0.5)
            e.tensor_scalar_mul(out=sdh_sb[:], in0=sdh_sb[:], scalar1=0.5).then_inc(dve.h, 1)
        dve.inc()

        p1 = contextlib.ExitStack()
        with p1:
            xt_sb = p1.enter_context(nc.sbuf_tensor("xt_sb", [128, 2 * D], F32))
            dump_sb = p1.enter_context(nc.sbuf_tensor("dump_sb", [128, 2 * D], F32))
            st_sb = p1.enter_context(nc.sbuf_tensor("st_sb", [128, 16 * 48], F32))
            xnt_sb = p1.enter_context(nc.sbuf_tensor("xnt_sb", [128, 4 * D], BF16))

            ln_tiles = [("seq", t) for t in range(SEQ // 128)] + \
                       [("own", t) for t in range(TOWN // 128)]
            NT = len(ln_tiles)
            SSQ, SSUM, M_, SS_, MSQ, VAR, LNV, RSTD, NEGM = [i * 48 for i in range(9)]
            sum_act, sum_dve, xn_ms, st_ms = {}, {}, {}, {}
            q_xt = [ring("q_xt0"), ring("q_xt1")]
            q_xnt = [ring("q_xnt%d" % i) for i in range(4)]

            for k, (src_, t) in enumerate(ln_tiles):
                sl = k % 2
                xsrc = x_seq if src_ == "seq" else x_own

                @block.sync
                def _(e, k=k, sl=sl, t=t, xsrc=xsrc):
                    if k >= 2:
                        e.wait_ge(dve.h, sum_dve[k - 2])
                    e.dma_start(out=xt_sb[:, sl * D:(sl + 1) * D],
                                in_=xsrc[t * 128:(t + 1) * 128, :]).then_inc(q_xt[sl].h, 16)
                xt_tgt = q_xt[sl].inc(16)

                @block.scalar
                def _(e, k=k, sl=sl, xt_tgt=xt_tgt):
                    e.wait_ge(q_xt[sl].h, xt_tgt)
                    if k >= 2:
                        e.wait_ge(dve.h, sum_dve[k - 2])
                    e.activation(out=dump_sb[:, sl * D:(sl + 1) * D],
                                 in_=xt_sb[:, sl * D:(sl + 1) * D],
                                 func=AF.Square).then_inc(act.h, 1)
                sum_act[k] = act.inc()

                @block.vector
                def _(e, k=k, sl=sl, xt_tgt=xt_tgt):
                    e.wait_ge(q_xt[sl].h, xt_tgt)
                    e.wait_ge(act.h, sum_act[k])
                    e.reduce_sum(out=st_sb[:, SSQ + k:SSQ + k + 1],
                                 in_=dump_sb[:, sl * D:(sl + 1) * D], axis=AX.X)
                    e.reduce_sum(out=st_sb[:, SSUM + k:SSUM + k + 1],
                                 in_=xt_sb[:, sl * D:(sl + 1) * D],
                                 axis=AX.X).then_inc(dve.h, 1)
                sum_dve[k] = dve.inc()

            @block.vector
            def _(e):
                e.wait_ge(act.h, sum_act[NT - 1])
                e.wait_ge(dve.h, sum_dve[NT - 1])
                e.tensor_scalar_mul(out=st_sb[:, M_:M_ + NT],
                                    in0=st_sb[:, SSUM:SSUM + NT], scalar1=1.0 / D)
                e.tensor_scalar_mul(out=st_sb[:, SS_:SS_ + NT],
                                    in0=st_sb[:, SSQ:SSQ + NT], scalar1=1.0 / D)
                e.tensor_copy(out=dump_sb[:, 0:D], in_=dump_sb[:, 0:D])
                e.tensor_copy(out=dump_sb[:, 0:D], in_=dump_sb[:, 0:D])
                e.tensor_mul(out=st_sb[:, MSQ:MSQ + NT], in0=st_sb[:, M_:M_ + NT],
                             in1=st_sb[:, M_:M_ + NT])
                e.tensor_copy(out=dump_sb[:, 0:D], in_=dump_sb[:, 0:D])
                e.tensor_copy(out=dump_sb[:, 0:D], in_=dump_sb[:, 0:D])
                e.tensor_sub(out=st_sb[:, VAR:VAR + NT], in0=st_sb[:, SS_:SS_ + NT],
                             in1=st_sb[:, MSQ:MSQ + NT])
                e.tensor_copy(out=dump_sb[:, 0:D], in_=dump_sb[:, 0:D])
                e.tensor_copy(out=dump_sb[:, 0:D],
                              in_=dump_sb[:, 0:D]).then_inc(dve.h, 1)
            stats_dve = dve.inc()

            @block.scalar
            def _(e):
                e.wait_ge(dve.h, stats_dve)
                e.activation(out=st_sb[:, LNV:LNV + NT], in_=st_sb[:, VAR:VAR + NT],
                             func=AF.Ln, bias=EPS)
                e.activation(out=dump_sb[:, 0:D], in_=dump_sb[:, 0:D], func=AF.Identity)
                e.activation(out=dump_sb[:, 0:D], in_=dump_sb[:, 0:D], func=AF.Identity)
                e.activation(out=st_sb[:, RSTD:RSTD + NT], in_=st_sb[:, LNV:LNV + NT],
                             func=AF.Exp, scale=const_sb[:, 1:2])
                e.activation(out=dump_sb[:, 0:D], in_=dump_sb[:, 0:D], func=AF.Identity)
                e.activation(out=dump_sb[:, 0:D], in_=dump_sb[:, 0:D],
                             func=AF.Identity).then_inc(act.h, 1)
            stats_act = act.inc()

            @block.vector
            def _(e):
                e.wait_ge(act.h, stats_act)
                e.scalar_tensor_tensor(out=st_sb[:, NEGM:NEGM + NT],
                                       in0=st_sb[:, M_:M_ + NT], scalar=-1.0,
                                       in1=st_sb[:, RSTD:RSTD + NT],
                                       op0=OP.mult, op1=OP.mult)
                e.tensor_copy(out=dump_sb[:, 0:D], in_=dump_sb[:, 0:D])
                e.tensor_copy(out=dump_sb[:, 0:D],
                              in_=dump_sb[:, 0:D]).then_inc(dve.h, 1)
            negm_dve = dve.inc()

            def emit_xn_store(j):
                src_, t = ln_tiles[j]
                xdst = xn_dram if src_ == "seq" else xnown_dram
                gate = xn_ms[j + 1] if (j + 1) in xn_ms else xn_ms[j]
                qs = q_xnt[j % 4]

                @block.sync
                def _(e, j=j, t=t, xdst=xdst, gate=gate, qs=qs):
                    e.wait_ge(dve.h, gate)
                    e.dma_start(out=xdst[t * 128:(t + 1) * 128, :],
                                in_=xnt_sb[:, (j % 4) * D:(j % 4 + 1) * D]
                                ).then_inc(qs.h, 16)
                st_ms[j] = qs.inc(16)

            for k, (src_, t) in enumerate(ln_tiles):
                sl = k % 2
                xsrc = x_seq if src_ == "seq" else x_own

                @block.sync
                def _(e, k=k, sl=sl, t=t, xsrc=xsrc):
                    if k >= 2:
                        e.wait_ge(dve.h, xn_ms[k - 2])
                    else:
                        e.wait_ge(dve.h, sum_dve[NT - 2 + k])
                    e.dma_start(out=xt_sb[:, sl * D:(sl + 1) * D],
                                in_=xsrc[t * 128:(t + 1) * 128, :]).then_inc(q_xt[sl].h, 16)
                xt_tgt = q_xt[sl].inc(16)

                @block.vector
                def _(e, k=k, sl=sl, xt_tgt=xt_tgt):
                    e.wait_ge(q_xt[sl].h, xt_tgt)
                    e.wait_ge(dve.h, negm_dve)
                    if k >= 4:
                        e.wait_ge(q_xnt[k % 4].h, st_ms[k - 4])
                    e.tensor_scalar(out=xnt_sb[:, (k % 4) * D:(k % 4 + 1) * D],
                                    in0=xt_sb[:, sl * D:(sl + 1) * D],
                                    scalar1=st_sb[:, RSTD + k:RSTD + k + 1],
                                    scalar2=st_sb[:, NEGM + k:NEGM + k + 1],
                                    op0=OP.mult, op1=OP.add).then_inc(dve.h, 1)
                xn_ms[k] = dve.inc()
                if k >= 1:
                    emit_xn_store(k - 1)

            @block.vector
            def _(e):
                e.tensor_copy(out=dump_sb[:, 0:D],
                              in_=dump_sb[:, 0:D]).then_inc(dve.h, 1)
            xn_ms[NT] = dve.inc()
            emit_xn_store(NT - 1)

            q_std = ring("q_std")

            @block.sync
            def _(e):
                e.wait_ge(dve.h, xn_ms[NT])
                e.dma_start(out=st_dram, in_=st_sb[:]).then_inc(q_std.h, 16)
            q_std.inc(16)

        barrier()

        p2 = contextlib.ExitStack()
        with p2:
            xnT_sb = p2.enter_context(nc.sbuf_tensor("xnT_sb", [128, 2 * 2048], BF16))
            q_xnT = [ring("q_xnT0"), ring("q_xnT1")]
            q_bcd = ring("q_bcd")
            mm_ms, evac_ms = {}, {}
            for th2 in range(2):
                for dc in range(NDC):
                    j = th2 * NDC + dc
                    sl = dc % 2

                    @block.sync
                    def _(e, j=j, sl=sl, th2=th2, dc=dc):
                        if j >= 2:
                            e.wait_ge(pe.h, mm_ms[j - 2])
                        e.dma_start(out=xnT_sb[:, sl * 2048:(sl + 1) * 2048],
                                    in_=xn_dram[th2 * 2048:(th2 + 1) * 2048,
                                                dc * 128:(dc + 1) * 128],
                                    transpose=True).then_inc(q_xnT[sl].h, 16)
                    xnT_tgt = q_xnT[sl].inc(16)

                    @block.tensor
                    def _(e, j=j, sl=sl, dc=dc, th2=th2, xnT_tgt=xnT_tgt):
                        e.wait_ge(q_xnT[sl].h, xnT_tgt)
                        if dc == 0 and th2 == 1:
                            e.wait_ge(dve.h, evac_ms[0])
                        for tch in range(4):
                            e.matmul(out=ps[:, tch * 512:(tch + 1) * 512],
                                     lhsT=xprojT_bf[:, dc * 160:dc * 160 + 128],
                                     rhs=xnT_sb[:, sl * 2048 + tch * 512:
                                                sl * 2048 + (tch + 1) * 512],
                                     start=(dc == 0), stop=(dc == NDC - 1))
                        for tch in range(4):
                            mm = e.matmul(out=ps[0:32, 2048 + tch * 512:2048 + (tch + 1) * 512],
                                          lhsT=xprojT_bf[:, dc * 160 + 128:dc * 160 + 160],
                                          rhs=xnT_sb[:, sl * 2048 + tch * 512:
                                                     sl * 2048 + (tch + 1) * 512],
                                          start=(dc == 0), stop=(dc == NDC - 1))
                            if tch == 3:
                                mm.then_inc(pe.h, 1)
                    mm_ms[j] = pe.inc()

                @block.vector
                def _(e, th2=th2):
                    e.wait_ge(pe.h, mm_ms[th2 * NDC + NDC - 1])
                    e.tensor_copy(out=dtlowT_bf[:, th2 * 2048:(th2 + 1) * 2048],
                                  in_=ps[:, 0:2048])
                    e.tensor_copy(out=bcT_bf[:, th2 * 2048:(th2 + 1) * 2048],
                                  in_=ps[0:32, 2048:4096])
                    e.tensor_copy(out=xnT_sb[:, 0:1024],
                                  in_=xnT_sb[:, 0:1024]).then_inc(dve.h, 1)
                evac_ms[th2] = dve.inc()

            @block.sync
            def _(e):
                e.wait_ge(dve.h, evac_ms[1])
                e.dma_start(out=bc_dram, in_=bcT_bf[:]).then_inc(q_bcd.h, 16)
            q_bcd.inc(16)

        barrier()

        p25 = contextlib.ExitStack()
        with p25:
            sp_a = p25.enter_context(nc.sbuf_tensor("sp_a", [128, 2048], F32))
            sp_b = p25.enter_context(nc.sbuf_tensor("sp_b", [128, 2048], F32))
            dtu_sb = p25.enter_context(nc.sbuf_tensor("dtu_sb", [128, 3 * 2048], BF16))
            xnm_sb = p25.enter_context(nc.sbuf_tensor("xnm_sb", [128, 2048], BF16))
            q_xnm = ring("q_xnm")
            q_dtu = ring("q_dtu")

            act_p25, dve_p25, park_ms = {}, {}, {}
            for th2 in range(2):
                for oc in range(ODC):
                    j = th2 * ODC + oc

                    @block.sync
                    def _(e, j=j, th2=th2, oc=oc):
                        if j >= 1:
                            e.wait_ge(dve.h, dve_p25[j - 1])
                        e.dma_start(out=xnm_sb[:],
                                    in_=xn_dram[th2 * 2048:(th2 + 1) * 2048,
                                                oc * 128:(oc + 1) * 128],
                                    transpose=True).then_inc(q_xnm.h, 16)
                    xnm_tgt = q_xnm.inc(16)

                    @block.tensor
                    def _(e, j=j, th2=th2, oc=oc):
                        if j >= 1:
                            e.wait_ge(act.h, act_p25[j - 1])
                        for i in range(4):
                            mm = e.matmul(out=ps[:, i * 512:(i + 1) * 512],
                                          lhsT=wdtT_bf[:, oc * 128:(oc + 1) * 128],
                                          rhs=dtlowT_bf[:, th2 * 2048 + i * 512:
                                                        th2 * 2048 + (i + 1) * 512],
                                          start=True, stop=True)
                            if i == 3:
                                mm.then_inc(pe.h, 1)
                    pe_dt = pe.inc()

                    @block.scalar
                    def _(e, j=j, oc=oc, pe_dt=pe_dt):
                        e.wait_ge(pe.h, pe_dt)
                        if j >= 1:
                            e.wait_ge(dve.h, dve_p25[j - 1])
                        bias = dtb_sb[:, oc:oc + 1]
                        e.activation(out=sp_a[:], in_=ps[:, 0:2048], func=AF.Abs, bias=bias)
                        e.activation(out=sp_b[:], in_=sp_a[:], func=AF.Exp,
                                     scale=const_sb[:, 2:3])
                        e.activation(out=sp_a[:], in_=sp_b[:], func=AF.Ln, bias=1.0)
                        e.activation(out=sp_b[:], in_=ps[:, 0:2048], func=AF.Relu,
                                     bias=bias).then_inc(act.h, 1)
                    act_p25[j] = act.inc()

                    @block.vector
                    def _(e, j=j, oc=oc, xnm_tgt=xnm_tgt):
                        e.wait_ge(act.h, act_p25[j])
                        if j >= 1:
                            e.wait_ge(q_dtu.h, park_ms[j - 1])
                        e.tensor_add(out=dtu_sb[:, 0:2048], in0=sp_a[:], in1=sp_b[:])
                        e.wait_ge(q_xnm.h, xnm_tgt)
                        e.tensor_mul(out=dtu_sb[:, 2048:4096],
                                     in0=dtu_sb[:, 0:2048], in1=xnm_sb[:])
                        e.tensor_scalar_mul(out=dtu_sb[:, 4096:6144], in0=xnm_sb[:],
                                            scalar1=Dp_sb[:, oc:oc + 1])
                        e.tensor_copy(out=sp_a[:], in_=sp_a[:]).then_inc(dve.h, 1)
                    dve_p25[j] = dve.inc()

                    @block.sync
                    def _(e, j=j, th2=th2, oc=oc):
                        e.wait_ge(dve.h, dve_p25[j])
                        rows = slice(oc * 128, (oc + 1) * 128)
                        cols = slice(th2 * 2048, (th2 + 1) * 2048)
                        e.dma_start(out=dt_dram[rows, cols],
                                    in_=dtu_sb[:, 0:2048]).then_inc(q_dtu.h, 16)
                        e.dma_start(out=u_dram[rows, cols],
                                    in_=dtu_sb[:, 2048:4096]).then_inc(q_dtu.h, 16)
                        e.dma_start(out=dxn_dram[rows, cols],
                                    in_=dtu_sb[:, 4096:6144]).then_inc(q_dtu.h, 16)
                    park_ms[j] = q_dtu.inc(48)

        barrier()

        @block.scalar
        def _(e):
            load_act_set(nc, e, "exp_and_others")

        p3 = contextlib.ExitStack()
        with p3:
            def p3sb(nm, shp, dt):
                return p3.enter_context(nc.sbuf_tensor(nm, shp, dt))
            dt_tc = p3sb("dt_tc", [128, ODC * TC], BF16)
            u_tc = p3sb("u_tc", [128, ODC * TC], BF16)
            y_tc = p3sb("y_tc", [128, ODC * TC], BF16)
            bbc_sb = p3sb("bbc_sb", [128, 2 * TC], BF16)
            cbc_sb = p3sb("cbc_sb", [128, 2 * TC], BF16)
            abar_sb = p3sb("abar_sb", [128, 2 * TC], F32)
            bx_sb = p3sb("bx_sb", [128, 2 * TC], BF16)
            h_sb = p3sb("h_sb", [128, 2 * TC], BF16)
            w_sb = p3sb("w_sb", [128, 2 * TC], BF16)
            xnf_sb = p3sb("xnf_sb", [128, NDC * 1024], BF16)
            wg2_sb = p3sb("wg2_sb", [128, 2 * NDC * 256], BF16)
            wu2_sb = p3sb("wu2_sb", [128, 2 * NDC * 256], BF16)
            th_sb = p3sb("th_sb", [128, 2 * 1024], BF16)
            gh_sb = p3sb("gh_sb", [128, 2 * 1024], BF16)
            ups_sb = p3sb("ups_sb", [128, 2 * 1024], BF16)
            th1_sb = p3sb("th1_sb", [128, 2 * 1024], BF16)
            t3_sb = p3sb("t3_sb", [128, 2 * 1024], BF16)
            actt_sb = p3sb("actt_sb", [128, 2 * 1024], BF16)

            q_tc = ring("q_tc")
            q_bc = [ring("q_bc0"), ring("q_bc1")]
            q_park = ring("q_park")
            q_xnf = ring("q_xnf")
            q_wp = [ring("q_wp0"), ring("q_wp1")]
            q_ast = [ring("q_ast0"), ring("q_ast1")]

            scan_ms, ybar_ms, w_ms, abar_ms = {}, {}, {}, {}
            bxlast_u, wlast_u, park_tc, tcload_ms = {}, {}, {}, {}
            fw_ms, fpe_ms, fact_ps, fact_th1, fpool_ms, fst_ms = {}, {}, {}, {}, {}, {}
            xnf_ms, x1_ms = {}, {}

            def emit_tc_loads(tc):
                @block.sync
                def _(e, tc=tc):
                    cols = slice(tc * TC, (tc + 1) * TC)
                    if tc >= 1:
                        kp = ((tc - 1) * 16 + 15) * ODC + ODC - 1
                        e.wait_ge(act.h, abar_ms[kp])
                        e.wait_ge(dve.h, bxlast_u[(tc - 1) * 16 + 15])
                        e.wait_ge(q_park.h, park_tc[(tc - 1, ODC - 1)])
                    for oc in range(ODC):
                        rows = slice(oc * 128, (oc + 1) * 128)
                        dst = slice(oc * TC, (oc + 1) * TC)
                        e.dma_start(out=dt_tc[:, dst],
                                    in_=dt_dram[rows, cols]).then_inc(q_tc.h, 16)
                        e.dma_start(out=u_tc[:, dst],
                                    in_=u_dram[rows, cols]).then_inc(q_tc.h, 16)
                        e.dma_start(out=y_tc[:, dst],
                                    in_=dxn_dram[rows, cols]).then_inc(q_tc.h, 16)
                tcload_ms[tc] = q_tc.inc(16 * 3 * ODC)

            def emit_mixer_unit(m):
                tc, n = divmod(m, 16)
                if n == 0:
                    emit_tc_loads(tc)
                bsl = m % 2
                bdst = slice(bsl * TC, (bsl + 1) * TC)

                @block.sync
                def _(e, m=m, tc=tc, n=n, bdst=bdst, bsl=bsl):
                    cols = slice(tc * TC, (tc + 1) * TC)
                    e.wait_ge(q_bcd.h, 16)
                    if m >= 2:
                        e.wait_ge(dve.h, bxlast_u[m - 2])
                        e.wait_ge(pool.h, wlast_u[m - 2])
                    e.dma_start(out=bbc_sb[:, bdst],
                                in_=bc_dram[n:n + 1, cols]
                                .partition_broadcast(128)).then_inc(q_bc[bsl].h, 16)
                    e.dma_start(out=cbc_sb[:, bdst],
                                in_=bc_dram[16 + n:17 + n, cols]
                                .partition_broadcast(128)).then_inc(q_bc[bsl].h, 16)
                bc_tgt = q_bc[bsl].inc(32)

                for oc in range(ODC):
                    k = m * ODC + oc
                    sl = k % 2
                    col = oc * 16 + n
                    tsl = slice(sl * TC, (sl + 1) * TC)
                    osl = slice(oc * TC, (oc + 1) * TC)

                    @block.scalar
                    def _(e, k=k, tc=tc, col=col, tsl=tsl, osl=osl):
                        e.wait_ge(q_tc.h, tcload_ms[tc])
                        if k >= 2:
                            e.wait_ge(dve.h, scan_ms[k - 2])
                        e.activation(out=abar_sb[:, tsl], in_=dt_tc[:, osl], func=AF.Exp,
                                     scale=A_sb[:, col:col + 1]).then_inc(act.h, 1)
                    abar_ms[k] = act.inc()

                    @block.vector
                    def _(e, k=k, tc=tc, col=col, sl=sl, tsl=tsl, osl=osl,
                          bsl=bsl, bc_tgt=bc_tgt):
                        e.wait_ge(q_bc[bsl].h, bc_tgt)
                        e.wait_ge(q_tc.h, tcload_ms[tc])
                        e.tensor_mul(out=bx_sb[:, tsl], in0=u_tc[:, osl],
                                     in1=bbc_sb[:, bsl * TC:(bsl + 1) * TC])
                        e.wait_ge(act.h, abar_ms[k])
                        if k >= 2:
                            e.wait_ge(pool.h, w_ms[k - 2])
                        init = 0.0 if tc == 0 else state_sb[:, col:col + 1]
                        e.tensor_tensor_scan(out=h_sb[:, tsl], data0=abar_sb[:, tsl],
                                             data1=bx_sb[:, tsl], initial=init,
                                             op0=OP.mult, op1=OP.add).then_inc(dve.h, 1)
                    scan_ms[k] = dve.inc()
                    if oc == ODC - 1:
                        bxlast_u[m] = scan_ms[k]

                    @block.gpsimd
                    def _(e, k=k, tsl=tsl, bsl=bsl):
                        e.wait_ge(dve.h, scan_ms[k])
                        if k >= 2:
                            e.wait_ge(dve.h, ybar_ms[k - 2])
                        e.tensor_mul(out=w_sb[:, tsl], in0=h_sb[:, tsl],
                                     in1=cbc_sb[:, bsl * TC:(bsl + 1) * TC]
                                     ).then_inc(pool.h, 1)
                    w_ms[k] = pool.inc()
                    if oc == ODC - 1:
                        wlast_u[m] = w_ms[k]

                    @block.vector
                    def _(e, k=k, sl=sl, col=col, tsl=tsl, osl=osl):
                        e.wait_ge(pool.h, w_ms[k])
                        e.tensor_add(out=y_tc[:, osl], in0=y_tc[:, osl],
                                     in1=w_sb[:, tsl])
                        e.tensor_copy(out=state_sb[:, col:col + 1],
                                      in_=h_sb[:, sl * TC + TC - 1:sl * TC + TC]
                                      ).then_inc(dve.h, 1)
                    ybar_ms[k] = dve.inc()

                if n == 15:
                    @block.vector
                    def _(e):
                        e.tensor_copy(out=bx_sb[:, 0:TC],
                                      in_=bx_sb[:, 0:TC]).then_inc(dve.h, 1)
                    park_gate = dve.inc()

                    for oc in range(ODC):
                        @block.sync
                        def _(e, tc=tc, oc=oc, park_gate=park_gate):
                            e.wait_ge(dve.h, park_gate)
                            e.dma_start(out=cc_in[oc * 128:(oc + 1) * 128,
                                                  tc * TC:(tc + 1) * TC],
                                        in_=y_tc[:, oc * TC:(oc + 1) * TC]
                                        ).then_inc(q_park.h, 16)
                        park_tc[(tc, oc)] = q_park.inc(16)

            def emit_ffn_pair_load(hf, qp):
                sl = qp % 2

                @block.sync
                def _(e, hf=hf, qp=qp, sl=sl):
                    if qp >= 2:
                        e.wait_ge(pe.h, fpe_ms[(hf, 2 * qp - 3)])
                    elif hf == 1:
                        e.wait_ge(pe.h, fpe_ms[(0, HR - 1)])
                    srcg = wg_bf.rearrange("(c p) h -> p c h", p=128)
                    srcu = wu_bf.rearrange("(c p) h -> p c h", p=128)
                    e.dma_start(out=wg2_sb[:, sl * NDC * 256:(sl + 1) * NDC * 256]
                                .rearrange("p (c h) -> p c h", c=NDC),
                                in_=srcg[:, :, qp * 256:(qp + 1) * 256]).then_inc(q_wp[sl].h, 16)
                    e.dma_start(out=wu2_sb[:, sl * NDC * 256:(sl + 1) * NDC * 256]
                                .rearrange("p (c h) -> p c h", c=NDC),
                                in_=srcu[:, :, qp * 256:(qp + 1) * 256]).then_inc(q_wp[sl].h, 16)
                fw_ms[(hf, qp)] = q_wp[sl].inc(32)

            def emit_ffn_half_prologue(hf):
                @block.sync
                def _(e, hf=hf):
                    if hf == 1:
                        e.wait_ge(pe.h, fpe_ms[(0, HR - 1)])
                    for dc in range(NDC):
                        e.dma_start(out=xnf_sb[:, dc * 1024:(dc + 1) * 1024],
                                    in_=xnown_dram[hf * 1024:(hf + 1) * 1024,
                                                   dc * 128:(dc + 1) * 128],
                                    transpose=True).then_inc(q_xnf.h, 16)
                xnf_ms[hf] = q_xnf.inc(16 * NDC)

                for tch in range(2):
                    @block.tensor
                    def _(e, hf=hf, tch=tch):
                        if tch == 0:
                            e.wait_ge(q_xnf.h, xnf_ms[hf])
                            if hf == 1:
                                e.wait_ge(dve.h, x1_ms[0])
                                e.wait_ge(act.h, fact_ps[(0, HR - 2)])
                                e.wait_ge(act.h, fact_ps[(0, HR - 1)])
                        else:
                            e.wait_ge(dve.h, dve.v)
                        for dc in range(NDC):
                            mm = e.matmul(out=ps[0:1, 0:512], lhsT=ones_bf[:],
                                          rhs=xnf_sb[:, dc * 1024 + tch * 512:
                                                     dc * 1024 + (tch + 1) * 512],
                                          start=(dc == 0), stop=(dc == NDC - 1))
                            if dc == NDC - 1:
                                mm.then_inc(pe.h, 1)
                    x1pe = pe.inc()

                    @block.vector
                    def _(e, hf=hf, tch=tch, x1pe=x1pe):
                        e.wait_ge(pe.h, x1pe)
                        dst = slice(hf * 1024 + tch * 512, hf * 1024 + (tch + 1) * 512)
                        e.tensor_copy(out=x1row_f[0:1, dst], in_=ps[0:1, 0:512])
                        e.tensor_copy(out=x1row_bf[0:1, dst], in_=x1row_f[0:1, dst])
                        e.tensor_copy(out=x1row_f[0:1, dst], in_=x1row_f[0:1, dst])
                        e.tensor_copy(out=x1row_f[0:1, dst],
                                      in_=x1row_f[0:1, dst]).then_inc(dve.h, 1)
                    x1_ms[hf] = dve.inc()

            def emit_ffn_unit(f):
                hf, hr = divmod(f, HR)
                if hr == 0:
                    emit_ffn_half_prologue(hf)
                    emit_ffn_pair_load(hf, 0)
                    emit_ffn_pair_load(hf, 1)
                if hr % 2 == 0 and hr + 2 < HR:
                    emit_ffn_pair_load(hf, hr // 2 + 1)
                qp = hr // 2
                wsl = qp % 2
                pb = (hr % 2) * 2048
                hh = (hr % 2) * 128
                pf = None
                if f >= 2:
                    pf = (hf, hr - 2) if hr >= 2 else (0, HR - 2 + hr)

                @block.tensor
                def _(e, f=f, hf=hf, hr=hr, qp=qp, wsl=wsl, pb=pb, hh=hh, pf=pf):
                    e.wait_ge(q_wp[wsl].h, fw_ms[(hf, qp)])
                    e.wait_ge(q_xnf.h, xnf_ms[hf])
                    e.wait_ge(dve.h, x1_ms[hf])
                    if pf is not None:
                        e.wait_ge(act.h, fact_ps[pf])
                    for mat, wsb in ((0, wg2_sb), (1, wu2_sb)):
                        ub = pb + mat * 1024
                        for dc in range(NDC):
                            lhs = wsb[:, wsl * NDC * 256 + dc * 256 + hh:
                                      wsl * NDC * 256 + dc * 256 + hh + 128]
                            for tch in range(2):
                                e.matmul(out=ps[:, ub + tch * 512:ub + (tch + 1) * 512],
                                         lhsT=lhs,
                                         rhs=xnf_sb[:, dc * 1024 + tch * 512:
                                                    dc * 1024 + (tch + 1) * 512],
                                         start=(dc == 0), stop=False)
                        for tch in range(2):
                            mm = e.matmul(out=ps[:, ub + tch * 512:ub + (tch + 1) * 512],
                                          lhsT=n128_bf[:],
                                          rhs=x1row_bf[0:1, hf * 1024 + tch * 512:
                                                       hf * 1024 + (tch + 1) * 512],
                                          start=False, stop=True)
                            if mat == 1 and tch == 1:
                                mm.then_inc(pe.h, 1)
                fpe_ms[(hf, hr)] = pe.inc()

                @block.scalar
                def _(e, f=f, hf=hf, hr=hr, pb=pb, pf=pf):
                    e.wait_ge(pe.h, fpe_ms[(hf, hr)])
                    if pf is not None:
                        e.wait_ge(pool.h, fpool_ms[pf])
                    sl = f % 2
                    dst = slice(sl * 1024, (sl + 1) * 1024)
                    sgc = sgh_sb[:, hr:hr + 1]
                    suc = su_sb[:, hr:hr + 1]
                    e.activation(out=th_sb[:, dst], in_=ps[:, pb:pb + 1024],
                                 func=AF.Tanh, scale=sgc)
                    e.activation(out=gh_sb[:, dst], in_=ps[:, pb:pb + 1024],
                                 func=AF.Identity, scale=sgc)
                    e.activation(out=ups_sb[:, dst], in_=ps[:, pb + 1024:pb + 2048],
                                 func=AF.Identity, scale=suc).then_inc(act.h, 1)
                    e.activation(out=th1_sb[:, dst], in_=th_sb[:, dst],
                                 func=AF.Identity, bias=1.0).then_inc(act.h, 1)
                fact_ps[(hf, hr)] = act.inc()
                fact_th1[(hf, hr)] = act.inc()

                @block.gpsimd
                def _(e, f=f, hf=hf, hr=hr, pf=pf):
                    e.wait_ge(act.h, fact_th1[(hf, hr)])
                    if pf is not None:
                        e.wait_ge(q_ast[f % 2].h, fst_ms[pf])
                    sl = f % 2
                    dst = slice(sl * 1024, (sl + 1) * 1024)
                    e.tensor_mul(out=t3_sb[:, dst], in0=gh_sb[:, dst], in1=ups_sb[:, dst])
                    e.tensor_mul(out=actt_sb[:, dst], in0=t3_sb[:, dst],
                                 in1=th1_sb[:, dst]).then_inc(pool.h, 1)
                fpool_ms[(hf, hr)] = pool.inc()

                @block.sync
                def _(e, f=f, hf=hf, hr=hr):
                    e.wait_ge(pool.h, fpool_ms[(hf, hr)])
                    sl = f % 2
                    e.dma_start(out=act_dram[hr * 128:(hr + 1) * 128,
                                             hf * 1024:(hf + 1) * 1024],
                                in_=actt_sb[:, sl * 1024:(sl + 1) * 1024]
                                ).then_inc(q_ast[sl].h, 16)
                fst_ms[(hf, hr)] = q_ast[f % 2].inc(16)

            fidx = 0
            for m in range(NTC * 16):
                emit_mixer_unit(m)
                tgt = (m + 1) * (2 * HR) // (NTC * 16)
                while fidx < tgt:
                    emit_ffn_unit(fidx)
                    fidx += 1
            while fidx < 2 * HR:
                emit_ffn_unit(fidx)
                fidx += 1

            @block.gpsimd
            def _(e):
                e.wait_ge(q_park.h, q_park.v)
                e.collective_compute(
                    "AllGather", OP.bypass,
                    replica_groups=[[0, 1], [2, 3], [4, 5], [6, 7]],
                    ins=[cc_in.opt()], outs=[cc_out.opt()]).then_inc(ccs.h)
            ccs.inc()

        barrier()

        p4 = contextlib.ExitStack()
        with p4:
            def p4sb(nm, shp, dt):
                return p4.enter_context(nc.sbuf_tensor(nm, shp, dt))
            actq_sb = p4sb("actq_sb", [128, HR * 512], BF16)
            s_sb = p4sb("s_sb", [128, NDC * 512], BF16)
            wdt2_sb = p4sb("wdt2_sb", [128, 2 * 512], BF16)
            th2_sb = p4sb("th2_sb", [128, 8 * 512], BF16)
            gh2_sb = p4sb("gh2_sb", [128, 8 * 512], BF16)
            g2_sb = p4sb("g2_sb", [128, 8 * 512], BF16)
            sg2_sb = p4sb("sg2_sb", [128, 8 * 512], BF16)
            sgt_sb = p4sb("sgt_sb", [128, 2 * 2048], BF16)
            xo_sb = p4sb("xo_sb", [128, 2 * 2048], F32)
            ot_sb = p4sb("ot_sb", [128, 2 * 2048], F32)

            q_aq = ring("q_aq")
            q_ssb = ring("q_ssb")
            q_wd2 = [ring("q_wd20"), ring("q_wd21")]
            q_sg = [ring("q_sg0"), ring("q_sg1")]
            q_cmb = [ring("q_cmb0"), ring("q_cmb1")]
            q_out = [ring("q_out0"), ring("q_out1")]

            aq_ms, qpe_hr, qpe_ms, qact_ms, qdve_ms, qpool_ms, qst_ms = {}, {}, {}, {}, {}, {}, {}
            wd_ms, a1e_ms, out_ms, out_pool, s_ld_tq, cmb_ms = {}, {}, {}, {}, {}, {}

            for tq in range(4):
                @block.sync
                def _(e, tq=tq):
                    if tq >= 1:
                        e.wait_ge(pe.h, qpe_ms[(tq - 1, 3)])
                    for hr in range(HR):
                        e.dma_start(out=actq_sb[:, hr * 512:(hr + 1) * 512],
                                    in_=act_dram[hr * 128:(hr + 1) * 128,
                                                 tq * 512:(tq + 1) * 512]).then_inc(q_aq.h, 16)
                aq_ms[tq] = q_aq.inc(16 * HR)

                @block.sync
                def _(e, tq=tq):
                    e.wait_ge(ccs.h, 1)
                    if tq >= 1:
                        e.wait_ge(pool.h, qpool_ms[(tq - 1, 3)])
                    for arm in nc.switch_core_id(engines=[e], n=128):
                        pos = int(arm) % 2
                        off = pos * TOWN + tq * 512
                        for g in range(NDC):
                            if g < ODC:
                                srcap = cc_out[pos * 1024 + g * 128:
                                               pos * 1024 + (g + 1) * 128,
                                               off:off + 512]
                            else:
                                p_ = 1 - pos
                                srcap = cc_out[p_ * 1024 + (g - ODC) * 128:
                                               p_ * 1024 + (g - ODC + 1) * 128,
                                               off:off + 512]
                            e.dma_start(out=s_sb[:, g * 512:(g + 1) * 512],
                                        in_=srcap).then_inc(q_ssb.h, 16)
                s_ld_tq[tq] = q_ssb.inc(16 * NDC)

                for g4 in range(4):
                    qi = tq * 4 + g4
                    base = (g4 % 2) * 2048
                    for hr in range(HR):
                        sl = hr % 2

                        @block.sync
                        def _(e, tq=tq, g4=g4, hr=hr, sl=sl):
                            if hr >= 2:
                                e.wait_ge(pe.h, qpe_hr[(tq, g4, hr - 2)])
                            elif g4 >= 1:
                                e.wait_ge(pe.h, qpe_hr[(tq, g4 - 1, HR - 2 + hr)])
                            elif tq >= 1:
                                e.wait_ge(pe.h, qpe_hr[(tq - 1, 3, HR - 2 + hr)])
                            e.dma_start(out=wdt2_sb[:, sl * 512:(sl + 1) * 512],
                                        in_=wd_bf[g4 * 512:(g4 + 1) * 512,
                                                  hr * 128:(hr + 1) * 128],
                                        transpose=True).then_inc(q_wd2[sl].h, 16)
                        wd_ms[(tq, g4, hr)] = q_wd2[sl].inc(16)

                        @block.tensor
                        def _(e, tq=tq, g4=g4, hr=hr, sl=sl, base=base):
                            e.wait_ge(q_wd2[sl].h, wd_ms[(tq, g4, hr)])
                            if hr == 0:
                                e.wait_ge(q_aq.h, aq_ms[tq])
                                qi2 = tq * 4 + g4
                                if qi2 >= 2:
                                    pq = divmod(qi2 - 2, 4)
                                    e.wait_ge(act.h, qact_ms[(pq[0], pq[1])])
                                if g4 == 1:
                                    e.wait_ge(dve.h, a1e_ms[tq])
                            rhs = actq_sb[:, hr * 512:(hr + 1) * 512]
                            for j in range(4):
                                mm = e.matmul(out=ps[:, base + j * 512:base + (j + 1) * 512],
                                              lhsT=wdt2_sb[:, sl * 512 + j * 128:
                                                           sl * 512 + (j + 1) * 128],
                                              rhs=rhs, start=(hr == 0), stop=False)
                                if j == 3 and g4 != 0:
                                    mm.then_inc(pe.h, 1)
                            if g4 == 0:
                                mm = e.matmul(out=ps[0:1, 2048:2560], lhsT=ones_bf[:],
                                              rhs=rhs, start=(hr == 0), stop=(hr == HR - 1))
                                mm.then_inc(pe.h, 1)
                        qpe_hr[(tq, g4, hr)] = pe.inc()

                    if g4 == 0:
                        @block.vector
                        def _(e, tq=tq):
                            e.wait_ge(pe.h, qpe_hr[(tq, 0, HR - 1)])
                            e.tensor_copy(out=a1row_f[:], in_=ps[0:1, 2048:2560])
                            e.tensor_copy(out=a1row_bf[:], in_=a1row_f[:])
                            e.tensor_copy(out=a1row_f[:], in_=a1row_f[:])
                            e.tensor_copy(out=a1row_f[:], in_=a1row_f[:])
                            e.tensor_copy(out=a1row_f[:],
                                          in_=a1row_f[:]).then_inc(dve.h, 1)
                        a1e_ms[tq] = dve.inc()

                    @block.tensor
                    def _(e, tq=tq, g4=g4, base=base):
                        if g4 == 0:
                            e.wait_ge(dve.h, a1e_ms[tq])
                        for j in range(4):
                            mm = e.matmul(out=ps[:, base + j * 512:base + (j + 1) * 512],
                                          lhsT=n128_bf[:], rhs=a1row_bf[:],
                                          start=False, stop=True)
                            if j == 3:
                                mm.then_inc(pe.h, 1)
                    qpe_ms[(tq, g4)] = pe.inc()

                    @block.scalar
                    def _(e, tq=tq, g4=g4, base=base, qi=qi):
                        e.wait_ge(pe.h, qpe_ms[(tq, g4)])
                        if qi >= 2:
                            pq = divmod(qi - 2, 4)
                            e.wait_ge(dve.h, qdve_ms[(pq[0], pq[1])])
                        for j in range(4):
                            gl = g4 * 4 + j
                            sl = (qi % 2) * 4 + j
                            dst = slice(sl * 512, (sl + 1) * 512)
                            psv = ps[:, base + j * 512:base + (j + 1) * 512]
                            sdc = sdh_sb[:, gl:gl + 1]
                            e.activation(out=th2_sb[:, dst], in_=psv, func=AF.Tanh,
                                         scale=sdc)
                            la = e.activation(out=gh2_sb[:, dst], in_=psv,
                                              func=AF.Identity, scale=sdc)
                            if j == 3:
                                la.then_inc(act.h, 1)
                    qact_ms[(tq, g4)] = act.inc()

                    @block.vector
                    def _(e, tq=tq, g4=g4, qi=qi):
                        e.wait_ge(act.h, qact_ms[(tq, g4)])
                        if qi >= 2:
                            pq = divmod(qi - 2, 4)
                            e.wait_ge(pool.h, qpool_ms[(pq[0], pq[1])])
                        for j in range(4):
                            sl = (qi % 2) * 4 + j
                            dst = slice(sl * 512, (sl + 1) * 512)
                            la = e.scalar_tensor_tensor(out=g2_sb[:, dst],
                                                        in0=th2_sb[:, dst], scalar=1.0,
                                                        in1=gh2_sb[:, dst],
                                                        op0=OP.add, op1=OP.mult)
                            if j == 3:
                                la.then_inc(dve.h, 1)
                    qdve_ms[(tq, g4)] = dve.inc()

                    @block.gpsimd
                    def _(e, tq=tq, g4=g4, qi=qi):
                        e.wait_ge(dve.h, qdve_ms[(tq, g4)])
                        e.wait_ge(q_ssb.h, s_ld_tq[tq])
                        if qi >= 2:
                            pq = divmod(qi - 2, 4)
                            e.wait_ge(q_sg[qi % 2].h, qst_ms[(pq[0], pq[1])])
                        for j in range(4):
                            gl = g4 * 4 + j
                            sl = (qi % 2) * 4 + j
                            dst = slice(sl * 512, (sl + 1) * 512)
                            la = e.tensor_mul(out=sg2_sb[:, dst], in0=g2_sb[:, dst],
                                              in1=s_sb[:, gl * 512:(gl + 1) * 512])
                            if j == 3:
                                la.then_inc(pool.h, 1)
                    qpool_ms[(tq, g4)] = pool.inc()

                    @block.sync
                    def _(e, tq=tq, g4=g4, qi=qi):
                        e.wait_ge(pool.h, qpool_ms[(tq, g4)])
                        for j in range(4):
                            gl = g4 * 4 + j
                            sl = (qi % 2) * 4 + j
                            e.dma_start(out=sg_dram[gl * 128:(gl + 1) * 128,
                                                    tq * 512:(tq + 1) * 512],
                                        in_=sg2_sb[:, sl * 512:(sl + 1) * 512]
                                        ).then_inc(q_sg[qi % 2].h, 16)
                    qst_ms[(tq, g4)] = q_sg[qi % 2].inc(16 * 4)

                for tt in range(4):
                    ti = tq * 4 + tt
                    row0 = ti * 128
                    csl = ti % 2

                    @block.sync
                    def _(e, tq=tq, ti=ti, row0=row0, csl=csl):
                        e.wait_ge(q_sg[0].h, q_sg[0].v)
                        e.wait_ge(q_sg[1].h, q_sg[1].v)
                        if ti >= 2:
                            e.wait_ge(pool.h, out_pool[ti - 2])
                        e.dma_start(out=sgt_sb[:, csl * 2048:(csl + 1) * 2048],
                                    in_=sg_dram[0:D, ti * 128:(ti + 1) * 128],
                                    transpose=True).then_inc(q_cmb[csl].h, 16)
                        e.dma_start(out=xo_sb[:, csl * 2048:(csl + 1) * 2048],
                                    in_=x_own[row0:row0 + 128, :]).then_inc(q_cmb[csl].h, 16)
                    cmb_ms[ti] = q_cmb[csl].inc(32)

                    @block.gpsimd
                    def _(e, ti=ti, csl=csl):
                        e.wait_ge(q_cmb[csl].h, cmb_ms[ti])
                        if ti >= 2:
                            e.wait_ge(q_out[csl].h, out_ms[ti - 2])
                        a = slice(csl * 2048, (csl + 1) * 2048)
                        e.tensor_add(out=ot_sb[:, a], in0=xo_sb[:, a],
                                     in1=sgt_sb[:, a]).then_inc(pool.h, 1)
                    out_pool[ti] = pool.inc()

                    @block.sync
                    def _(e, ti=ti, row0=row0, csl=csl):
                        e.wait_ge(pool.h, out_pool[ti])
                        e.dma_start(out=out_own[row0:row0 + 128, :],
                                    in_=ot_sb[:, csl * 2048:(csl + 1) * 2048]
                                    ).then_inc(q_out[csl].h, 16)
                    out_ms[ti] = q_out[csl].inc(16)

            if debug:
                barrier()

                @block.gpsimd
                def _(e):
                    e.dma_start(out=dbg["dbg_xn"], in_=xn_dram).then_inc(gq.h, 16)
                    e.dma_start(out=dbg["dbg_bc"], in_=bc_dram).then_inc(gq.h, 16)
                    e.dma_start(out=dbg["dbg_dt"], in_=dt_dram).then_inc(gq.h, 16)
                    e.dma_start(out=dbg["dbg_u"], in_=u_dram).then_inc(gq.h, 16)
                    e.dma_start(out=dbg["dbg_s"], in_=cc_in).then_inc(gq.h, 16)
                    e.dma_start(out=dbg["dbg_act"], in_=act_dram).then_inc(gq.h, 16)
                    e.dma_start(out=dbg["dbg_st"], in_=st_dram).then_inc(gq.h, 16)
                gq.inc(16 * 7)

            @block.sync
            def _(e):
                e.wait_ge(q_out[0].h, q_out[0].v)
                e.wait_ge(q_out[1].h, q_out[1].v)

            @block.gpsimd
            def _(e):
                e.wait_ge(gq.h, gq.v)

    return nc


_NC_CACHE = {}


def _get_nc(debug=False):
    key = bool(debug)
    if key not in _NC_CACHE:
        _NC_CACHE[key] = build(debug=key)
    return _NC_CACHE[key]


def _perm(dh):
    return np.arange(D) if dh == 0 else np.concatenate(
        [np.arange(1024, 2048), np.arange(0, 1024)])


def make_inputs(L0_attn_x, L0_attn_x_proj_w, L0_attn_dt_proj_w, L0_attn_dt_proj_b,
                L0_attn_A_log, L0_attn_D_param, L0_ffn_wq_gate, L0_ffn_s_gate,
                L0_ffn_wq_up, L0_ffn_s_up, L0_ffn_wq_down, L0_ffn_s_down):
    x = np.ascontiguousarray(np.asarray(L0_attn_x, dtype=np.float32))
    xp = np.asarray(L0_attn_x_proj_w, dtype=np.float32)
    wdt = np.asarray(L0_attn_dt_proj_w, dtype=np.float32)
    dtb = np.asarray(L0_attn_dt_proj_b, dtype=np.float32)
    alog = np.asarray(L0_attn_A_log, dtype=np.float32)
    dp = np.asarray(L0_attn_D_param, dtype=np.float32)
    wqg = np.asarray(L0_ffn_wq_gate, dtype=np.int32)
    wqu = np.asarray(L0_ffn_wq_up, dtype=np.int32)
    wqd = np.asarray(L0_ffn_wq_down, dtype=np.int32)
    sgv = np.asarray(L0_ffn_s_gate, dtype=np.float32)
    suv = np.asarray(L0_ffn_s_up, dtype=np.float32)
    sdv = np.asarray(L0_ffn_s_down, dtype=np.float32)

    per_dh = {}
    for dh in range(2):
        perm = _perm(dh)
        own = perm[:1024]
        per_dh[dh] = dict(
            wgT=np.ascontiguousarray(wqg.T[perm, :]),
            wuT=np.ascontiguousarray(wqu.T[perm, :]),
            wdn=np.ascontiguousarray(wqd[perm, :]),
            xprojT=np.ascontiguousarray(xp.T[perm, :]),
            wdtT_own=np.ascontiguousarray(wdt.T[:, own]),
            dtb_own=np.ascontiguousarray(dtb[own].reshape(ODC, 128).T),
            Alog_own=np.ascontiguousarray(
                alog[own].reshape(ODC, 128, N).transpose(1, 0, 2).reshape(128, ODC * N)),
            Dp_own=np.ascontiguousarray(dp[own].reshape(ODC, 128).T),
            sd_in=np.ascontiguousarray(sdv[perm].reshape(NDC, 128).T),
        )
    sg_t = np.ascontiguousarray(sgv.reshape(HR, 128).T)
    su_t = np.ascontiguousarray(suv.reshape(HR, 128).T)
    cst_arr = np.broadcast_to(
        np.array([EPS, -0.5, -1.0, 1.0], np.float32), (128, 4)).copy()
    ones_arr = np.ones((128, 1), ml_dtypes.bfloat16)
    n128_arr = np.full((1, 128), -128.0, ml_dtypes.bfloat16)

    in_maps = []
    for c in range(8):
        seq, dh = c // 2, c % 2
        perm = _perm(dh)
        xs = np.ascontiguousarray(x[seq][:, perm])
        m = dict(per_dh[dh])
        m.update(x_seq=xs,
                 x_own=np.ascontiguousarray(xs[dh * TOWN:(dh + 1) * TOWN]),
                 sg_in=sg_t, su_in=su_t,
                 cst_in=cst_arr, ones_in=ones_arr, n128_in=n128_arr)
        in_maps.append(m)
    return in_maps


def run(inputs, debug=False, **kw):
    in_maps = make_inputs(**inputs)
    nc = _get_nc(debug=debug)
    if kw.get("trace") and "trace_cores" not in kw:
        kw["trace_cores"] = [0]
    return run_bass_kernel_spmd(nc, in_maps, core_ids=list(range(8)), **kw)


def kernel(**inputs):
    res = run(inputs, debug=False)
    out = np.zeros((4, SEQ, D), dtype=np.float32)
    for c in range(8):
        seq, dh = c // 2, c % 2
        inv = np.argsort(_perm(dh))
        r = np.asarray(res.results[c]["out_own"])
        out[seq, dh * TOWN:(dh + 1) * TOWN, :] = r[:, inv]
    return out


# version 04
# speedup vs baseline: 1.1239x; 1.1115x over previous
import numpy as np
import ml_dtypes
import contextlib
import concourse.bass as bass
import concourse.mybir as mybir
from concourse.bass_utils import run_bass_kernel_spmd

F32 = mybir.dt.float32
BF16 = mybir.dt.bfloat16
I32 = mybir.dt.int32
AF = mybir.ActivationFunctionType
OP = mybir.AluOpType
AX = mybir.AxisListType

SEQ, TOWN, D, N, R, HID = 4096, 2048, 2048, 16, 128, 5632
NDC, ODC, HR = 16, 8, 44
TC, NTC = 1024, 4
EPS = 1e-5


def load_act_set(nc, e, name):
    from concourse.hw_specs import get_activation_tables
    idx = list(get_activation_tables(nc.m.arch).keys()).index(name)
    e.add_instruction(mybir.InstLoadActFuncSet(
        name=nc.get_next_instruction_name(), act_func_set_id=idx, ins=[], outs=[]))


class Sem:
    def __init__(self, h):
        self.h = h
        self.v = 0

    def inc(self, k=1):
        self.v += k
        return self.v


def build(debug=False):
    nc = bass.Bass("TRN2", target_bir_lowering=False)

    x_seq = nc.dram_tensor("x_seq", [SEQ, D], F32, kind="ExternalInput").ap()
    x_own = nc.dram_tensor("x_own", [TOWN, D], F32, kind="ExternalInput").ap()
    xprojT = nc.dram_tensor("xprojT", [D, 160], F32, kind="ExternalInput").ap()
    wdtT_own = nc.dram_tensor("wdtT_own", [R, ODC * 128], F32, kind="ExternalInput").ap()
    dtb_own = nc.dram_tensor("dtb_own", [128, ODC], F32, kind="ExternalInput").ap()
    Alog_own = nc.dram_tensor("Alog_own", [128, ODC * N], F32, kind="ExternalInput").ap()
    Dp_own = nc.dram_tensor("Dp_own", [128, ODC], F32, kind="ExternalInput").ap()
    wgT = nc.dram_tensor("wgT", [D, HID], I32, kind="ExternalInput").ap()
    wuT = nc.dram_tensor("wuT", [D, HID], I32, kind="ExternalInput").ap()
    wdn = nc.dram_tensor("wdn", [D, HID], I32, kind="ExternalInput").ap()
    sg_in = nc.dram_tensor("sg_in", [128, HR], F32, kind="ExternalInput").ap()
    su_in = nc.dram_tensor("su_in", [128, HR], F32, kind="ExternalInput").ap()
    sd_in = nc.dram_tensor("sd_in", [128, NDC], F32, kind="ExternalInput").ap()
    cst_in = nc.dram_tensor("cst_in", [128, 4], F32, kind="ExternalInput").ap()
    ones_in = nc.dram_tensor("ones_in", [128, 1], BF16, kind="ExternalInput").ap()
    n128_in = nc.dram_tensor("n128_in", [1, 128], BF16, kind="ExternalInput").ap()
    out_own = nc.dram_tensor("out_own", [TOWN, D], F32, kind="ExternalOutput").ap()

    xn_dram = nc.dram_tensor("xn_dram", [SEQ, D], BF16, kind="Internal").ap()
    xnown_dram = nc.dram_tensor("xnown_dram", [TOWN, D], BF16, kind="Internal").ap()
    bc_dram = nc.dram_tensor("bc_dram", [32, SEQ], BF16, kind="Internal").ap()
    dt_dram = nc.dram_tensor("dt_dram", [ODC * 128, SEQ], BF16, kind="Internal").ap()
    u_dram = nc.dram_tensor("u_dram", [ODC * 128, SEQ], BF16, kind="Internal").ap()
    dxn_dram = nc.dram_tensor("dxn_dram", [ODC * 128, SEQ], BF16, kind="Internal").ap()
    wg_bf = nc.dram_tensor("wg_bf", [D, HID], BF16, kind="Internal").ap()
    wu_bf = nc.dram_tensor("wu_bf", [D, HID], BF16, kind="Internal").ap()
    wd_bf = nc.dram_tensor("wd_bf", [D, HID], BF16, kind="Internal").ap()
    act_dram = nc.dram_tensor("act_dram", [HID, TOWN], BF16, kind="Internal").ap()
    cc_in = nc.dram_tensor("cc_in", [ODC * 128, SEQ], BF16, kind="Internal").ap()
    cc_out = nc.dram_tensor("cc_out", [2 * ODC * 128, SEQ], BF16, kind="Internal").ap()
    sg_dram = nc.dram_tensor("sg_dram", [D, TOWN], BF16, kind="Internal").ap()
    st_dram = nc.dram_tensor("st_dram", [128, 16 * 48], F32, kind="Internal").ap()
    dbg = {}
    if debug:
        for nm, shp in (("dbg_xn", [SEQ, D]), ("dbg_bc", [32, SEQ]),
                        ("dbg_dt", [ODC * 128, SEQ]), ("dbg_u", [ODC * 128, SEQ]),
                        ("dbg_s", [ODC * 128, SEQ]), ("dbg_act", [HID, TOWN]),
                        ("dbg_st", [128, 16 * 48])):
            dbg[nm] = nc.dram_tensor(nm, shp, F32, kind="ExternalOutput").ap()

    es = contextlib.ExitStack()

    def sb(name, shape, dt):
        return es.enter_context(nc.sbuf_tensor(name, shape, dt))

    with es, nc.Block() as block:
        xprojT_bf = sb("xprojT_bf", [128, NDC * 160], BF16)
        wdtT_bf = sb("wdtT_bf", [128, ODC * 128], BF16)
        dtb_sb = sb("dtb_sb", [128, ODC], F32)
        A_sb = sb("A_sb", [128, ODC * N], F32)
        Dp_sb = sb("Dp_sb", [128, ODC], F32)
        sgh_sb = sb("sgh_sb", [128, HR], F32)
        su_sb = sb("su_sb", [128, HR], F32)
        sdh_sb = sb("sdh_sb", [128, NDC], F32)
        ones_bf = sb("ones_bf", [128, 1], BF16)
        n128_bf = sb("n128_bf", [1, 128], BF16)
        dtlowT_bf = sb("dtlowT_bf", [128, SEQ], BF16)
        bcT_bf = sb("bcT_bf", [32, SEQ], BF16)
        x1row_f = sb("x1row_f", [1, TOWN], F32)
        x1row_bf = sb("x1row_bf", [1, TOWN], BF16)
        a1row_f = sb("a1row_f", [1, 512], F32)
        a1row_bf = sb("a1row_bf", [1, 512], BF16)
        state_sb = sb("state_sb", [128, 128], F32)
        const_sb = sb("const_sb", [128, 4], F32)
        nc.const_aps.register(F32, EPS, const_sb[:, 0:1])
        nc.const_aps.register(F32, -0.5, const_sb[:, 1:2])
        nc.const_aps.register(F32, -1.0, const_sb[:, 2:3])
        nc.const_aps.register(F32, 1.0, const_sb[:, 3:4])
        ps = es.enter_context(nc.psum_tensor("ps", [128, 4096], F32))

        pe = Sem(es.enter_context(nc.semaphore("pe")))
        act = Sem(es.enter_context(nc.semaphore("act")))
        dve = Sem(es.enter_context(nc.semaphore("dve")))
        pool = Sem(es.enter_context(nc.semaphore("pool")))
        ccs = Sem(es.enter_context(nc.semaphore("ccs")))
        bar = Sem(es.enter_context(nc.semaphore("bar")))
        rings = {}

        def ring(name):
            if name not in rings:
                rings[name] = Sem(es.enter_context(nc.semaphore(name)))
            return rings[name]

        gq = ring("gq")

        def barrier():
            tgt = bar.v + 5
            ring_waits = [(r.h, r.v) for r in rings.values() if r.v]

            @block.sync
            def _(e):
                for h, v in ring_waits:
                    e.wait_ge(h, v)
                e.sem_inc(bar.h, 1)
                e.wait_ge(bar.h, tgt)

            @block.tensor
            def _(e):
                e.wait_ge(pe.h, pe.v)
                e.sem_inc(bar.h, 1)
                e.wait_ge(bar.h, tgt)

            @block.vector
            def _(e):
                e.wait_ge(dve.h, dve.v)
                e.sem_inc(bar.h, 1)
                e.wait_ge(bar.h, tgt)

            @block.scalar
            def _(e):
                e.wait_ge(act.h, act.v)
                e.sem_inc(bar.h, 1)
                e.wait_ge(bar.h, tgt)

            @block.gpsimd
            def _(e):
                e.wait_ge(gq.h, gq.v)
                e.wait_ge(pool.h, pool.v)
                e.sem_inc(bar.h, 1)
                e.wait_ge(bar.h, tgt)
            bar.v = tgt

        @block.gpsimd
        def _(g):
            g.dma_start(out=xprojT_bf[:].rearrange("p (c f) -> p c f", c=NDC),
                        in_=xprojT.rearrange("(c p) f -> p c f", p=128)).then_inc(gq.h, 16)
            g.dma_start(out=wdtT_bf[:], in_=wdtT_own).then_inc(gq.h, 16)
            g.dma_start(out=wg_bf, in_=wgT).then_inc(gq.h, 16)
            g.dma_start(out=wu_bf, in_=wuT).then_inc(gq.h, 16)
            g.dma_start(out=wd_bf, in_=wdn).then_inc(gq.h, 16)
        gq.inc(16 * 5)

        q_par = ring("q_par")

        @block.sync
        def _(e):
            e.dma_start(out=dtb_sb[:], in_=dtb_own).then_inc(q_par.h, 16)
            e.dma_start(out=A_sb[:], in_=Alog_own).then_inc(q_par.h, 16)
            e.dma_start(out=Dp_sb[:], in_=Dp_own).then_inc(q_par.h, 16)
            e.dma_start(out=sgh_sb[:], in_=sg_in).then_inc(q_par.h, 16)
            e.dma_start(out=su_sb[:], in_=su_in).then_inc(q_par.h, 16)
            e.dma_start(out=sdh_sb[:], in_=sd_in).then_inc(q_par.h, 16)
            e.dma_start(out=const_sb[:], in_=cst_in).then_inc(q_par.h, 16)
            e.dma_start(out=ones_bf[:], in_=ones_in).then_inc(q_par.h, 16)
            e.dma_start(out=n128_bf[:], in_=n128_in).then_inc(q_par.h, 16)
        q_par.inc(16 * 9)

        @block.scalar
        def _(e):
            load_act_set(nc, e, "natural_log_exp_and_others")
            e.wait_ge(q_par.h, q_par.v)
            e.activation(out=A_sb[:], in_=A_sb[:], func=AF.Exp).then_inc(act.h, 1)
        a_exp = act.inc()

        @block.vector
        def _(e):
            e.wait_ge(act.h, a_exp)
            e.wait_ge(q_par.h, q_par.v)
            e.tensor_scalar_mul(out=A_sb[:], in0=A_sb[:], scalar1=-1.0)
            e.tensor_scalar_mul(out=sgh_sb[:], in0=sgh_sb[:], scalar1=0.5)
            e.tensor_scalar_mul(out=sdh_sb[:], in0=sdh_sb[:], scalar1=0.5).then_inc(dve.h, 1)
        dve.inc()

        p1 = contextlib.ExitStack()
        with p1:
            xt_sb = p1.enter_context(nc.sbuf_tensor("xt_sb", [128, 4 * D], F32))
            dump_sb = p1.enter_context(nc.sbuf_tensor("dump_sb", [128, 2 * D], F32))
            st_sb = p1.enter_context(nc.sbuf_tensor("st_sb", [128, 16 * 48], F32))
            xnt_sb = p1.enter_context(nc.sbuf_tensor("xnt_sb", [128, 4 * D], BF16))

            ln_tiles = [("seq", t) for t in range(SEQ // 128)] + \
                       [("own", t) for t in range(TOWN // 128)]
            NT = len(ln_tiles)
            SSQ, SSUM, M_, SS_, MSQ, VAR, LNV, RSTD, NEGM = [i * 48 for i in range(9)]
            sum_act, sum_dve, xn_ms, st_ms = {}, {}, {}, {}
            q_xt = [ring("q_xt%d" % i) for i in range(4)]
            q_xnt = [ring("q_xnt%d" % i) for i in range(4)]

            for k, (src_, t) in enumerate(ln_tiles):
                sl = k % 4
                dsl = k % 2
                xsrc = x_seq if src_ == "seq" else x_own

                @block.sync
                def _(e, k=k, sl=sl, t=t, xsrc=xsrc):
                    if k >= 4:
                        e.wait_ge(dve.h, sum_dve[k - 4])
                    e.dma_start(out=xt_sb[:, sl * D:(sl + 1) * D],
                                in_=xsrc[t * 128:(t + 1) * 128, :]).then_inc(q_xt[sl].h, 16)
                xt_tgt = q_xt[sl].inc(16)

                @block.scalar
                def _(e, k=k, sl=sl, dsl=dsl, xt_tgt=xt_tgt):
                    e.wait_ge(q_xt[sl].h, xt_tgt)
                    if k >= 2:
                        e.wait_ge(dve.h, sum_dve[k - 2])
                    e.activation(out=dump_sb[:, dsl * D:(dsl + 1) * D],
                                 in_=xt_sb[:, sl * D:(sl + 1) * D],
                                 func=AF.Square).then_inc(act.h, 1)
                sum_act[k] = act.inc()

                @block.vector
                def _(e, k=k, sl=sl, dsl=dsl, xt_tgt=xt_tgt):
                    e.wait_ge(q_xt[sl].h, xt_tgt)
                    e.wait_ge(act.h, sum_act[k])
                    e.reduce_sum(out=st_sb[:, SSQ + k:SSQ + k + 1],
                                 in_=dump_sb[:, dsl * D:(dsl + 1) * D], axis=AX.X)
                    e.reduce_sum(out=st_sb[:, SSUM + k:SSUM + k + 1],
                                 in_=xt_sb[:, sl * D:(sl + 1) * D],
                                 axis=AX.X).then_inc(dve.h, 1)
                sum_dve[k] = dve.inc()

            @block.vector
            def _(e):
                e.wait_ge(act.h, sum_act[NT - 1])
                e.wait_ge(dve.h, sum_dve[NT - 1])
                e.tensor_scalar_mul(out=st_sb[:, M_:M_ + NT],
                                    in0=st_sb[:, SSUM:SSUM + NT], scalar1=1.0 / D)
                e.tensor_scalar_mul(out=st_sb[:, SS_:SS_ + NT],
                                    in0=st_sb[:, SSQ:SSQ + NT], scalar1=1.0 / D)
                e.tensor_copy(out=dump_sb[:, 0:D], in_=dump_sb[:, 0:D])
                e.tensor_copy(out=dump_sb[:, 0:D], in_=dump_sb[:, 0:D])
                e.tensor_mul(out=st_sb[:, MSQ:MSQ + NT], in0=st_sb[:, M_:M_ + NT],
                             in1=st_sb[:, M_:M_ + NT])
                e.tensor_copy(out=dump_sb[:, 0:D], in_=dump_sb[:, 0:D])
                e.tensor_copy(out=dump_sb[:, 0:D], in_=dump_sb[:, 0:D])
                e.tensor_sub(out=st_sb[:, VAR:VAR + NT], in0=st_sb[:, SS_:SS_ + NT],
                             in1=st_sb[:, MSQ:MSQ + NT])
                e.tensor_copy(out=dump_sb[:, 0:D], in_=dump_sb[:, 0:D])
                e.tensor_copy(out=dump_sb[:, 0:D],
                              in_=dump_sb[:, 0:D]).then_inc(dve.h, 1)
            stats_dve = dve.inc()

            @block.scalar
            def _(e):
                e.wait_ge(dve.h, stats_dve)
                e.activation(out=st_sb[:, LNV:LNV + NT], in_=st_sb[:, VAR:VAR + NT],
                             func=AF.Ln, bias=EPS)
                e.activation(out=dump_sb[:, 0:D], in_=dump_sb[:, 0:D], func=AF.Identity)
                e.activation(out=dump_sb[:, 0:D], in_=dump_sb[:, 0:D], func=AF.Identity)
                e.activation(out=st_sb[:, RSTD:RSTD + NT], in_=st_sb[:, LNV:LNV + NT],
                             func=AF.Exp, scale=const_sb[:, 1:2])
                e.activation(out=dump_sb[:, 0:D], in_=dump_sb[:, 0:D], func=AF.Identity)
                e.activation(out=dump_sb[:, 0:D], in_=dump_sb[:, 0:D],
                             func=AF.Identity).then_inc(act.h, 1)
            stats_act = act.inc()

            @block.vector
            def _(e):
                e.wait_ge(act.h, stats_act)
                e.scalar_tensor_tensor(out=st_sb[:, NEGM:NEGM + NT],
                                       in0=st_sb[:, M_:M_ + NT], scalar=-1.0,
                                       in1=st_sb[:, RSTD:RSTD + NT],
                                       op0=OP.mult, op1=OP.mult)
                e.tensor_copy(out=dump_sb[:, 0:D], in_=dump_sb[:, 0:D])
                e.tensor_copy(out=dump_sb[:, 0:D],
                              in_=dump_sb[:, 0:D]).then_inc(dve.h, 1)
            negm_dve = dve.inc()

            def emit_xn_store(j):
                src_, t = ln_tiles[j]
                xdst = xn_dram if src_ == "seq" else xnown_dram
                gate = xn_ms[j + 1] if (j + 1) in xn_ms else xn_ms[j]
                qs = q_xnt[j % 4]

                @block.sync
                def _(e, j=j, t=t, xdst=xdst, gate=gate, qs=qs):
                    e.wait_ge(dve.h, gate)
                    e.dma_start(out=xdst[t * 128:(t + 1) * 128, :],
                                in_=xnt_sb[:, (j % 4) * D:(j % 4 + 1) * D]
                                ).then_inc(qs.h, 16)
                st_ms[j] = qs.inc(16)

            for k, (src_, t) in enumerate(ln_tiles):
                sl = k % 4
                xsrc = x_seq if src_ == "seq" else x_own

                @block.sync
                def _(e, k=k, sl=sl, t=t, xsrc=xsrc):
                    if k >= 4:
                        e.wait_ge(dve.h, xn_ms[k - 4])
                    else:
                        e.wait_ge(dve.h, sum_dve[NT - 4 + k])
                    e.dma_start(out=xt_sb[:, sl * D:(sl + 1) * D],
                                in_=xsrc[t * 128:(t + 1) * 128, :]).then_inc(q_xt[sl].h, 16)
                xt_tgt = q_xt[sl].inc(16)

                @block.vector
                def _(e, k=k, sl=sl, xt_tgt=xt_tgt):
                    e.wait_ge(q_xt[sl].h, xt_tgt)
                    e.wait_ge(dve.h, negm_dve)
                    if k >= 4:
                        e.wait_ge(q_xnt[k % 4].h, st_ms[k - 4])
                    e.tensor_scalar(out=xnt_sb[:, (k % 4) * D:(k % 4 + 1) * D],
                                    in0=xt_sb[:, sl * D:(sl + 1) * D],
                                    scalar1=st_sb[:, RSTD + k:RSTD + k + 1],
                                    scalar2=st_sb[:, NEGM + k:NEGM + k + 1],
                                    op0=OP.mult, op1=OP.add).then_inc(dve.h, 1)
                xn_ms[k] = dve.inc()
                if k >= 1:
                    emit_xn_store(k - 1)

            @block.vector
            def _(e):
                e.tensor_copy(out=dump_sb[:, 0:D],
                              in_=dump_sb[:, 0:D]).then_inc(dve.h, 1)
            xn_ms[NT] = dve.inc()
            emit_xn_store(NT - 1)

            q_std = ring("q_std")

            @block.sync
            def _(e):
                e.wait_ge(dve.h, xn_ms[NT])
                e.dma_start(out=st_dram, in_=st_sb[:]).then_inc(q_std.h, 16)
            q_std.inc(16)

        barrier()

        p2 = contextlib.ExitStack()
        with p2:
            xnT_sb = p2.enter_context(nc.sbuf_tensor("xnT_sb", [128, 4 * 2048], BF16))
            q_xnT = [ring("q_xnT%d" % i) for i in range(4)]
            q_bcd = ring("q_bcd")
            mm_ms, evac_ms = {}, {}
            for th2 in range(2):
                for dc in range(NDC):
                    j = th2 * NDC + dc
                    sl = j % 4

                    @block.sync
                    def _(e, j=j, sl=sl, th2=th2, dc=dc):
                        if j >= 4:
                            e.wait_ge(pe.h, mm_ms[j - 4])
                        e.dma_start(out=xnT_sb[:, sl * 2048:(sl + 1) * 2048],
                                    in_=xn_dram[th2 * 2048:(th2 + 1) * 2048,
                                                dc * 128:(dc + 1) * 128],
                                    transpose=True).then_inc(q_xnT[sl].h, 16)
                    xnT_tgt = q_xnT[sl].inc(16)

                    @block.tensor
                    def _(e, j=j, sl=sl, dc=dc, th2=th2, xnT_tgt=xnT_tgt):
                        e.wait_ge(q_xnT[sl].h, xnT_tgt)
                        if dc == 0 and th2 == 1:
                            e.wait_ge(dve.h, evac_ms[0])
                        for tch in range(4):
                            e.matmul(out=ps[:, tch * 512:(tch + 1) * 512],
                                     lhsT=xprojT_bf[:, dc * 160:dc * 160 + 128],
                                     rhs=xnT_sb[:, sl * 2048 + tch * 512:
                                                sl * 2048 + (tch + 1) * 512],
                                     start=(dc == 0), stop=(dc == NDC - 1))
                        for tch in range(4):
                            mm = e.matmul(out=ps[0:32, 2048 + tch * 512:2048 + (tch + 1) * 512],
                                          lhsT=xprojT_bf[:, dc * 160 + 128:dc * 160 + 160],
                                          rhs=xnT_sb[:, sl * 2048 + tch * 512:
                                                     sl * 2048 + (tch + 1) * 512],
                                          start=(dc == 0), stop=(dc == NDC - 1))
                            if tch == 3:
                                mm.then_inc(pe.h, 1)
                    mm_ms[j] = pe.inc()

                @block.vector
                def _(e, th2=th2):
                    e.wait_ge(pe.h, mm_ms[th2 * NDC + NDC - 1])
                    e.tensor_copy(out=dtlowT_bf[:, th2 * 2048:(th2 + 1) * 2048],
                                  in_=ps[:, 0:2048])
                    e.tensor_copy(out=bcT_bf[:, th2 * 2048:(th2 + 1) * 2048],
                                  in_=ps[0:32, 2048:4096])
                    e.tensor_copy(out=a1row_f[:], in_=a1row_f[:])
                    e.tensor_copy(out=a1row_f[:], in_=a1row_f[:])
                    e.tensor_copy(out=a1row_f[:], in_=a1row_f[:]).then_inc(dve.h, 1)
                evac_ms[th2] = dve.inc()

            @block.sync
            def _(e):
                e.wait_ge(dve.h, evac_ms[1])
                e.dma_start(out=bc_dram, in_=bcT_bf[:]).then_inc(q_bcd.h, 16)
            q_bcd.inc(16)

        barrier()

        p25 = contextlib.ExitStack()
        with p25:
            sp_a = p25.enter_context(nc.sbuf_tensor("sp_a", [128, 2048], F32))
            sp_b = p25.enter_context(nc.sbuf_tensor("sp_b", [128, 2048], F32))
            dtu_sb = p25.enter_context(nc.sbuf_tensor("dtu_sb", [128, 3 * 2048], BF16))
            xnm_sb = p25.enter_context(nc.sbuf_tensor("xnm_sb", [128, 2048], BF16))
            q_xnm = ring("q_xnm")
            q_dtu = ring("q_dtu")

            act_p25, dve_p25, park_ms = {}, {}, {}
            for th2 in range(2):
                for oc in range(ODC):
                    j = th2 * ODC + oc

                    @block.sync
                    def _(e, j=j, th2=th2, oc=oc):
                        if j >= 1:
                            e.wait_ge(dve.h, dve_p25[j - 1])
                        e.dma_start(out=xnm_sb[:],
                                    in_=xn_dram[th2 * 2048:(th2 + 1) * 2048,
                                                oc * 128:(oc + 1) * 128],
                                    transpose=True).then_inc(q_xnm.h, 16)
                    xnm_tgt = q_xnm.inc(16)

                    @block.tensor
                    def _(e, j=j, th2=th2, oc=oc):
                        if j >= 1:
                            e.wait_ge(act.h, act_p25[j - 1])
                        for i in range(4):
                            mm = e.matmul(out=ps[:, i * 512:(i + 1) * 512],
                                          lhsT=wdtT_bf[:, oc * 128:(oc + 1) * 128],
                                          rhs=dtlowT_bf[:, th2 * 2048 + i * 512:
                                                        th2 * 2048 + (i + 1) * 512],
                                          start=True, stop=True)
                            if i == 3:
                                mm.then_inc(pe.h, 1)
                    pe_dt = pe.inc()

                    @block.scalar
                    def _(e, j=j, oc=oc, pe_dt=pe_dt):
                        e.wait_ge(pe.h, pe_dt)
                        if j >= 1:
                            e.wait_ge(dve.h, dve_p25[j - 1])
                        bias = dtb_sb[:, oc:oc + 1]
                        e.activation(out=sp_a[:], in_=ps[:, 0:2048], func=AF.Abs, bias=bias)
                        e.activation(out=sp_b[:], in_=sp_a[:], func=AF.Exp,
                                     scale=const_sb[:, 2:3])
                        e.activation(out=sp_a[:], in_=sp_b[:], func=AF.Ln, bias=1.0)
                        e.activation(out=sp_b[:], in_=ps[:, 0:2048], func=AF.Relu,
                                     bias=bias).then_inc(act.h, 1)
                    act_p25[j] = act.inc()

                    @block.vector
                    def _(e, j=j, oc=oc, xnm_tgt=xnm_tgt):
                        e.wait_ge(act.h, act_p25[j])
                        if j >= 1:
                            e.wait_ge(q_dtu.h, park_ms[j - 1])
                        e.tensor_add(out=dtu_sb[:, 0:2048], in0=sp_a[:], in1=sp_b[:])
                        e.wait_ge(q_xnm.h, xnm_tgt)
                        e.tensor_mul(out=dtu_sb[:, 2048:4096],
                                     in0=dtu_sb[:, 0:2048], in1=xnm_sb[:])
                        e.tensor_scalar_mul(out=dtu_sb[:, 4096:6144], in0=xnm_sb[:],
                                            scalar1=Dp_sb[:, oc:oc + 1])
                        e.tensor_copy(out=sp_a[:], in_=sp_a[:]).then_inc(dve.h, 1)
                    dve_p25[j] = dve.inc()

                    @block.sync
                    def _(e, j=j, th2=th2, oc=oc):
                        e.wait_ge(dve.h, dve_p25[j])
                        rows = slice(oc * 128, (oc + 1) * 128)
                        cols = slice(th2 * 2048, (th2 + 1) * 2048)
                        e.dma_start(out=dt_dram[rows, cols],
                                    in_=dtu_sb[:, 0:2048]).then_inc(q_dtu.h, 16)
                        e.dma_start(out=u_dram[rows, cols],
                                    in_=dtu_sb[:, 2048:4096]).then_inc(q_dtu.h, 16)
                        e.dma_start(out=dxn_dram[rows, cols],
                                    in_=dtu_sb[:, 4096:6144]).then_inc(q_dtu.h, 16)
                    park_ms[j] = q_dtu.inc(48)

        barrier()

        @block.scalar
        def _(e):
            load_act_set(nc, e, "exp_and_others")

        p3 = contextlib.ExitStack()
        with p3:
            def p3sb(nm, shp, dt):
                return p3.enter_context(nc.sbuf_tensor(nm, shp, dt))
            dt_tc = p3sb("dt_tc", [128, ODC * TC], BF16)
            u_tc = p3sb("u_tc", [128, ODC * TC], BF16)
            y_tc = p3sb("y_tc", [128, ODC * TC], BF16)
            bbc_sb = p3sb("bbc_sb", [128, 2 * TC], BF16)
            cbc_sb = p3sb("cbc_sb", [128, 2 * TC], BF16)
            abar_sb = p3sb("abar_sb", [128, 2 * TC], F32)
            bx_sb = p3sb("bx_sb", [128, 2 * TC], BF16)
            h_sb = p3sb("h_sb", [128, 2 * TC], BF16)
            w_sb = p3sb("w_sb", [128, 2 * TC], BF16)
            xnf_sb = p3sb("xnf_sb", [128, NDC * 1024], BF16)
            wg2_sb = p3sb("wg2_sb", [128, 2 * NDC * 256], BF16)
            wu2_sb = p3sb("wu2_sb", [128, 2 * NDC * 256], BF16)
            th_sb = p3sb("th_sb", [128, 2 * 1024], BF16)
            gh_sb = p3sb("gh_sb", [128, 2 * 1024], BF16)
            ups_sb = p3sb("ups_sb", [128, 2 * 1024], BF16)
            th1_sb = p3sb("th1_sb", [128, 2 * 1024], BF16)
            t3_sb = p3sb("t3_sb", [128, 2 * 1024], BF16)
            actt_sb = p3sb("actt_sb", [128, 2 * 1024], BF16)

            q_tc = ring("q_tc")
            q_bc = [ring("q_bc0"), ring("q_bc1")]
            q_park = ring("q_park")
            q_xnf = ring("q_xnf")
            q_wp = [ring("q_wp0"), ring("q_wp1")]
            q_ast = [ring("q_ast0"), ring("q_ast1")]

            scan_ms, ybar_ms, w_ms, abar_ms = {}, {}, {}, {}
            bxlast_u, wlast_u, park_tc, tcload_ms = {}, {}, {}, {}
            fw_ms, fpe_ms, fact_ps, fact_th1, fpool_ms, fst_ms = {}, {}, {}, {}, {}, {}
            xnf_ms, x1_ms = {}, {}

            def emit_tc_loads(tc):
                @block.sync
                def _(e, tc=tc):
                    cols = slice(tc * TC, (tc + 1) * TC)
                    if tc >= 1:
                        kp = ((tc - 1) * 16 + 15) * ODC + ODC - 1
                        e.wait_ge(act.h, abar_ms[kp])
                        e.wait_ge(dve.h, bxlast_u[(tc - 1) * 16 + 15])
                        e.wait_ge(q_park.h, park_tc[(tc - 1, ODC - 1)])
                    for oc in range(ODC):
                        rows = slice(oc * 128, (oc + 1) * 128)
                        dst = slice(oc * TC, (oc + 1) * TC)
                        e.dma_start(out=dt_tc[:, dst],
                                    in_=dt_dram[rows, cols]).then_inc(q_tc.h, 16)
                        e.dma_start(out=u_tc[:, dst],
                                    in_=u_dram[rows, cols]).then_inc(q_tc.h, 16)
                        e.dma_start(out=y_tc[:, dst],
                                    in_=dxn_dram[rows, cols]).then_inc(q_tc.h, 16)
                tcload_ms[tc] = q_tc.inc(16 * 3 * ODC)

            pending_y = [None]

            def emit_y(k):
                oc = k % ODC
                m = k // ODC
                tc, n = divmod(m, 16)
                sl = k % 2
                col = oc * 16 + n
                tsl = slice(sl * TC, (sl + 1) * TC)
                osl = slice(oc * TC, (oc + 1) * TC)

                @block.vector
                def _(e, k=k, sl=sl, col=col, tsl=tsl, osl=osl):
                    e.wait_ge(pool.h, w_ms[k])
                    e.tensor_add(out=y_tc[:, osl], in0=y_tc[:, osl],
                                 in1=w_sb[:, tsl])
                    e.tensor_copy(out=state_sb[:, col:col + 1],
                                  in_=h_sb[:, sl * TC + TC - 1:sl * TC + TC]
                                  ).then_inc(dve.h, 1)
                ybar_ms[k] = dve.inc()

            def emit_mixer_unit(m):
                tc, n = divmod(m, 16)
                if n == 0:
                    emit_tc_loads(tc)
                bsl = m % 2
                bdst = slice(bsl * TC, (bsl + 1) * TC)

                @block.sync
                def _(e, m=m, tc=tc, n=n, bdst=bdst, bsl=bsl):
                    cols = slice(tc * TC, (tc + 1) * TC)
                    e.wait_ge(q_bcd.h, 16)
                    if m >= 2:
                        e.wait_ge(dve.h, bxlast_u[m - 2])
                        e.wait_ge(pool.h, wlast_u[m - 2])
                    e.dma_start(out=bbc_sb[:, bdst],
                                in_=bc_dram[n:n + 1, cols]
                                .partition_broadcast(128)).then_inc(q_bc[bsl].h, 16)
                    e.dma_start(out=cbc_sb[:, bdst],
                                in_=bc_dram[16 + n:17 + n, cols]
                                .partition_broadcast(128)).then_inc(q_bc[bsl].h, 16)
                bc_tgt = q_bc[bsl].inc(32)

                for oc in range(ODC):
                    k = m * ODC + oc
                    sl = k % 2
                    col = oc * 16 + n
                    tsl = slice(sl * TC, (sl + 1) * TC)
                    osl = slice(oc * TC, (oc + 1) * TC)

                    @block.scalar
                    def _(e, k=k, tc=tc, col=col, tsl=tsl, osl=osl):
                        e.wait_ge(q_tc.h, tcload_ms[tc])
                        if k >= 2:
                            e.wait_ge(dve.h, scan_ms[k - 2])
                        e.activation(out=abar_sb[:, tsl], in_=dt_tc[:, osl], func=AF.Exp,
                                     scale=A_sb[:, col:col + 1]).then_inc(act.h, 1)
                    abar_ms[k] = act.inc()

                    @block.vector
                    def _(e, k=k, tc=tc, col=col, sl=sl, tsl=tsl, osl=osl,
                          bsl=bsl, bc_tgt=bc_tgt):
                        e.wait_ge(q_bc[bsl].h, bc_tgt)
                        e.wait_ge(q_tc.h, tcload_ms[tc])
                        e.tensor_mul(out=bx_sb[:, tsl], in0=u_tc[:, osl],
                                     in1=bbc_sb[:, bsl * TC:(bsl + 1) * TC])
                        e.wait_ge(act.h, abar_ms[k])
                        if k >= 2:
                            e.wait_ge(pool.h, w_ms[k - 2])
                        init = 0.0 if tc == 0 else state_sb[:, col:col + 1]
                        e.tensor_tensor_scan(out=h_sb[:, tsl], data0=abar_sb[:, tsl],
                                             data1=bx_sb[:, tsl], initial=init,
                                             op0=OP.mult, op1=OP.add).then_inc(dve.h, 1)
                    scan_ms[k] = dve.inc()
                    if oc == ODC - 1:
                        bxlast_u[m] = scan_ms[k]

                    @block.gpsimd
                    def _(e, k=k, tsl=tsl, bsl=bsl):
                        e.wait_ge(dve.h, scan_ms[k])
                        if k >= 2:
                            e.wait_ge(dve.h, ybar_ms[k - 2])
                        e.tensor_mul(out=w_sb[:, tsl], in0=h_sb[:, tsl],
                                     in1=cbc_sb[:, bsl * TC:(bsl + 1) * TC]
                                     ).then_inc(pool.h, 1)
                    w_ms[k] = pool.inc()
                    if oc == ODC - 1:
                        wlast_u[m] = w_ms[k]

                    if pending_y[0] is not None:
                        emit_y(pending_y[0])
                    pending_y[0] = k

                if n == 15:
                    emit_y(pending_y[0])
                    pending_y[0] = None
                    @block.vector
                    def _(e):
                        e.tensor_copy(out=bx_sb[:, 0:TC],
                                      in_=bx_sb[:, 0:TC]).then_inc(dve.h, 1)
                    park_gate = dve.inc()

                    for oc in range(ODC):
                        @block.sync
                        def _(e, tc=tc, oc=oc, park_gate=park_gate):
                            e.wait_ge(dve.h, park_gate)
                            e.dma_start(out=cc_in[oc * 128:(oc + 1) * 128,
                                                  tc * TC:(tc + 1) * TC],
                                        in_=y_tc[:, oc * TC:(oc + 1) * TC]
                                        ).then_inc(q_park.h, 16)
                        park_tc[(tc, oc)] = q_park.inc(16)

            def emit_ffn_pair_load(hf, qp):
                sl = qp % 2

                @block.sync
                def _(e, hf=hf, qp=qp, sl=sl):
                    if qp >= 2:
                        e.wait_ge(pe.h, fpe_ms[(hf, 2 * qp - 3)])
                    elif hf == 1:
                        e.wait_ge(pe.h, fpe_ms[(0, HR - 1)])
                    srcg = wg_bf.rearrange("(c p) h -> p c h", p=128)
                    srcu = wu_bf.rearrange("(c p) h -> p c h", p=128)
                    e.dma_start(out=wg2_sb[:, sl * NDC * 256:(sl + 1) * NDC * 256]
                                .rearrange("p (c h) -> p c h", c=NDC),
                                in_=srcg[:, :, qp * 256:(qp + 1) * 256]).then_inc(q_wp[sl].h, 16)
                    e.dma_start(out=wu2_sb[:, sl * NDC * 256:(sl + 1) * NDC * 256]
                                .rearrange("p (c h) -> p c h", c=NDC),
                                in_=srcu[:, :, qp * 256:(qp + 1) * 256]).then_inc(q_wp[sl].h, 16)
                fw_ms[(hf, qp)] = q_wp[sl].inc(32)

            def emit_ffn_half_prologue(hf):
                @block.sync
                def _(e, hf=hf):
                    if hf == 1:
                        e.wait_ge(pe.h, fpe_ms[(0, HR - 1)])
                    for dc in range(NDC):
                        e.dma_start(out=xnf_sb[:, dc * 1024:(dc + 1) * 1024],
                                    in_=xnown_dram[hf * 1024:(hf + 1) * 1024,
                                                   dc * 128:(dc + 1) * 128],
                                    transpose=True).then_inc(q_xnf.h, 16)
                xnf_ms[hf] = q_xnf.inc(16 * NDC)

                for tch in range(2):
                    @block.tensor
                    def _(e, hf=hf, tch=tch):
                        if tch == 0:
                            e.wait_ge(q_xnf.h, xnf_ms[hf])
                            if hf == 1:
                                e.wait_ge(dve.h, x1_ms[0])
                                e.wait_ge(act.h, fact_ps[(0, HR - 2)])
                                e.wait_ge(act.h, fact_ps[(0, HR - 1)])
                        else:
                            e.wait_ge(dve.h, dve.v)
                        for dc in range(NDC):
                            mm = e.matmul(out=ps[0:1, 0:512], lhsT=ones_bf[:],
                                          rhs=xnf_sb[:, dc * 1024 + tch * 512:
                                                     dc * 1024 + (tch + 1) * 512],
                                          start=(dc == 0), stop=(dc == NDC - 1))
                            if dc == NDC - 1:
                                mm.then_inc(pe.h, 1)
                    x1pe = pe.inc()

                    @block.vector
                    def _(e, hf=hf, tch=tch, x1pe=x1pe):
                        e.wait_ge(pe.h, x1pe)
                        dst = slice(hf * 1024 + tch * 512, hf * 1024 + (tch + 1) * 512)
                        e.tensor_copy(out=x1row_f[0:1, dst], in_=ps[0:1, 0:512])
                        e.tensor_copy(out=x1row_bf[0:1, dst], in_=x1row_f[0:1, dst])
                        e.tensor_copy(out=x1row_f[0:1, dst], in_=x1row_f[0:1, dst])
                        e.tensor_copy(out=x1row_f[0:1, dst],
                                      in_=x1row_f[0:1, dst]).then_inc(dve.h, 1)
                    x1_ms[hf] = dve.inc()

            def emit_ffn_unit(f):
                hf, hr = divmod(f, HR)
                if hr == 0:
                    emit_ffn_half_prologue(hf)
                    emit_ffn_pair_load(hf, 0)
                    emit_ffn_pair_load(hf, 1)
                if hr % 2 == 0 and hr + 2 < HR:
                    emit_ffn_pair_load(hf, hr // 2 + 1)
                qp = hr // 2
                wsl = qp % 2
                pb = (hr % 2) * 2048
                hh = (hr % 2) * 128
                pf = None
                if f >= 2:
                    pf = (hf, hr - 2) if hr >= 2 else (0, HR - 2 + hr)

                @block.tensor
                def _(e, f=f, hf=hf, hr=hr, qp=qp, wsl=wsl, pb=pb, hh=hh, pf=pf):
                    e.wait_ge(q_wp[wsl].h, fw_ms[(hf, qp)])
                    e.wait_ge(q_xnf.h, xnf_ms[hf])
                    e.wait_ge(dve.h, x1_ms[hf])
                    if pf is not None:
                        e.wait_ge(act.h, fact_ps[pf])
                    for mat, wsb in ((0, wg2_sb), (1, wu2_sb)):
                        ub = pb + mat * 1024
                        for dc in range(NDC):
                            lhs = wsb[:, wsl * NDC * 256 + dc * 256 + hh:
                                      wsl * NDC * 256 + dc * 256 + hh + 128]
                            for tch in range(2):
                                e.matmul(out=ps[:, ub + tch * 512:ub + (tch + 1) * 512],
                                         lhsT=lhs,
                                         rhs=xnf_sb[:, dc * 1024 + tch * 512:
                                                    dc * 1024 + (tch + 1) * 512],
                                         start=(dc == 0), stop=False)
                        for tch in range(2):
                            mm = e.matmul(out=ps[:, ub + tch * 512:ub + (tch + 1) * 512],
                                          lhsT=n128_bf[:],
                                          rhs=x1row_bf[0:1, hf * 1024 + tch * 512:
                                                       hf * 1024 + (tch + 1) * 512],
                                          start=False, stop=True)
                            if mat == 1 and tch == 1:
                                mm.then_inc(pe.h, 1)
                fpe_ms[(hf, hr)] = pe.inc()

                @block.scalar
                def _(e, f=f, hf=hf, hr=hr, pb=pb, pf=pf):
                    e.wait_ge(pe.h, fpe_ms[(hf, hr)])
                    if pf is not None:
                        e.wait_ge(pool.h, fpool_ms[pf])
                    sl = f % 2
                    dst = slice(sl * 1024, (sl + 1) * 1024)
                    sgc = sgh_sb[:, hr:hr + 1]
                    suc = su_sb[:, hr:hr + 1]
                    e.activation(out=th_sb[:, dst], in_=ps[:, pb:pb + 1024],
                                 func=AF.Tanh, scale=sgc)
                    e.activation(out=gh_sb[:, dst], in_=ps[:, pb:pb + 1024],
                                 func=AF.Identity, scale=sgc)
                    e.activation(out=ups_sb[:, dst], in_=ps[:, pb + 1024:pb + 2048],
                                 func=AF.Identity, scale=suc).then_inc(act.h, 1)
                    e.activation(out=th1_sb[:, dst], in_=th_sb[:, dst],
                                 func=AF.Identity, bias=1.0).then_inc(act.h, 1)
                fact_ps[(hf, hr)] = act.inc()
                fact_th1[(hf, hr)] = act.inc()

                @block.gpsimd
                def _(e, f=f, hf=hf, hr=hr, pf=pf):
                    e.wait_ge(act.h, fact_th1[(hf, hr)])
                    if pf is not None:
                        e.wait_ge(q_ast[f % 2].h, fst_ms[pf])
                    sl = f % 2
                    dst = slice(sl * 1024, (sl + 1) * 1024)
                    e.tensor_mul(out=t3_sb[:, dst], in0=gh_sb[:, dst], in1=ups_sb[:, dst])
                    e.tensor_mul(out=actt_sb[:, dst], in0=t3_sb[:, dst],
                                 in1=th1_sb[:, dst]).then_inc(pool.h, 1)
                fpool_ms[(hf, hr)] = pool.inc()

                @block.sync
                def _(e, f=f, hf=hf, hr=hr):
                    e.wait_ge(pool.h, fpool_ms[(hf, hr)])
                    sl = f % 2
                    e.dma_start(out=act_dram[hr * 128:(hr + 1) * 128,
                                             hf * 1024:(hf + 1) * 1024],
                                in_=actt_sb[:, sl * 1024:(sl + 1) * 1024]
                                ).then_inc(q_ast[sl].h, 16)
                fst_ms[(hf, hr)] = q_ast[f % 2].inc(16)

            fidx = 0
            for m in range(NTC * 16):
                emit_mixer_unit(m)
                tgt = (m + 1) * (2 * HR) // (NTC * 16)
                while fidx < tgt:
                    emit_ffn_unit(fidx)
                    fidx += 1
            while fidx < 2 * HR:
                emit_ffn_unit(fidx)
                fidx += 1

            @block.gpsimd
            def _(e):
                e.wait_ge(q_park.h, q_park.v)
                e.collective_compute(
                    "AllGather", OP.bypass,
                    replica_groups=[[0, 1], [2, 3], [4, 5], [6, 7]],
                    ins=[cc_in.opt()], outs=[cc_out.opt()]).then_inc(ccs.h)
            ccs.inc()

        barrier()

        p4 = contextlib.ExitStack()
        with p4:
            def p4sb(nm, shp, dt):
                return p4.enter_context(nc.sbuf_tensor(nm, shp, dt))
            actq_sb = p4sb("actq_sb", [128, HR * 512], BF16)
            s_sb = p4sb("s_sb", [128, NDC * 512], BF16)
            wdt2_sb = p4sb("wdt2_sb", [128, 8 * 512], BF16)
            th2_sb = p4sb("th2_sb", [128, 8 * 512], BF16)
            gh2_sb = p4sb("gh2_sb", [128, 8 * 512], BF16)
            g2_sb = p4sb("g2_sb", [128, 8 * 512], BF16)
            sg2_sb = p4sb("sg2_sb", [128, 8 * 512], BF16)
            sgt_sb = p4sb("sgt_sb", [128, 2 * 2048], BF16)
            xo_sb = p4sb("xo_sb", [128, 2 * 2048], F32)
            ot_sb = p4sb("ot_sb", [128, 2 * 2048], F32)

            q_aq = ring("q_aq")
            q_ssb = ring("q_ssb")
            q_wd2 = [ring("q_wd2%d" % i) for i in range(8)]
            wd_seq = []
            q_sg = [ring("q_sg0"), ring("q_sg1")]
            q_cmb = [ring("q_cmb0"), ring("q_cmb1")]
            q_out = [ring("q_out0"), ring("q_out1")]

            aq_ms, qpe_hr, qpe_ms, qact_ms, qdve_ms, qpool_ms, qst_ms = {}, {}, {}, {}, {}, {}, {}
            wd_ms, a1e_ms, out_ms, out_pool, s_ld_tq, cmb_ms = {}, {}, {}, {}, {}, {}
            wd_slot = {}

            for tq in range(4):
                @block.sync
                def _(e, tq=tq):
                    if tq >= 1:
                        e.wait_ge(pe.h, qpe_ms[(tq - 1, 3)])
                    for hr in range(HR):
                        e.dma_start(out=actq_sb[:, hr * 512:(hr + 1) * 512],
                                    in_=act_dram[hr * 128:(hr + 1) * 128,
                                                 tq * 512:(tq + 1) * 512]).then_inc(q_aq.h, 16)
                aq_ms[tq] = q_aq.inc(16 * HR)

                @block.sync
                def _(e, tq=tq):
                    e.wait_ge(ccs.h, 1)
                    if tq >= 1:
                        e.wait_ge(pool.h, qpool_ms[(tq - 1, 3)])
                    for arm in nc.switch_core_id(engines=[e], n=128):
                        pos = int(arm) % 2
                        off = pos * TOWN + tq * 512
                        for g in range(NDC):
                            if g < ODC:
                                srcap = cc_out[pos * 1024 + g * 128:
                                               pos * 1024 + (g + 1) * 128,
                                               off:off + 512]
                            else:
                                p_ = 1 - pos
                                srcap = cc_out[p_ * 1024 + (g - ODC) * 128:
                                               p_ * 1024 + (g - ODC + 1) * 128,
                                               off:off + 512]
                            e.dma_start(out=s_sb[:, g * 512:(g + 1) * 512],
                                        in_=srcap).then_inc(q_ssb.h, 16)
                s_ld_tq[tq] = q_ssb.inc(16 * NDC)

                for g4 in range(4):
                    qi = tq * 4 + g4
                    base = (g4 % 2) * 2048
                    for hr in range(HR):
                        wi = len(wd_seq)
                        sl = wi % 8
                        wd_seq.append((tq, g4, hr))

                        @block.sync
                        def _(e, tq=tq, g4=g4, hr=hr, sl=sl, wi=wi):
                            if wi >= 8:
                                e.wait_ge(pe.h, qpe_hr[wd_seq[wi - 8]])
                            e.dma_start(out=wdt2_sb[:, sl * 512:(sl + 1) * 512],
                                        in_=wd_bf[g4 * 512:(g4 + 1) * 512,
                                                  hr * 128:(hr + 1) * 128],
                                        transpose=True).then_inc(q_wd2[sl].h, 16)
                        wd_ms[(tq, g4, hr)] = q_wd2[sl].inc(16)
                        wd_slot[(tq, g4, hr)] = sl

                        @block.tensor
                        def _(e, tq=tq, g4=g4, hr=hr, sl=sl, base=base):
                            e.wait_ge(q_wd2[sl].h, wd_ms[(tq, g4, hr)])
                            if hr == 0:
                                e.wait_ge(q_aq.h, aq_ms[tq])
                                qi2 = tq * 4 + g4
                                if qi2 >= 2:
                                    pq = divmod(qi2 - 2, 4)
                                    e.wait_ge(act.h, qact_ms[(pq[0], pq[1])])
                                if g4 == 1:
                                    e.wait_ge(dve.h, a1e_ms[tq])
                            rhs = actq_sb[:, hr * 512:(hr + 1) * 512]
                            for j in range(4):
                                mm = e.matmul(out=ps[:, base + j * 512:base + (j + 1) * 512],
                                              lhsT=wdt2_sb[:, sl * 512 + j * 128:
                                                           sl * 512 + (j + 1) * 128],
                                              rhs=rhs, start=(hr == 0), stop=False)
                                if j == 3 and g4 != 0:
                                    mm.then_inc(pe.h, 1)
                            if g4 == 0:
                                mm = e.matmul(out=ps[0:1, 2048:2560], lhsT=ones_bf[:],
                                              rhs=rhs, start=(hr == 0), stop=(hr == HR - 1))
                                mm.then_inc(pe.h, 1)
                        qpe_hr[(tq, g4, hr)] = pe.inc()

                    if g4 == 0:
                        @block.vector
                        def _(e, tq=tq):
                            e.wait_ge(pe.h, qpe_hr[(tq, 0, HR - 1)])
                            e.tensor_copy(out=a1row_f[:], in_=ps[0:1, 2048:2560])
                            e.tensor_copy(out=a1row_bf[:], in_=a1row_f[:])
                            e.tensor_copy(out=a1row_f[:], in_=a1row_f[:])
                            e.tensor_copy(out=a1row_f[:], in_=a1row_f[:])
                            e.tensor_copy(out=a1row_f[:],
                                          in_=a1row_f[:]).then_inc(dve.h, 1)
                        a1e_ms[tq] = dve.inc()

                    @block.tensor
                    def _(e, tq=tq, g4=g4, base=base):
                        if g4 == 0:
                            e.wait_ge(dve.h, a1e_ms[tq])
                        for j in range(4):
                            mm = e.matmul(out=ps[:, base + j * 512:base + (j + 1) * 512],
                                          lhsT=n128_bf[:], rhs=a1row_bf[:],
                                          start=False, stop=True)
                            if j == 3:
                                mm.then_inc(pe.h, 1)
                    qpe_ms[(tq, g4)] = pe.inc()

                    @block.scalar
                    def _(e, tq=tq, g4=g4, base=base, qi=qi):
                        e.wait_ge(pe.h, qpe_ms[(tq, g4)])
                        if qi >= 2:
                            pq = divmod(qi - 2, 4)
                            e.wait_ge(dve.h, qdve_ms[(pq[0], pq[1])])
                        for j in range(4):
                            gl = g4 * 4 + j
                            sl = (qi % 2) * 4 + j
                            dst = slice(sl * 512, (sl + 1) * 512)
                            psv = ps[:, base + j * 512:base + (j + 1) * 512]
                            sdc = sdh_sb[:, gl:gl + 1]
                            e.activation(out=th2_sb[:, dst], in_=psv, func=AF.Tanh,
                                         scale=sdc)
                            la = e.activation(out=gh2_sb[:, dst], in_=psv,
                                              func=AF.Identity, scale=sdc)
                            if j == 3:
                                la.then_inc(act.h, 1)
                    qact_ms[(tq, g4)] = act.inc()

                    @block.vector
                    def _(e, tq=tq, g4=g4, qi=qi):
                        e.wait_ge(act.h, qact_ms[(tq, g4)])
                        if qi >= 2:
                            pq = divmod(qi - 2, 4)
                            e.wait_ge(pool.h, qpool_ms[(pq[0], pq[1])])
                        for j in range(4):
                            sl = (qi % 2) * 4 + j
                            dst = slice(sl * 512, (sl + 1) * 512)
                            la = e.scalar_tensor_tensor(out=g2_sb[:, dst],
                                                        in0=th2_sb[:, dst], scalar=1.0,
                                                        in1=gh2_sb[:, dst],
                                                        op0=OP.add, op1=OP.mult)
                            if j == 3:
                                la.then_inc(dve.h, 1)
                    qdve_ms[(tq, g4)] = dve.inc()

                    @block.gpsimd
                    def _(e, tq=tq, g4=g4, qi=qi):
                        e.wait_ge(dve.h, qdve_ms[(tq, g4)])
                        e.wait_ge(q_ssb.h, s_ld_tq[tq])
                        if qi >= 2:
                            pq = divmod(qi - 2, 4)
                            e.wait_ge(q_sg[qi % 2].h, qst_ms[(pq[0], pq[1])])
                        for j in range(4):
                            gl = g4 * 4 + j
                            sl = (qi % 2) * 4 + j
                            dst = slice(sl * 512, (sl + 1) * 512)
                            la = e.tensor_mul(out=sg2_sb[:, dst], in0=g2_sb[:, dst],
                                              in1=s_sb[:, gl * 512:(gl + 1) * 512])
                            if j == 3:
                                la.then_inc(pool.h, 1)
                    qpool_ms[(tq, g4)] = pool.inc()

                    @block.sync
                    def _(e, tq=tq, g4=g4, qi=qi):
                        e.wait_ge(pool.h, qpool_ms[(tq, g4)])
                        for j in range(4):
                            gl = g4 * 4 + j
                            sl = (qi % 2) * 4 + j
                            e.dma_start(out=sg_dram[gl * 128:(gl + 1) * 128,
                                                    tq * 512:(tq + 1) * 512],
                                        in_=sg2_sb[:, sl * 512:(sl + 1) * 512]
                                        ).then_inc(q_sg[qi % 2].h, 16)
                    qst_ms[(tq, g4)] = q_sg[qi % 2].inc(16 * 4)

                for tt in range(4):
                    ti = tq * 4 + tt
                    row0 = ti * 128
                    csl = ti % 2

                    @block.sync
                    def _(e, tq=tq, ti=ti, row0=row0, csl=csl):
                        e.wait_ge(q_sg[0].h, q_sg[0].v)
                        e.wait_ge(q_sg[1].h, q_sg[1].v)
                        if ti >= 2:
                            e.wait_ge(pool.h, out_pool[ti - 2])
                        e.dma_start(out=sgt_sb[:, csl * 2048:(csl + 1) * 2048],
                                    in_=sg_dram[0:D, ti * 128:(ti + 1) * 128],
                                    transpose=True).then_inc(q_cmb[csl].h, 16)
                        e.dma_start(out=xo_sb[:, csl * 2048:(csl + 1) * 2048],
                                    in_=x_own[row0:row0 + 128, :]).then_inc(q_cmb[csl].h, 16)
                    cmb_ms[ti] = q_cmb[csl].inc(32)

                    @block.gpsimd
                    def _(e, ti=ti, csl=csl):
                        e.wait_ge(q_cmb[csl].h, cmb_ms[ti])
                        if ti >= 2:
                            e.wait_ge(q_out[csl].h, out_ms[ti - 2])
                        a = slice(csl * 2048, (csl + 1) * 2048)
                        e.tensor_add(out=ot_sb[:, a], in0=xo_sb[:, a],
                                     in1=sgt_sb[:, a]).then_inc(pool.h, 1)
                    out_pool[ti] = pool.inc()

                    @block.sync
                    def _(e, ti=ti, row0=row0, csl=csl):
                        e.wait_ge(pool.h, out_pool[ti])
                        e.dma_start(out=out_own[row0:row0 + 128, :],
                                    in_=ot_sb[:, csl * 2048:(csl + 1) * 2048]
                                    ).then_inc(q_out[csl].h, 16)
                    out_ms[ti] = q_out[csl].inc(16)

            if debug:
                barrier()

                @block.gpsimd
                def _(e):
                    e.dma_start(out=dbg["dbg_xn"], in_=xn_dram).then_inc(gq.h, 16)
                    e.dma_start(out=dbg["dbg_bc"], in_=bc_dram).then_inc(gq.h, 16)
                    e.dma_start(out=dbg["dbg_dt"], in_=dt_dram).then_inc(gq.h, 16)
                    e.dma_start(out=dbg["dbg_u"], in_=u_dram).then_inc(gq.h, 16)
                    e.dma_start(out=dbg["dbg_s"], in_=cc_in).then_inc(gq.h, 16)
                    e.dma_start(out=dbg["dbg_act"], in_=act_dram).then_inc(gq.h, 16)
                    e.dma_start(out=dbg["dbg_st"], in_=st_dram).then_inc(gq.h, 16)
                gq.inc(16 * 7)

            @block.sync
            def _(e):
                e.wait_ge(q_out[0].h, q_out[0].v)
                e.wait_ge(q_out[1].h, q_out[1].v)

            @block.gpsimd
            def _(e):
                e.wait_ge(gq.h, gq.v)

    return nc


_NC_CACHE = {}


def _get_nc(debug=False):
    key = bool(debug)
    if key not in _NC_CACHE:
        _NC_CACHE[key] = build(debug=key)
    return _NC_CACHE[key]


def _perm(dh):
    return np.arange(D) if dh == 0 else np.concatenate(
        [np.arange(1024, 2048), np.arange(0, 1024)])


def make_inputs(L0_attn_x, L0_attn_x_proj_w, L0_attn_dt_proj_w, L0_attn_dt_proj_b,
                L0_attn_A_log, L0_attn_D_param, L0_ffn_wq_gate, L0_ffn_s_gate,
                L0_ffn_wq_up, L0_ffn_s_up, L0_ffn_wq_down, L0_ffn_s_down):
    x = np.ascontiguousarray(np.asarray(L0_attn_x, dtype=np.float32))
    xp = np.asarray(L0_attn_x_proj_w, dtype=np.float32)
    wdt = np.asarray(L0_attn_dt_proj_w, dtype=np.float32)
    dtb = np.asarray(L0_attn_dt_proj_b, dtype=np.float32)
    alog = np.asarray(L0_attn_A_log, dtype=np.float32)
    dp = np.asarray(L0_attn_D_param, dtype=np.float32)
    wqg = np.asarray(L0_ffn_wq_gate, dtype=np.int32)
    wqu = np.asarray(L0_ffn_wq_up, dtype=np.int32)
    wqd = np.asarray(L0_ffn_wq_down, dtype=np.int32)
    sgv = np.asarray(L0_ffn_s_gate, dtype=np.float32)
    suv = np.asarray(L0_ffn_s_up, dtype=np.float32)
    sdv = np.asarray(L0_ffn_s_down, dtype=np.float32)

    per_dh = {}
    for dh in range(2):
        perm = _perm(dh)
        own = perm[:1024]
        per_dh[dh] = dict(
            wgT=np.ascontiguousarray(wqg.T[perm, :]),
            wuT=np.ascontiguousarray(wqu.T[perm, :]),
            wdn=np.ascontiguousarray(wqd[perm, :]),
            xprojT=np.ascontiguousarray(xp.T[perm, :]),
            wdtT_own=np.ascontiguousarray(wdt.T[:, own]),
            dtb_own=np.ascontiguousarray(dtb[own].reshape(ODC, 128).T),
            Alog_own=np.ascontiguousarray(
                alog[own].reshape(ODC, 128, N).transpose(1, 0, 2).reshape(128, ODC * N)),
            Dp_own=np.ascontiguousarray(dp[own].reshape(ODC, 128).T),
            sd_in=np.ascontiguousarray(sdv[perm].reshape(NDC, 128).T),
        )
    sg_t = np.ascontiguousarray(sgv.reshape(HR, 128).T)
    su_t = np.ascontiguousarray(suv.reshape(HR, 128).T)
    cst_arr = np.broadcast_to(
        np.array([EPS, -0.5, -1.0, 1.0], np.float32), (128, 4)).copy()
    ones_arr = np.ones((128, 1), ml_dtypes.bfloat16)
    n128_arr = np.full((1, 128), -128.0, ml_dtypes.bfloat16)

    in_maps = []
    for c in range(8):
        seq, dh = c // 2, c % 2
        perm = _perm(dh)
        xs = np.ascontiguousarray(x[seq][:, perm])
        m = dict(per_dh[dh])
        m.update(x_seq=xs,
                 x_own=np.ascontiguousarray(xs[dh * TOWN:(dh + 1) * TOWN]),
                 sg_in=sg_t, su_in=su_t,
                 cst_in=cst_arr, ones_in=ones_arr, n128_in=n128_arr)
        in_maps.append(m)
    return in_maps


def run(inputs, debug=False, **kw):
    in_maps = make_inputs(**inputs)
    nc = _get_nc(debug=debug)
    if kw.get("trace") and "trace_cores" not in kw:
        kw["trace_cores"] = [0]
    return run_bass_kernel_spmd(nc, in_maps, core_ids=list(range(8)), **kw)


def kernel(**inputs):
    res = run(inputs, debug=False)
    out = np.zeros((4, SEQ, D), dtype=np.float32)
    for c in range(8):
        seq, dh = c // 2, c % 2
        inv = np.argsort(_perm(dh))
        r = np.asarray(res.results[c]["out_own"])
        out[seq, dh * TOWN:(dh + 1) * TOWN, :] = r[:, inv]
    return out


# version 07
# speedup vs baseline: 1.2487x; 1.1111x over previous
import numpy as np
import ml_dtypes
import contextlib
import concourse.bass as bass
import concourse.mybir as mybir
from concourse.bass_utils import run_bass_kernel_spmd

F32 = mybir.dt.float32
BF16 = mybir.dt.bfloat16
I32 = mybir.dt.int32
AF = mybir.ActivationFunctionType
OP = mybir.AluOpType
AX = mybir.AxisListType

SEQ, TOWN, D, N, R, HID = 4096, 2048, 2048, 16, 128, 5632
NDC, ODC, HR = 16, 8, 44
TC, NTC = 1024, 4
EPS = 1e-5


def load_act_set(nc, e, name):
    from concourse.hw_specs import get_activation_tables
    idx = list(get_activation_tables(nc.m.arch).keys()).index(name)
    e.add_instruction(mybir.InstLoadActFuncSet(
        name=nc.get_next_instruction_name(), act_func_set_id=idx, ins=[], outs=[]))


class Sem:
    def __init__(self, h):
        self.h = h
        self.v = 0

    def inc(self, k=1):
        self.v += k
        return self.v


def build(debug=False):
    nc = bass.Bass("TRN2", target_bir_lowering=False)

    x_seq = nc.dram_tensor("x_seq", [SEQ, D], F32, kind="ExternalInput").ap()
    x_own = nc.dram_tensor("x_own", [TOWN, D], F32, kind="ExternalInput").ap()
    xprojT = nc.dram_tensor("xprojT", [D, 160], F32, kind="ExternalInput").ap()
    wdtT_own = nc.dram_tensor("wdtT_own", [R, ODC * 128], F32, kind="ExternalInput").ap()
    dtb_own = nc.dram_tensor("dtb_own", [128, ODC], F32, kind="ExternalInput").ap()
    Alog_own = nc.dram_tensor("Alog_own", [128, ODC * N], F32, kind="ExternalInput").ap()
    Dp_own = nc.dram_tensor("Dp_own", [128, ODC], F32, kind="ExternalInput").ap()
    wgT = nc.dram_tensor("wgT", [D, HID], I32, kind="ExternalInput").ap()
    wuT = nc.dram_tensor("wuT", [D, HID], I32, kind="ExternalInput").ap()
    wdn = nc.dram_tensor("wdn", [HID, D], I32, kind="ExternalInput").ap()
    sg_in = nc.dram_tensor("sg_in", [128, HR], F32, kind="ExternalInput").ap()
    su_in = nc.dram_tensor("su_in", [128, HR], F32, kind="ExternalInput").ap()
    sd_in = nc.dram_tensor("sd_in", [128, NDC], F32, kind="ExternalInput").ap()
    cst_in = nc.dram_tensor("cst_in", [128, 4], F32, kind="ExternalInput").ap()
    ones_in = nc.dram_tensor("ones_in", [128, 1], BF16, kind="ExternalInput").ap()
    n128_in = nc.dram_tensor("n128_in", [1, 128], BF16, kind="ExternalInput").ap()
    out_own = nc.dram_tensor("out_own", [TOWN, D], F32, kind="ExternalOutput").ap()

    xn_dram = nc.dram_tensor("xn_dram", [SEQ, D], BF16, kind="Internal").ap()
    xnown_dram = nc.dram_tensor("xnown_dram", [TOWN, D], BF16, kind="Internal").ap()
    bc_dram = nc.dram_tensor("bc_dram", [32, SEQ], BF16, kind="Internal").ap()
    dt_dram = nc.dram_tensor("dt_dram", [ODC * 128, SEQ], BF16, kind="Internal").ap()
    u_dram = nc.dram_tensor("u_dram", [ODC * 128, SEQ], BF16, kind="Internal").ap()
    dxn_dram = nc.dram_tensor("dxn_dram", [ODC * 128, SEQ], BF16, kind="Internal").ap()
    wg_bf = nc.dram_tensor("wg_bf", [D, HID], BF16, kind="Internal").ap()
    wu_bf = nc.dram_tensor("wu_bf", [D, HID], BF16, kind="Internal").ap()
    wd_bf = nc.dram_tensor("wd_bf", [HID, D], BF16, kind="Internal").ap()
    act_dram = nc.dram_tensor("act_dram", [HID, TOWN], BF16, kind="Internal").ap()
    cc_in = nc.dram_tensor("cc_in", [ODC * 128, SEQ], BF16, kind="Internal").ap()
    cc_out = nc.dram_tensor("cc_out", [2 * ODC * 128, SEQ], BF16, kind="Internal").ap()
    sg_dram = nc.dram_tensor("sg_dram", [D, TOWN], BF16, kind="Internal").ap()
    st_dram = nc.dram_tensor("st_dram", [128, 16 * 48], F32, kind="Internal").ap()
    dbg = {}
    if debug:
        for nm, shp in (("dbg_xn", [SEQ, D]), ("dbg_bc", [32, SEQ]),
                        ("dbg_dt", [ODC * 128, SEQ]), ("dbg_u", [ODC * 128, SEQ]),
                        ("dbg_s", [ODC * 128, SEQ]), ("dbg_act", [HID, TOWN]),
                        ("dbg_st", [128, 16 * 48])):
            dbg[nm] = nc.dram_tensor(nm, shp, F32, kind="ExternalOutput").ap()

    es = contextlib.ExitStack()

    def sb(name, shape, dt):
        return es.enter_context(nc.sbuf_tensor(name, shape, dt))

    with es, nc.Block() as block:
        xprojT_bf = sb("xprojT_bf", [128, NDC * 160], BF16)
        wdtT_bf = sb("wdtT_bf", [128, ODC * 128], BF16)
        dtb_sb = sb("dtb_sb", [128, ODC], F32)
        A_sb = sb("A_sb", [128, ODC * N], F32)
        Dp_sb = sb("Dp_sb", [128, ODC], F32)
        sgh_sb = sb("sgh_sb", [128, HR], F32)
        su_sb = sb("su_sb", [128, HR], F32)
        sdh_sb = sb("sdh_sb", [128, NDC], F32)
        ones_bf = sb("ones_bf", [128, 1], BF16)
        n128_bf = sb("n128_bf", [1, 128], BF16)
        dtlowT_bf = sb("dtlowT_bf", [128, SEQ], BF16)
        bcT_bf = sb("bcT_bf", [32, SEQ], BF16)
        x1row_f = sb("x1row_f", [1, TOWN], F32)
        x1row_bf = sb("x1row_bf", [1, TOWN], BF16)
        a1row_f = sb("a1row_f", [1, 512], F32)
        a1row_bf = sb("a1row_bf", [1, 512], BF16)
        state_sb = sb("state_sb", [128, 128], F32)
        const_sb = sb("const_sb", [128, 4], F32)
        nc.const_aps.register(F32, EPS, const_sb[:, 0:1])
        nc.const_aps.register(F32, -0.5, const_sb[:, 1:2])
        nc.const_aps.register(F32, -1.0, const_sb[:, 2:3])
        nc.const_aps.register(F32, 1.0, const_sb[:, 3:4])
        ps = es.enter_context(nc.psum_tensor("ps", [128, 4096], F32))

        pe = Sem(es.enter_context(nc.semaphore("pe")))
        act = Sem(es.enter_context(nc.semaphore("act")))
        dve = Sem(es.enter_context(nc.semaphore("dve")))
        pool = Sem(es.enter_context(nc.semaphore("pool")))
        ccs = Sem(es.enter_context(nc.semaphore("ccs")))
        bar = Sem(es.enter_context(nc.semaphore("bar")))
        rings = {}

        def ring(name):
            if name not in rings:
                rings[name] = Sem(es.enter_context(nc.semaphore(name)))
            return rings[name]

        gq = ring("gq")

        def barrier():
            tgt = bar.v + 5
            ring_waits = [(r.h, r.v) for r in rings.values() if r.v]

            @block.sync
            def _(e):
                for h, v in ring_waits:
                    e.wait_ge(h, v)
                e.sem_inc(bar.h, 1)
                e.wait_ge(bar.h, tgt)

            @block.tensor
            def _(e):
                e.wait_ge(pe.h, pe.v)
                e.sem_inc(bar.h, 1)
                e.wait_ge(bar.h, tgt)

            @block.vector
            def _(e):
                e.wait_ge(dve.h, dve.v)
                e.sem_inc(bar.h, 1)
                e.wait_ge(bar.h, tgt)

            @block.scalar
            def _(e):
                e.wait_ge(act.h, act.v)
                e.sem_inc(bar.h, 1)
                e.wait_ge(bar.h, tgt)

            @block.gpsimd
            def _(e):
                e.wait_ge(gq.h, gq.v)
                e.wait_ge(pool.h, pool.v)
                e.sem_inc(bar.h, 1)
                e.wait_ge(bar.h, tgt)
            bar.v = tgt

        gq2 = ring("gq2")

        @block.gpsimd
        def _(g):
            g.dma_start(out=xprojT_bf[:].rearrange("p (c f) -> p c f", c=NDC),
                        in_=xprojT.rearrange("(c p) f -> p c f", p=128)).then_inc(gq2.h, 16)
            g.dma_start(out=wdtT_bf[:], in_=wdtT_own).then_inc(gq2.h, 16)
            g.dma_start(out=wg_bf, in_=wgT).then_inc(gq.h, 16)
            g.dma_start(out=wu_bf, in_=wuT).then_inc(gq.h, 16)
            g.dma_start(out=wd_bf, in_=wdn).then_inc(gq.h, 16)
        gq.inc(16 * 3)
        gq2.inc(16 * 2)

        q_par = ring("q_par")

        @block.sync
        def _(e):
            e.dma_start(out=dtb_sb[:], in_=dtb_own).then_inc(q_par.h, 16)
            e.dma_start(out=A_sb[:], in_=Alog_own).then_inc(q_par.h, 16)
            e.dma_start(out=Dp_sb[:], in_=Dp_own).then_inc(q_par.h, 16)
            e.dma_start(out=sgh_sb[:], in_=sg_in).then_inc(q_par.h, 16)
            e.dma_start(out=su_sb[:], in_=su_in).then_inc(q_par.h, 16)
            e.dma_start(out=sdh_sb[:], in_=sd_in).then_inc(q_par.h, 16)
            e.dma_start(out=const_sb[:], in_=cst_in).then_inc(q_par.h, 16)
            e.dma_start(out=ones_bf[:], in_=ones_in).then_inc(q_par.h, 16)
            e.dma_start(out=n128_bf[:], in_=n128_in).then_inc(q_par.h, 16)
        q_par.inc(16 * 9)

        @block.scalar
        def _(e):
            load_act_set(nc, e, "natural_log_exp_and_others")
            e.wait_ge(q_par.h, q_par.v)
            e.activation(out=A_sb[:], in_=A_sb[:], func=AF.Exp).then_inc(act.h, 1)
        a_exp = act.inc()

        @block.vector
        def _(e):
            e.wait_ge(act.h, a_exp)
            e.wait_ge(q_par.h, q_par.v)
            e.tensor_scalar_mul(out=A_sb[:], in0=A_sb[:], scalar1=-1.0)
            e.tensor_scalar_mul(out=sgh_sb[:], in0=sgh_sb[:], scalar1=0.5)
            e.tensor_scalar_mul(out=sdh_sb[:], in0=sdh_sb[:], scalar1=0.5).then_inc(dve.h, 1)
        dve.inc()

        p2c = contextlib.ExitStack()
        xnT_sb = p2c.enter_context(nc.sbuf_tensor("xnT_sb", [128, 4 * 2048], BF16))
        sp_a = p2c.enter_context(nc.sbuf_tensor("sp_a", [128, 2048], F32))
        sp_b = p2c.enter_context(nc.sbuf_tensor("sp_b", [128, 2048], F32))
        dtu_sb = p2c.enter_context(nc.sbuf_tensor("dtu_sb", [128, 3 * 2048], BF16))
        xnm_sb = p2c.enter_context(nc.sbuf_tensor("xnm_sb", [128, 2048], BF16))

        p1 = contextlib.ExitStack()
        with p1:
            xt_sb = p1.enter_context(nc.sbuf_tensor("xt_sb", [128, 4 * D], F32))
            dump_sb = p1.enter_context(nc.sbuf_tensor("dump_sb", [128, 2 * D], F32))
            st_sb = p1.enter_context(nc.sbuf_tensor("st_sb", [128, 16 * 48], F32))
            xnt_sb = p1.enter_context(nc.sbuf_tensor("xnt_sb", [128, 4 * D], BF16))

            ln_tiles = [("seq", t) for t in range(SEQ // 128)] + \
                       [("own", t) for t in range(TOWN // 128)]
            NT = len(ln_tiles)
            SSQ, SSUM, M_, SS_, MSQ, VAR, LNV, RSTD, NEGM = [i * 48 for i in range(9)]
            sum_act, sum_dve, xn_ms, st_ms = {}, {}, {}, {}
            q_xt = [ring("q_xt%d" % i) for i in range(4)]
            q_xnt = [ring("q_xnt%d" % i) for i in range(4)]
            st_ms1 = st_ms
            q_xnt1 = q_xnt

            for k, (src_, t) in enumerate(ln_tiles):
                sl = k % 4
                dsl = k % 2
                xsrc = x_seq if src_ == "seq" else x_own

                @block.sync
                def _(e, k=k, sl=sl, t=t, xsrc=xsrc):
                    if k >= 4:
                        e.wait_ge(dve.h, sum_dve[k - 4])
                    e.dma_start(out=xt_sb[:, sl * D:(sl + 1) * D],
                                in_=xsrc[t * 128:(t + 1) * 128, :]).then_inc(q_xt[sl].h, 16)
                xt_tgt = q_xt[sl].inc(16)

                @block.scalar
                def _(e, k=k, sl=sl, dsl=dsl, xt_tgt=xt_tgt):
                    e.wait_ge(q_xt[sl].h, xt_tgt)
                    if k >= 2:
                        e.wait_ge(dve.h, sum_dve[k - 2])
                    e.activation(out=dump_sb[:, dsl * D:(dsl + 1) * D],
                                 in_=xt_sb[:, sl * D:(sl + 1) * D],
                                 func=AF.Square).then_inc(act.h, 1)
                sum_act[k] = act.inc()

                @block.vector
                def _(e, k=k, sl=sl, dsl=dsl, xt_tgt=xt_tgt):
                    e.wait_ge(q_xt[sl].h, xt_tgt)
                    e.wait_ge(act.h, sum_act[k])
                    e.reduce_sum(out=st_sb[:, SSQ + k:SSQ + k + 1],
                                 in_=dump_sb[:, dsl * D:(dsl + 1) * D], axis=AX.X)
                    e.reduce_sum(out=st_sb[:, SSUM + k:SSUM + k + 1],
                                 in_=xt_sb[:, sl * D:(sl + 1) * D],
                                 axis=AX.X).then_inc(dve.h, 1)
                sum_dve[k] = dve.inc()

            @block.vector
            def _(e):
                e.wait_ge(act.h, sum_act[NT - 1])
                e.wait_ge(dve.h, sum_dve[NT - 1])
                e.tensor_scalar_mul(out=st_sb[:, M_:M_ + NT],
                                    in0=st_sb[:, SSUM:SSUM + NT], scalar1=1.0 / D)
                e.tensor_scalar_mul(out=st_sb[:, SS_:SS_ + NT],
                                    in0=st_sb[:, SSQ:SSQ + NT], scalar1=1.0 / D)
                e.tensor_copy(out=dump_sb[:, 0:D], in_=dump_sb[:, 0:D])
                e.tensor_copy(out=dump_sb[:, 0:D], in_=dump_sb[:, 0:D])
                e.tensor_mul(out=st_sb[:, MSQ:MSQ + NT], in0=st_sb[:, M_:M_ + NT],
                             in1=st_sb[:, M_:M_ + NT])
                e.tensor_copy(out=dump_sb[:, 0:D], in_=dump_sb[:, 0:D])
                e.tensor_copy(out=dump_sb[:, 0:D], in_=dump_sb[:, 0:D])
                e.tensor_sub(out=st_sb[:, VAR:VAR + NT], in0=st_sb[:, SS_:SS_ + NT],
                             in1=st_sb[:, MSQ:MSQ + NT])
                e.tensor_copy(out=dump_sb[:, 0:D], in_=dump_sb[:, 0:D])
                e.tensor_copy(out=dump_sb[:, 0:D],
                              in_=dump_sb[:, 0:D]).then_inc(dve.h, 1)
            stats_dve = dve.inc()

            @block.scalar
            def _(e):
                e.wait_ge(dve.h, stats_dve)
                e.activation(out=st_sb[:, LNV:LNV + NT], in_=st_sb[:, VAR:VAR + NT],
                             func=AF.Ln, bias=EPS)
                e.activation(out=dump_sb[:, 0:D], in_=dump_sb[:, 0:D], func=AF.Identity)
                e.activation(out=dump_sb[:, 0:D], in_=dump_sb[:, 0:D], func=AF.Identity)
                e.activation(out=st_sb[:, RSTD:RSTD + NT], in_=st_sb[:, LNV:LNV + NT],
                             func=AF.Exp, scale=const_sb[:, 1:2])
                e.activation(out=dump_sb[:, 0:D], in_=dump_sb[:, 0:D], func=AF.Identity)
                e.activation(out=dump_sb[:, 0:D], in_=dump_sb[:, 0:D],
                             func=AF.Identity).then_inc(act.h, 1)
            stats_act = act.inc()

            @block.vector
            def _(e):
                e.wait_ge(act.h, stats_act)
                e.scalar_tensor_tensor(out=st_sb[:, NEGM:NEGM + NT],
                                       in0=st_sb[:, M_:M_ + NT], scalar=-1.0,
                                       in1=st_sb[:, RSTD:RSTD + NT],
                                       op0=OP.mult, op1=OP.mult)
                e.tensor_copy(out=dump_sb[:, 0:D], in_=dump_sb[:, 0:D])
                e.tensor_copy(out=dump_sb[:, 0:D],
                              in_=dump_sb[:, 0:D]).then_inc(dve.h, 1)
            negm_dve = dve.inc()

            def emit_xn_store(j):
                src_, t = ln_tiles[j]
                xdst = xn_dram if src_ == "seq" else xnown_dram
                gate = xn_ms[j + 1] if (j + 1) in xn_ms else xn_ms[j]
                qs = q_xnt[j % 4]

                @block.sync
                def _(e, j=j, t=t, xdst=xdst, gate=gate, qs=qs):
                    e.wait_ge(dve.h, gate)
                    e.dma_start(out=xdst[t * 128:(t + 1) * 128, :],
                                in_=xnt_sb[:, (j % 4) * D:(j % 4 + 1) * D]
                                ).then_inc(qs.h, 16)
                st_ms[j] = qs.inc(16)

            for k, (src_, t) in enumerate(ln_tiles):
                sl = k % 4
                xsrc = x_seq if src_ == "seq" else x_own

                @block.sync
                def _(e, k=k, sl=sl, t=t, xsrc=xsrc):
                    if k >= 4:
                        e.wait_ge(dve.h, xn_ms[k - 4])
                    else:
                        e.wait_ge(dve.h, sum_dve[NT - 4 + k])
                    e.dma_start(out=xt_sb[:, sl * D:(sl + 1) * D],
                                in_=xsrc[t * 128:(t + 1) * 128, :]).then_inc(q_xt[sl].h, 16)
                xt_tgt = q_xt[sl].inc(16)

                @block.vector
                def _(e, k=k, sl=sl, xt_tgt=xt_tgt):
                    e.wait_ge(q_xt[sl].h, xt_tgt)
                    e.wait_ge(dve.h, negm_dve)
                    if k >= 4:
                        e.wait_ge(q_xnt[k % 4].h, st_ms[k - 4])
                    e.tensor_scalar(out=xnt_sb[:, (k % 4) * D:(k % 4 + 1) * D],
                                    in0=xt_sb[:, sl * D:(sl + 1) * D],
                                    scalar1=st_sb[:, RSTD + k:RSTD + k + 1],
                                    scalar2=st_sb[:, NEGM + k:NEGM + k + 1],
                                    op0=OP.mult, op1=OP.add).then_inc(dve.h, 1)
                xn_ms[k] = dve.inc()
                if k >= 1:
                    emit_xn_store(k - 1)

            @block.vector
            def _(e):
                e.tensor_copy(out=dump_sb[:, 0:D],
                              in_=dump_sb[:, 0:D]).then_inc(dve.h, 1)
            xn_ms[NT] = dve.inc()
            emit_xn_store(NT - 1)

            q_std = ring("q_std")

            @block.sync
            def _(e):
                e.wait_ge(dve.h, xn_ms[NT])
                e.dma_start(out=st_dram, in_=st_sb[:]).then_inc(q_std.h, 16)
            q_std.inc(16)

        p2 = contextlib.ExitStack()
        with p2:
            q_xnT = [ring("q_xnT%d" % i) for i in range(4)]
            q_bcd = ring("q_bcd")
            mm_ms, evac_ms = {}, {}
            for th2 in range(2):
                for dc in range(NDC):
                    j = th2 * NDC + dc
                    sl = j % 4

                    half_tiles = range(th2 * 16, th2 * 16 + 16)
                    halfw = []
                    for i in range(4):
                        vals = [st_ms1[jj] for jj in half_tiles if jj % 4 == i]
                        if vals:
                            halfw.append((i, max(vals)))

                    @block.sync
                    def _(e, j=j, sl=sl, th2=th2, dc=dc, halfw=tuple(halfw)):
                        if j == 0 or (j == NDC and th2 == 1):
                            for i, v in halfw:
                                e.wait_ge(q_xnt1[i].h, v)
                        if j >= 4:
                            e.wait_ge(pe.h, mm_ms[j - 4])
                        e.dma_start(out=xnT_sb[:, sl * 2048:(sl + 1) * 2048],
                                    in_=xn_dram[th2 * 2048:(th2 + 1) * 2048,
                                                dc * 128:(dc + 1) * 128],
                                    transpose=True).then_inc(q_xnT[sl].h, 16)
                    xnT_tgt = q_xnT[sl].inc(16)

                    @block.tensor
                    def _(e, j=j, sl=sl, dc=dc, th2=th2, xnT_tgt=xnT_tgt):
                        if j == 0:
                            e.wait_ge(gq2.h, 32)
                        e.wait_ge(q_xnT[sl].h, xnT_tgt)
                        if dc == 0 and th2 == 1:
                            e.wait_ge(dve.h, evac_ms[0])
                        for tch in range(4):
                            e.matmul(out=ps[:, tch * 512:(tch + 1) * 512],
                                     lhsT=xprojT_bf[:, dc * 160:dc * 160 + 128],
                                     rhs=xnT_sb[:, sl * 2048 + tch * 512:
                                                sl * 2048 + (tch + 1) * 512],
                                     start=(dc == 0), stop=(dc == NDC - 1))
                        for tch in range(4):
                            mm = e.matmul(out=ps[0:32, 2048 + tch * 512:2048 + (tch + 1) * 512],
                                          lhsT=xprojT_bf[:, dc * 160 + 128:dc * 160 + 160],
                                          rhs=xnT_sb[:, sl * 2048 + tch * 512:
                                                     sl * 2048 + (tch + 1) * 512],
                                          start=(dc == 0), stop=(dc == NDC - 1))
                            if tch == 3:
                                mm.then_inc(pe.h, 1)
                    mm_ms[j] = pe.inc()

                @block.vector
                def _(e, th2=th2):
                    e.wait_ge(pe.h, mm_ms[th2 * NDC + NDC - 1])
                    e.tensor_copy(out=dtlowT_bf[:, th2 * 2048:(th2 + 1) * 2048],
                                  in_=ps[:, 0:2048])
                    e.tensor_copy(out=bcT_bf[:, th2 * 2048:(th2 + 1) * 2048],
                                  in_=ps[0:32, 2048:4096])
                    e.tensor_copy(out=a1row_f[:], in_=a1row_f[:])
                    e.tensor_copy(out=a1row_f[:], in_=a1row_f[:])
                    e.tensor_copy(out=a1row_f[:], in_=a1row_f[:]).then_inc(dve.h, 1)
                evac_ms[th2] = dve.inc()

            @block.sync
            def _(e):
                e.wait_ge(dve.h, evac_ms[1])
                e.dma_start(out=bc_dram, in_=bcT_bf[:]).then_inc(q_bcd.h, 16)
            q_bcd.inc(16)

        p25 = contextlib.ExitStack()
        with p25:
            q_xnm = ring("q_xnm")
            q_dtu = ring("q_dtu")

            act_p25, dve_p25, park_ms = {}, {}, {}
            for th2 in range(2):
                for oc in range(ODC):
                    j = th2 * ODC + oc

                    @block.sync
                    def _(e, j=j, th2=th2, oc=oc):
                        if j >= 1:
                            e.wait_ge(dve.h, dve_p25[j - 1])
                        e.dma_start(out=xnm_sb[:],
                                    in_=xn_dram[th2 * 2048:(th2 + 1) * 2048,
                                                oc * 128:(oc + 1) * 128],
                                    transpose=True).then_inc(q_xnm.h, 16)
                    xnm_tgt = q_xnm.inc(16)

                    @block.tensor
                    def _(e, j=j, th2=th2, oc=oc):
                        if j == 0:
                            e.wait_ge(dve.h, evac_ms[1])
                        if j >= 1:
                            e.wait_ge(act.h, act_p25[j - 1])
                        for i in range(4):
                            mm = e.matmul(out=ps[:, i * 512:(i + 1) * 512],
                                          lhsT=wdtT_bf[:, oc * 128:(oc + 1) * 128],
                                          rhs=dtlowT_bf[:, th2 * 2048 + i * 512:
                                                        th2 * 2048 + (i + 1) * 512],
                                          start=True, stop=True)
                            if i == 3:
                                mm.then_inc(pe.h, 1)
                    pe_dt = pe.inc()

                    @block.scalar
                    def _(e, j=j, oc=oc, pe_dt=pe_dt):
                        e.wait_ge(pe.h, pe_dt)
                        if j >= 1:
                            e.wait_ge(dve.h, dve_p25[j - 1])
                        bias = dtb_sb[:, oc:oc + 1]
                        e.activation(out=sp_a[:], in_=ps[:, 0:2048], func=AF.Abs, bias=bias)
                        e.activation(out=sp_b[:], in_=sp_a[:], func=AF.Exp,
                                     scale=const_sb[:, 2:3])
                        e.activation(out=sp_a[:], in_=sp_b[:], func=AF.Ln, bias=1.0)
                        e.activation(out=sp_b[:], in_=ps[:, 0:2048], func=AF.Relu,
                                     bias=bias).then_inc(act.h, 1)
                    act_p25[j] = act.inc()

                    @block.vector
                    def _(e, j=j, oc=oc, xnm_tgt=xnm_tgt):
                        e.wait_ge(act.h, act_p25[j])
                        if j >= 1:
                            e.wait_ge(q_dtu.h, park_ms[j - 1])
                        e.tensor_add(out=dtu_sb[:, 0:2048], in0=sp_a[:], in1=sp_b[:])
                        e.wait_ge(q_xnm.h, xnm_tgt)
                        e.tensor_mul(out=dtu_sb[:, 2048:4096],
                                     in0=dtu_sb[:, 0:2048], in1=xnm_sb[:])
                        e.tensor_scalar_mul(out=dtu_sb[:, 4096:6144], in0=xnm_sb[:],
                                            scalar1=Dp_sb[:, oc:oc + 1])
                        e.tensor_copy(out=sp_a[:], in_=sp_a[:]).then_inc(dve.h, 1)
                    dve_p25[j] = dve.inc()

                    @block.sync
                    def _(e, j=j, th2=th2, oc=oc):
                        e.wait_ge(dve.h, dve_p25[j])
                        rows = slice(oc * 128, (oc + 1) * 128)
                        cols = slice(th2 * 2048, (th2 + 1) * 2048)
                        e.dma_start(out=dt_dram[rows, cols],
                                    in_=dtu_sb[:, 0:2048]).then_inc(q_dtu.h, 16)
                        e.dma_start(out=u_dram[rows, cols],
                                    in_=dtu_sb[:, 2048:4096]).then_inc(q_dtu.h, 16)
                        e.dma_start(out=dxn_dram[rows, cols],
                                    in_=dtu_sb[:, 4096:6144]).then_inc(q_dtu.h, 16)
                    park_ms[j] = q_dtu.inc(48)

        p2c.close()
        barrier()

        @block.scalar
        def _(e):
            load_act_set(nc, e, "exp_and_others")

        p3 = contextlib.ExitStack()
        with p3:
            def p3sb(nm, shp, dt):
                return p3.enter_context(nc.sbuf_tensor(nm, shp, dt))
            dt_tc = p3sb("dt_tc", [128, ODC * TC], BF16)
            u_tc = p3sb("u_tc", [128, ODC * TC], BF16)
            y_tc = p3sb("y_tc", [128, ODC * TC], BF16)
            bbc_sb = p3sb("bbc_sb", [128, 2 * TC], BF16)
            cbc_sb = p3sb("cbc_sb", [128, 2 * TC], BF16)
            abar_sb = p3sb("abar_sb", [128, 2 * TC], F32)
            bx_sb = p3sb("bx_sb", [128, 2 * TC], BF16)
            h_sb = p3sb("h_sb", [128, 2 * TC], BF16)
            w_sb = p3sb("w_sb", [128, 2 * TC], BF16)
            xnf_sb = p3sb("xnf_sb", [128, NDC * 1024], BF16)
            wg2_sb = p3sb("wg2_sb", [128, 2 * NDC * 256], BF16)
            wu2_sb = p3sb("wu2_sb", [128, 2 * NDC * 256], BF16)
            th_sb = p3sb("th_sb", [128, 2 * 1024], BF16)
            gh_sb = p3sb("gh_sb", [128, 2 * 1024], BF16)
            ups_sb = p3sb("ups_sb", [128, 2 * 1024], BF16)
            th1_sb = p3sb("th1_sb", [128, 2 * 1024], BF16)
            t3_sb = p3sb("t3_sb", [128, 2 * 1024], BF16)
            actt_sb = p3sb("actt_sb", [128, 2 * 1024], BF16)

            q_tc = ring("q_tc")
            q_bc = [ring("q_bc0"), ring("q_bc1")]
            q_park = ring("q_park")
            q_xnf = ring("q_xnf")
            q_wp = [ring("q_wp0"), ring("q_wp1")]
            q_ast = [ring("q_ast0"), ring("q_ast1")]

            scan_ms, ybar_ms, w_ms, abar_ms = {}, {}, {}, {}
            bxlast_u, wlast_u, park_tc, tcload_ms = {}, {}, {}, {}
            fw_ms, fpe_ms, fact_ps, fact_th1, fpool_ms, fst_ms = {}, {}, {}, {}, {}, {}
            xnf_ms, x1_ms = {}, {}

            def emit_tc_loads(tc):
                @block.sync
                def _(e, tc=tc):
                    cols = slice(tc * TC, (tc + 1) * TC)
                    if tc >= 1:
                        kp = ((tc - 1) * 16 + 15) * ODC + ODC - 1
                        e.wait_ge(act.h, abar_ms[kp])
                        e.wait_ge(dve.h, bxlast_u[(tc - 1) * 16 + 15])
                        e.wait_ge(q_park.h, park_tc[(tc - 1, ODC - 1)])
                    for oc in range(ODC):
                        rows = slice(oc * 128, (oc + 1) * 128)
                        dst = slice(oc * TC, (oc + 1) * TC)
                        e.dma_start(out=dt_tc[:, dst],
                                    in_=dt_dram[rows, cols]).then_inc(q_tc.h, 16)
                        e.dma_start(out=u_tc[:, dst],
                                    in_=u_dram[rows, cols]).then_inc(q_tc.h, 16)
                        e.dma_start(out=y_tc[:, dst],
                                    in_=dxn_dram[rows, cols]).then_inc(q_tc.h, 16)
                tcload_ms[tc] = q_tc.inc(16 * 3 * ODC)

            pending_y = [None]

            def emit_y(k):
                oc = k % ODC
                m = k // ODC
                tc, n = divmod(m, 16)
                sl = k % 2
                col = oc * 16 + n
                tsl = slice(sl * TC, (sl + 1) * TC)
                osl = slice(oc * TC, (oc + 1) * TC)

                @block.vector
                def _(e, k=k, sl=sl, col=col, tsl=tsl, osl=osl):
                    e.wait_ge(pool.h, w_ms[k])
                    e.tensor_add(out=y_tc[:, osl], in0=y_tc[:, osl],
                                 in1=w_sb[:, tsl])
                    e.tensor_copy(out=state_sb[:, col:col + 1],
                                  in_=h_sb[:, sl * TC + TC - 1:sl * TC + TC]
                                  ).then_inc(dve.h, 1)
                ybar_ms[k] = dve.inc()

            def emit_mixer_unit(m):
                tc, n = divmod(m, 16)
                if n == 0:
                    emit_tc_loads(tc)
                bsl = m % 2
                bdst = slice(bsl * TC, (bsl + 1) * TC)

                @block.sync
                def _(e, m=m, tc=tc, n=n, bdst=bdst, bsl=bsl):
                    cols = slice(tc * TC, (tc + 1) * TC)
                    e.wait_ge(q_bcd.h, 16)
                    if m >= 2:
                        e.wait_ge(dve.h, bxlast_u[m - 2])
                        e.wait_ge(pool.h, wlast_u[m - 2])
                    e.dma_start(out=bbc_sb[:, bdst],
                                in_=bc_dram[n:n + 1, cols]
                                .partition_broadcast(128)).then_inc(q_bc[bsl].h, 16)
                    e.dma_start(out=cbc_sb[:, bdst],
                                in_=bc_dram[16 + n:17 + n, cols]
                                .partition_broadcast(128)).then_inc(q_bc[bsl].h, 16)
                bc_tgt = q_bc[bsl].inc(32)

                for oc in range(ODC):
                    k = m * ODC + oc
                    sl = k % 2
                    col = oc * 16 + n
                    tsl = slice(sl * TC, (sl + 1) * TC)
                    osl = slice(oc * TC, (oc + 1) * TC)

                    @block.scalar
                    def _(e, k=k, tc=tc, col=col, tsl=tsl, osl=osl):
                        e.wait_ge(q_tc.h, tcload_ms[tc])
                        if k >= 2:
                            e.wait_ge(dve.h, scan_ms[k - 2])
                        e.activation(out=abar_sb[:, tsl], in_=dt_tc[:, osl], func=AF.Exp,
                                     scale=A_sb[:, col:col + 1]).then_inc(act.h, 1)
                    abar_ms[k] = act.inc()

                    @block.vector
                    def _(e, k=k, tc=tc, col=col, sl=sl, tsl=tsl, osl=osl,
                          bsl=bsl, bc_tgt=bc_tgt):
                        e.wait_ge(q_bc[bsl].h, bc_tgt)
                        e.wait_ge(q_tc.h, tcload_ms[tc])
                        e.tensor_mul(out=bx_sb[:, tsl], in0=u_tc[:, osl],
                                     in1=bbc_sb[:, bsl * TC:(bsl + 1) * TC])
                        e.wait_ge(act.h, abar_ms[k])
                        if k >= 2:
                            e.wait_ge(pool.h, w_ms[k - 2])
                        init = 0.0 if tc == 0 else state_sb[:, col:col + 1]
                        e.tensor_tensor_scan(out=h_sb[:, tsl], data0=abar_sb[:, tsl],
                                             data1=bx_sb[:, tsl], initial=init,
                                             op0=OP.mult, op1=OP.add).then_inc(dve.h, 1)
                    scan_ms[k] = dve.inc()
                    if oc == ODC - 1:
                        bxlast_u[m] = scan_ms[k]

                    @block.gpsimd
                    def _(e, k=k, tsl=tsl, bsl=bsl):
                        e.wait_ge(dve.h, scan_ms[k])
                        if k >= 2:
                            e.wait_ge(dve.h, ybar_ms[k - 2])
                        e.tensor_mul(out=w_sb[:, tsl], in0=h_sb[:, tsl],
                                     in1=cbc_sb[:, bsl * TC:(bsl + 1) * TC]
                                     ).then_inc(pool.h, 1)
                    w_ms[k] = pool.inc()
                    if oc == ODC - 1:
                        wlast_u[m] = w_ms[k]

                    if pending_y[0] is not None:
                        emit_y(pending_y[0])
                    pending_y[0] = k

                if n == 15:
                    emit_y(pending_y[0])
                    pending_y[0] = None
                    @block.vector
                    def _(e):
                        e.tensor_copy(out=bx_sb[:, 0:TC],
                                      in_=bx_sb[:, 0:TC]).then_inc(dve.h, 1)
                    park_gate = dve.inc()

                    for oc in range(ODC):
                        @block.sync
                        def _(e, tc=tc, oc=oc, park_gate=park_gate):
                            e.wait_ge(dve.h, park_gate)
                            e.dma_start(out=cc_in[oc * 128:(oc + 1) * 128,
                                                  tc * TC:(tc + 1) * TC],
                                        in_=y_tc[:, oc * TC:(oc + 1) * TC]
                                        ).then_inc(q_park.h, 16)
                        park_tc[(tc, oc)] = q_park.inc(16)

            def emit_ffn_pair_load(hf, qp):
                sl = qp % 2

                @block.sync
                def _(e, hf=hf, qp=qp, sl=sl):
                    if qp >= 2:
                        e.wait_ge(pe.h, fpe_ms[(hf, 2 * qp - 3)])
                    elif hf == 1:
                        e.wait_ge(pe.h, fpe_ms[(0, HR - 1)])
                    srcg = wg_bf.rearrange("(c p) h -> p c h", p=128)
                    srcu = wu_bf.rearrange("(c p) h -> p c h", p=128)
                    e.dma_start(out=wg2_sb[:, sl * NDC * 256:(sl + 1) * NDC * 256]
                                .rearrange("p (c h) -> p c h", c=NDC),
                                in_=srcg[:, :, qp * 256:(qp + 1) * 256]).then_inc(q_wp[sl].h, 16)
                    e.dma_start(out=wu2_sb[:, sl * NDC * 256:(sl + 1) * NDC * 256]
                                .rearrange("p (c h) -> p c h", c=NDC),
                                in_=srcu[:, :, qp * 256:(qp + 1) * 256]).then_inc(q_wp[sl].h, 16)
                fw_ms[(hf, qp)] = q_wp[sl].inc(32)

            def emit_ffn_half_prologue(hf):
                @block.sync
                def _(e, hf=hf):
                    if hf == 1:
                        e.wait_ge(pe.h, fpe_ms[(0, HR - 1)])
                    for dc in range(NDC):
                        e.dma_start(out=xnf_sb[:, dc * 1024:(dc + 1) * 1024],
                                    in_=xnown_dram[hf * 1024:(hf + 1) * 1024,
                                                   dc * 128:(dc + 1) * 128],
                                    transpose=True).then_inc(q_xnf.h, 16)
                xnf_ms[hf] = q_xnf.inc(16 * NDC)

                for tch in range(2):
                    @block.tensor
                    def _(e, hf=hf, tch=tch):
                        if tch == 0:
                            e.wait_ge(q_xnf.h, xnf_ms[hf])
                            if hf == 1:
                                e.wait_ge(dve.h, x1_ms[0])
                                e.wait_ge(act.h, fact_ps[(0, HR - 2)])
                                e.wait_ge(act.h, fact_ps[(0, HR - 1)])
                        else:
                            e.wait_ge(dve.h, dve.v)
                        for dc in range(NDC):
                            mm = e.matmul(out=ps[0:1, 0:512], lhsT=ones_bf[:],
                                          rhs=xnf_sb[:, dc * 1024 + tch * 512:
                                                     dc * 1024 + (tch + 1) * 512],
                                          start=(dc == 0), stop=(dc == NDC - 1))
                            if dc == NDC - 1:
                                mm.then_inc(pe.h, 1)
                    x1pe = pe.inc()

                    @block.vector
                    def _(e, hf=hf, tch=tch, x1pe=x1pe):
                        e.wait_ge(pe.h, x1pe)
                        dst = slice(hf * 1024 + tch * 512, hf * 1024 + (tch + 1) * 512)
                        e.tensor_copy(out=x1row_f[0:1, dst], in_=ps[0:1, 0:512])
                        e.tensor_copy(out=x1row_bf[0:1, dst], in_=x1row_f[0:1, dst])
                        e.tensor_copy(out=x1row_f[0:1, dst], in_=x1row_f[0:1, dst])
                        e.tensor_copy(out=x1row_f[0:1, dst],
                                      in_=x1row_f[0:1, dst]).then_inc(dve.h, 1)
                    x1_ms[hf] = dve.inc()

            def emit_ffn_unit(f):
                hf, hr = divmod(f, HR)
                if hr == 0:
                    emit_ffn_half_prologue(hf)
                    emit_ffn_pair_load(hf, 0)
                    emit_ffn_pair_load(hf, 1)
                if hr % 2 == 0 and hr + 2 < HR:
                    emit_ffn_pair_load(hf, hr // 2 + 1)
                qp = hr // 2
                wsl = qp % 2
                pb = (hr % 2) * 2048
                hh = (hr % 2) * 128
                pf = None
                if f >= 2:
                    pf = (hf, hr - 2) if hr >= 2 else (0, HR - 2 + hr)

                @block.tensor
                def _(e, f=f, hf=hf, hr=hr, qp=qp, wsl=wsl, pb=pb, hh=hh, pf=pf):
                    e.wait_ge(q_wp[wsl].h, fw_ms[(hf, qp)])
                    e.wait_ge(q_xnf.h, xnf_ms[hf])
                    e.wait_ge(dve.h, x1_ms[hf])
                    if pf is not None:
                        e.wait_ge(act.h, fact_ps[pf])
                    for mat, wsb in ((0, wg2_sb), (1, wu2_sb)):
                        ub = pb + mat * 1024
                        for dc in range(NDC):
                            lhs = wsb[:, wsl * NDC * 256 + dc * 256 + hh:
                                      wsl * NDC * 256 + dc * 256 + hh + 128]
                            for tch in range(2):
                                e.matmul(out=ps[:, ub + tch * 512:ub + (tch + 1) * 512],
                                         lhsT=lhs,
                                         rhs=xnf_sb[:, dc * 1024 + tch * 512:
                                                    dc * 1024 + (tch + 1) * 512],
                                         start=(dc == 0), stop=False)
                        for tch in range(2):
                            mm = e.matmul(out=ps[:, ub + tch * 512:ub + (tch + 1) * 512],
                                          lhsT=n128_bf[:],
                                          rhs=x1row_bf[0:1, hf * 1024 + tch * 512:
                                                       hf * 1024 + (tch + 1) * 512],
                                          start=False, stop=True)
                            if mat == 1 and tch == 1:
                                mm.then_inc(pe.h, 1)
                fpe_ms[(hf, hr)] = pe.inc()

                @block.scalar
                def _(e, f=f, hf=hf, hr=hr, pb=pb, pf=pf):
                    e.wait_ge(pe.h, fpe_ms[(hf, hr)])
                    if pf is not None:
                        e.wait_ge(pool.h, fpool_ms[pf])
                    sl = f % 2
                    dst = slice(sl * 1024, (sl + 1) * 1024)
                    sgc = sgh_sb[:, hr:hr + 1]
                    suc = su_sb[:, hr:hr + 1]
                    e.activation(out=th_sb[:, dst], in_=ps[:, pb:pb + 1024],
                                 func=AF.Tanh, scale=sgc)
                    e.activation(out=gh_sb[:, dst], in_=ps[:, pb:pb + 1024],
                                 func=AF.Identity, scale=sgc)
                    e.activation(out=ups_sb[:, dst], in_=ps[:, pb + 1024:pb + 2048],
                                 func=AF.Identity, scale=suc).then_inc(act.h, 1)
                    e.activation(out=th1_sb[:, dst], in_=th_sb[:, dst],
                                 func=AF.Identity, bias=1.0).then_inc(act.h, 1)
                fact_ps[(hf, hr)] = act.inc()
                fact_th1[(hf, hr)] = act.inc()

                @block.gpsimd
                def _(e, f=f, hf=hf, hr=hr, pf=pf):
                    e.wait_ge(act.h, fact_th1[(hf, hr)])
                    if pf is not None:
                        e.wait_ge(q_ast[f % 2].h, fst_ms[pf])
                    sl = f % 2
                    dst = slice(sl * 1024, (sl + 1) * 1024)
                    e.tensor_mul(out=t3_sb[:, dst], in0=gh_sb[:, dst], in1=ups_sb[:, dst])
                    e.tensor_mul(out=actt_sb[:, dst], in0=t3_sb[:, dst],
                                 in1=th1_sb[:, dst]).then_inc(pool.h, 1)
                fpool_ms[(hf, hr)] = pool.inc()

                @block.sync
                def _(e, f=f, hf=hf, hr=hr):
                    e.wait_ge(pool.h, fpool_ms[(hf, hr)])
                    sl = f % 2
                    e.dma_start(out=act_dram[hr * 128:(hr + 1) * 128,
                                             hf * 1024:(hf + 1) * 1024],
                                in_=actt_sb[:, sl * 1024:(sl + 1) * 1024]
                                ).then_inc(q_ast[sl].h, 16)
                fst_ms[(hf, hr)] = q_ast[f % 2].inc(16)

            fidx = 0
            for m in range(NTC * 16):
                emit_mixer_unit(m)
                tgt = (m + 1) * (2 * HR) // (NTC * 16)
                while fidx < tgt:
                    emit_ffn_unit(fidx)
                    fidx += 1
            while fidx < 2 * HR:
                emit_ffn_unit(fidx)
                fidx += 1

            @block.gpsimd
            def _(e):
                e.wait_ge(q_park.h, q_park.v)
                e.collective_compute(
                    "AllGather", OP.bypass,
                    replica_groups=[[0, 1], [2, 3], [4, 5], [6, 7]],
                    ins=[cc_in.opt()], outs=[cc_out.opt()]).then_inc(ccs.h)
            ccs.inc()

        barrier()

        p4 = contextlib.ExitStack()
        with p4:
            def p4sb(nm, shp, dt):
                return p4.enter_context(nc.sbuf_tensor(nm, shp, dt))
            actq_sb = p4sb("actq_sb", [128, HR * 512], BF16)
            s_sb = p4sb("s_sb", [128, NDC * 512], BF16)
            wdt2_sb = p4sb("wdt2_sb", [128, 12 * 512], BF16)
            th2_sb = p4sb("th2_sb", [128, 8 * 512], BF16)
            gh2_sb = p4sb("gh2_sb", [128, 8 * 512], BF16)
            g2_sb = p4sb("g2_sb", [128, 8 * 512], BF16)
            sg2_sb = p4sb("sg2_sb", [128, 8 * 512], BF16)
            sgt_sb = p4sb("sgt_sb", [128, 2 * 2048], BF16)
            xo_sb = p4sb("xo_sb", [128, 2 * 2048], F32)
            ot_sb = p4sb("ot_sb", [128, 2 * 2048], F32)

            q_aq = ring("q_aq")
            q_ssb = ring("q_ssb")
            q_wd2 = [ring("q_wd2%d" % i) for i in range(12)]
            wd_seq = []
            q_sg = [ring("q_sg0"), ring("q_sg1")]
            q_cmb = [ring("q_cmb0"), ring("q_cmb1")]
            q_out = [ring("q_out0"), ring("q_out1")]

            aq_ms, qpe_hr, qpe_ms, qact_ms, qdve_ms, qpool_ms, qst_ms = {}, {}, {}, {}, {}, {}, {}
            wd_ms, a1e_ms, out_ms, out_pool, s_ld_tq, cmb_ms = {}, {}, {}, {}, {}, {}
            wd_slot = {}

            for tq in range(4):
                @block.scalar
                def _(e, tq=tq):
                    if tq >= 1:
                        e.wait_ge(pe.h, qpe_ms[(tq - 1, 3)])
                    for hr in range(HR):
                        e.dma_start(out=actq_sb[:, hr * 512:(hr + 1) * 512],
                                    in_=act_dram[hr * 128:(hr + 1) * 128,
                                                 tq * 512:(tq + 1) * 512]).then_inc(q_aq.h, 16)
                aq_ms[tq] = q_aq.inc(16 * HR)

                @block.scalar
                def _(e, tq=tq):
                    e.wait_ge(ccs.h, 1)
                    if tq >= 1:
                        e.wait_ge(pool.h, qpool_ms[(tq - 1, 3)])
                    for arm in nc.switch_core_id(engines=[e], n=128):
                        pos = int(arm) % 2
                        off = pos * TOWN + tq * 512
                        for g in range(NDC):
                            if g < ODC:
                                srcap = cc_out[pos * 1024 + g * 128:
                                               pos * 1024 + (g + 1) * 128,
                                               off:off + 512]
                            else:
                                p_ = 1 - pos
                                srcap = cc_out[p_ * 1024 + (g - ODC) * 128:
                                               p_ * 1024 + (g - ODC + 1) * 128,
                                               off:off + 512]
                            e.dma_start(out=s_sb[:, g * 512:(g + 1) * 512],
                                        in_=srcap).then_inc(q_ssb.h, 16)
                s_ld_tq[tq] = q_ssb.inc(16 * NDC)

                for g4 in range(4):
                    qi = tq * 4 + g4
                    base = (g4 % 2) * 2048
                    for hr in range(HR):
                        wi = len(wd_seq)
                        sl = wi % 12
                        wd_seq.append((tq, g4, hr))

                        @block.scalar
                        def _(e, tq=tq, g4=g4, hr=hr, sl=sl, wi=wi):
                            if wi >= 12:
                                e.wait_ge(pe.h, qpe_hr[wd_seq[wi - 12]])
                            e.dma_start(out=wdt2_sb[:, sl * 512:(sl + 1) * 512],
                                        in_=wd_bf[hr * 128:(hr + 1) * 128,
                                                  g4 * 512:(g4 + 1) * 512]
                                        ).then_inc(q_wd2[sl].h, 16)
                        wd_ms[(tq, g4, hr)] = q_wd2[sl].inc(16)
                        wd_slot[(tq, g4, hr)] = sl

                        @block.tensor
                        def _(e, tq=tq, g4=g4, hr=hr, sl=sl, base=base):
                            e.wait_ge(q_wd2[sl].h, wd_ms[(tq, g4, hr)])
                            if hr == 0:
                                e.wait_ge(q_aq.h, aq_ms[tq])
                                qi2 = tq * 4 + g4
                                if qi2 >= 2:
                                    pq = divmod(qi2 - 2, 4)
                                    e.wait_ge(act.h, qact_ms[(pq[0], pq[1])])
                                if g4 == 1:
                                    e.wait_ge(dve.h, a1e_ms[tq])
                            rhs = actq_sb[:, hr * 512:(hr + 1) * 512]
                            for j in range(4):
                                mm = e.matmul(out=ps[:, base + j * 512:base + (j + 1) * 512],
                                              lhsT=wdt2_sb[:, sl * 512 + j * 128:
                                                           sl * 512 + (j + 1) * 128],
                                              rhs=rhs, start=(hr == 0), stop=False)
                                if j == 3 and g4 != 0:
                                    mm.then_inc(pe.h, 1)
                            if g4 == 0:
                                mm = e.matmul(out=ps[0:1, 2048:2560], lhsT=ones_bf[:],
                                              rhs=rhs, start=(hr == 0), stop=(hr == HR - 1))
                                mm.then_inc(pe.h, 1)
                        qpe_hr[(tq, g4, hr)] = pe.inc()

                    if g4 == 0:
                        @block.vector
                        def _(e, tq=tq):
                            e.wait_ge(pe.h, qpe_hr[(tq, 0, HR - 1)])
                            e.tensor_copy(out=a1row_f[:], in_=ps[0:1, 2048:2560])
                            e.tensor_copy(out=a1row_bf[:], in_=a1row_f[:])
                            e.tensor_copy(out=a1row_f[:], in_=a1row_f[:])
                            e.tensor_copy(out=a1row_f[:], in_=a1row_f[:])
                            e.tensor_copy(out=a1row_f[:],
                                          in_=a1row_f[:]).then_inc(dve.h, 1)
                        a1e_ms[tq] = dve.inc()

                    @block.tensor
                    def _(e, tq=tq, g4=g4, base=base):
                        if g4 == 0:
                            e.wait_ge(dve.h, a1e_ms[tq])
                        for j in range(4):
                            mm = e.matmul(out=ps[:, base + j * 512:base + (j + 1) * 512],
                                          lhsT=n128_bf[:], rhs=a1row_bf[:],
                                          start=False, stop=True)
                            if j == 3:
                                mm.then_inc(pe.h, 1)
                    qpe_ms[(tq, g4)] = pe.inc()

                    @block.scalar
                    def _(e, tq=tq, g4=g4, base=base, qi=qi):
                        e.wait_ge(pe.h, qpe_ms[(tq, g4)])
                        if qi >= 2:
                            pq = divmod(qi - 2, 4)
                            e.wait_ge(dve.h, qdve_ms[(pq[0], pq[1])])
                        for j in range(4):
                            gl = g4 * 4 + j
                            sl = (qi % 2) * 4 + j
                            dst = slice(sl * 512, (sl + 1) * 512)
                            psv = ps[:, base + j * 512:base + (j + 1) * 512]
                            sdc = sdh_sb[:, gl:gl + 1]
                            e.activation(out=th2_sb[:, dst], in_=psv, func=AF.Tanh,
                                         scale=sdc)
                            la = e.activation(out=gh2_sb[:, dst], in_=psv,
                                              func=AF.Identity, scale=sdc)
                            if j == 3:
                                la.then_inc(act.h, 1)
                    qact_ms[(tq, g4)] = act.inc()

                    @block.vector
                    def _(e, tq=tq, g4=g4, qi=qi):
                        e.wait_ge(act.h, qact_ms[(tq, g4)])
                        if qi >= 2:
                            pq = divmod(qi - 2, 4)
                            e.wait_ge(pool.h, qpool_ms[(pq[0], pq[1])])
                        for j in range(4):
                            sl = (qi % 2) * 4 + j
                            dst = slice(sl * 512, (sl + 1) * 512)
                            la = e.scalar_tensor_tensor(out=g2_sb[:, dst],
                                                        in0=th2_sb[:, dst], scalar=1.0,
                                                        in1=gh2_sb[:, dst],
                                                        op0=OP.add, op1=OP.mult)
                            if j == 3:
                                la.then_inc(dve.h, 1)
                    qdve_ms[(tq, g4)] = dve.inc()

                    @block.gpsimd
                    def _(e, tq=tq, g4=g4, qi=qi):
                        e.wait_ge(dve.h, qdve_ms[(tq, g4)])
                        e.wait_ge(q_ssb.h, s_ld_tq[tq])
                        if qi >= 2:
                            pq = divmod(qi - 2, 4)
                            e.wait_ge(q_sg[qi % 2].h, qst_ms[(pq[0], pq[1])])
                        for j in range(4):
                            gl = g4 * 4 + j
                            sl = (qi % 2) * 4 + j
                            dst = slice(sl * 512, (sl + 1) * 512)
                            la = e.tensor_mul(out=sg2_sb[:, dst], in0=g2_sb[:, dst],
                                              in1=s_sb[:, gl * 512:(gl + 1) * 512])
                            if j == 3:
                                la.then_inc(pool.h, 1)
                    qpool_ms[(tq, g4)] = pool.inc()

                    @block.sync
                    def _(e, tq=tq, g4=g4, qi=qi):
                        e.wait_ge(pool.h, qpool_ms[(tq, g4)])
                        for j in range(4):
                            gl = g4 * 4 + j
                            sl = (qi % 2) * 4 + j
                            e.dma_start(out=sg_dram[gl * 128:(gl + 1) * 128,
                                                    tq * 512:(tq + 1) * 512],
                                        in_=sg2_sb[:, sl * 512:(sl + 1) * 512]
                                        ).then_inc(q_sg[qi % 2].h, 16)
                    qst_ms[(tq, g4)] = q_sg[qi % 2].inc(16 * 4)

                for tt in range(4):
                    ti = tq * 4 + tt
                    row0 = ti * 128
                    csl = ti % 2

                    @block.sync
                    def _(e, tq=tq, ti=ti, row0=row0, csl=csl):
                        e.wait_ge(q_sg[0].h, q_sg[0].v)
                        e.wait_ge(q_sg[1].h, q_sg[1].v)
                        if ti >= 2:
                            e.wait_ge(pool.h, out_pool[ti - 2])
                        e.dma_start(out=sgt_sb[:, csl * 2048:(csl + 1) * 2048],
                                    in_=sg_dram[0:D, ti * 128:(ti + 1) * 128],
                                    transpose=True).then_inc(q_cmb[csl].h, 16)
                        e.dma_start(out=xo_sb[:, csl * 2048:(csl + 1) * 2048],
                                    in_=x_own[row0:row0 + 128, :]).then_inc(q_cmb[csl].h, 16)
                    cmb_ms[ti] = q_cmb[csl].inc(32)

                    @block.gpsimd
                    def _(e, ti=ti, csl=csl):
                        e.wait_ge(q_cmb[csl].h, cmb_ms[ti])
                        if ti >= 2:
                            e.wait_ge(q_out[csl].h, out_ms[ti - 2])
                        a = slice(csl * 2048, (csl + 1) * 2048)
                        e.tensor_add(out=ot_sb[:, a], in0=xo_sb[:, a],
                                     in1=sgt_sb[:, a]).then_inc(pool.h, 1)
                    out_pool[ti] = pool.inc()

                    @block.sync
                    def _(e, ti=ti, row0=row0, csl=csl):
                        e.wait_ge(pool.h, out_pool[ti])
                        e.dma_start(out=out_own[row0:row0 + 128, :],
                                    in_=ot_sb[:, csl * 2048:(csl + 1) * 2048]
                                    ).then_inc(q_out[csl].h, 16)
                    out_ms[ti] = q_out[csl].inc(16)

            if debug:
                barrier()

                @block.gpsimd
                def _(e):
                    e.dma_start(out=dbg["dbg_xn"], in_=xn_dram).then_inc(gq.h, 16)
                    e.dma_start(out=dbg["dbg_bc"], in_=bc_dram).then_inc(gq.h, 16)
                    e.dma_start(out=dbg["dbg_dt"], in_=dt_dram).then_inc(gq.h, 16)
                    e.dma_start(out=dbg["dbg_u"], in_=u_dram).then_inc(gq.h, 16)
                    e.dma_start(out=dbg["dbg_s"], in_=cc_in).then_inc(gq.h, 16)
                    e.dma_start(out=dbg["dbg_act"], in_=act_dram).then_inc(gq.h, 16)
                    e.dma_start(out=dbg["dbg_st"], in_=st_dram).then_inc(gq.h, 16)
                gq.inc(16 * 7)

            @block.sync
            def _(e):
                e.wait_ge(q_out[0].h, q_out[0].v)
                e.wait_ge(q_out[1].h, q_out[1].v)

            @block.gpsimd
            def _(e):
                e.wait_ge(gq.h, gq.v)

    return nc


_NC_CACHE = {}


def _get_nc(debug=False):
    key = bool(debug)
    if key not in _NC_CACHE:
        _NC_CACHE[key] = build(debug=key)
    return _NC_CACHE[key]


def _perm(dh):
    return np.arange(D) if dh == 0 else np.concatenate(
        [np.arange(1024, 2048), np.arange(0, 1024)])


def make_inputs(L0_attn_x, L0_attn_x_proj_w, L0_attn_dt_proj_w, L0_attn_dt_proj_b,
                L0_attn_A_log, L0_attn_D_param, L0_ffn_wq_gate, L0_ffn_s_gate,
                L0_ffn_wq_up, L0_ffn_s_up, L0_ffn_wq_down, L0_ffn_s_down):
    x = np.ascontiguousarray(np.asarray(L0_attn_x, dtype=np.float32))
    xp = np.asarray(L0_attn_x_proj_w, dtype=np.float32)
    wdt = np.asarray(L0_attn_dt_proj_w, dtype=np.float32)
    dtb = np.asarray(L0_attn_dt_proj_b, dtype=np.float32)
    alog = np.asarray(L0_attn_A_log, dtype=np.float32)
    dp = np.asarray(L0_attn_D_param, dtype=np.float32)
    wqg = np.asarray(L0_ffn_wq_gate, dtype=np.int32)
    wqu = np.asarray(L0_ffn_wq_up, dtype=np.int32)
    wqd = np.asarray(L0_ffn_wq_down, dtype=np.int32)
    sgv = np.asarray(L0_ffn_s_gate, dtype=np.float32)
    suv = np.asarray(L0_ffn_s_up, dtype=np.float32)
    sdv = np.asarray(L0_ffn_s_down, dtype=np.float32)

    per_dh = {}
    for dh in range(2):
        perm = _perm(dh)
        own = perm[:1024]
        per_dh[dh] = dict(
            wgT=np.ascontiguousarray(wqg.T[perm, :]),
            wuT=np.ascontiguousarray(wqu.T[perm, :]),
            wdn=np.ascontiguousarray(wqd[perm, :].T),
            xprojT=np.ascontiguousarray(xp.T[perm, :]),
            wdtT_own=np.ascontiguousarray(wdt.T[:, own]),
            dtb_own=np.ascontiguousarray(dtb[own].reshape(ODC, 128).T),
            Alog_own=np.ascontiguousarray(
                alog[own].reshape(ODC, 128, N).transpose(1, 0, 2).reshape(128, ODC * N)),
            Dp_own=np.ascontiguousarray(dp[own].reshape(ODC, 128).T),
            sd_in=np.ascontiguousarray(sdv[perm].reshape(NDC, 128).T),
        )
    sg_t = np.ascontiguousarray(sgv.reshape(HR, 128).T)
    su_t = np.ascontiguousarray(suv.reshape(HR, 128).T)
    cst_arr = np.broadcast_to(
        np.array([EPS, -0.5, -1.0, 1.0], np.float32), (128, 4)).copy()
    ones_arr = np.ones((128, 1), ml_dtypes.bfloat16)
    n128_arr = np.full((1, 128), -128.0, ml_dtypes.bfloat16)

    in_maps = []
    for c in range(8):
        seq, dh = c // 2, c % 2
        perm = _perm(dh)
        xs = np.ascontiguousarray(x[seq][:, perm])
        m = dict(per_dh[dh])
        m.update(x_seq=xs,
                 x_own=np.ascontiguousarray(xs[dh * TOWN:(dh + 1) * TOWN]),
                 sg_in=sg_t, su_in=su_t,
                 cst_in=cst_arr, ones_in=ones_arr, n128_in=n128_arr)
        in_maps.append(m)
    return in_maps


def run(inputs, debug=False, **kw):
    in_maps = make_inputs(**inputs)
    nc = _get_nc(debug=debug)
    if kw.get("trace") and "trace_cores" not in kw:
        kw["trace_cores"] = [0]
    return run_bass_kernel_spmd(nc, in_maps, core_ids=list(range(8)), **kw)


def kernel(**inputs):
    res = run(inputs, debug=False)
    out = np.zeros((4, SEQ, D), dtype=np.float32)
    for c in range(8):
        seq, dh = c // 2, c % 2
        inv = np.argsort(_perm(dh))
        r = np.asarray(res.results[c]["out_own"])
        out[seq, dh * TOWN:(dh + 1) * TOWN, :] = r[:, inv]
    return out
